# version 02
# baseline (speedup 1.0000x reference)
import math
import os
import numpy as np
import ml_dtypes
import concourse.bass as bass
import concourse.mybir as mybir
from concourse.bass_utils import run_bass_kernel_spmd
from contextlib import ExitStack

os.environ.setdefault("TRN_ALLOW_ICI_ALL_TO_ALL", "1")

F32 = mybir.dt.float32
BF16 = mybir.dt.bfloat16
AF = mybir.ActivationFunctionType
ALU = mybir.AluOpType
BF16NP = ml_dtypes.bfloat16

S, D, NCORES = 8192, 2048, 8
TOK = S // NCORES
TT = 512
DCN = 16
HID, HCN = 8192, 64
NH, DH, DL, LCN = 16, 128, 512, 4
SCALE_QK = 1.0 / math.sqrt(128.0)
RSQN = 1.0 / math.sqrt(8192.0)
EPS_LN, EPS_RMS = 1e-5, 1e-6
MASKVAL = -1.0e30

ENGS = ("tensor", "vector", "scalar", "sync", "gpsimd")

_STAGE = "full"
_CACHE = {}


def _hadamard(n):
    h = np.array([[1.0]], np.float32)
    while h.shape[0] < n:
        h = np.block([[h, h], [h, -h]])
    return h.astype(np.float32)


class Plan:
    SELF_SEM = {"vector": "sV", "scalar": "sA"}

    def __init__(self, nc):
        self.nc = nc
        self.ops = {e: [] for e in ENGS}
        self.sems = {}
        self.cnt = {}
        self.last_w = {e: {} for e in ENGS}
        self.last_r = {e: {} for e in ENGS}
        self.guar = {e: {} for e in ENGS}

    def mksem(self, st, name):
        h = st.enter_context(self.nc.semaphore(name))
        self.sems[name] = h
        self.cnt[name] = 0

    def wait(self, eng, sem, val):
        if val is None or val <= 0:
            return
        if self.guar[eng].get(sem, 0) >= val:
            return
        self.guar[eng][sem] = val
        self.ops[eng].append(("wait", self.sems[sem], val))

    @staticmethod
    def _apname(v):
        t = getattr(v, "tensor", None)
        if t is not None:
            return getattr(t, "name", None)
        if hasattr(v, "name") and hasattr(v, "dtype"):
            return v.name
        return None

    def _classify(self, args, kw):
        writes, reads = [], []
        for i, a in enumerate(args):
            nm = self._apname(a)
            if nm is not None:
                (writes if i == 0 else reads).append(nm)
        for k, v in kw.items():
            nm = self._apname(v)
            if nm is not None:
                (writes if k in ("out", "accum_out") else reads).append(nm)
        return writes, reads

    def op(self, eng, method, *args, inc=None, incv=1, **kw):
        if eng in self.SELF_SEM and method != "dma_start":
            ssem = self.SELF_SEM[eng]
            writes, reads = self._classify(args, kw)
            need = 0
            for nm in writes + reads:
                need = max(need, self.last_w[eng].get(nm, 0))
            for nm in writes:
                need = max(need, self.last_r[eng].get(nm, 0))
            self.wait(eng, ssem, need)
            if inc is None:
                inc = ssem
            assert inc == ssem, (eng, inc)
            self.cnt[inc] += incv
            tot = self.cnt[inc]
            for nm in writes:
                self.last_w[eng][nm] = tot
            for nm in reads:
                self.last_r[eng][nm] = tot
            self.ops[eng].append(("op", method, args, kw, self.sems[inc], incv))
            return tot
        sem = self.sems[inc] if inc is not None else None
        tot = None
        if inc is not None:
            self.cnt[inc] += incv
            tot = self.cnt[inc]
        self.ops[eng].append(("op", method, args, kw, sem, incv))
        return tot

    def dma(self, eng, out, in_, inc, **kw):
        return self.op(eng, "dma_start", out=out, in_=in_, inc=inc, incv=16, **kw)

    def barrier(self):
        snap = dict(self.cnt)
        for e in ENGS:
            for s, v in snap.items():
                self.wait(e, s, v)

    def run(self, eng_obj, eng_name):
        for rec in self.ops[eng_name]:
            if rec[0] == "wait":
                eng_obj.wait_ge(rec[1], rec[2])
            else:
                _, method, args, kw, sem, iv = rec
                inst = getattr(eng_obj, method)(*args, **kw)
                if sem is not None:
                    inst.then_inc(sem, iv)


class SbufMap:
    def __init__(self, nc, st, total_bytes):
        self.nc = nc
        self.total = total_bytes
        self.base = (nc.sbuf_base + 31) // 32 * 32
        assert self.base + total_bytes <= nc.sbuf_top, (self.base, nc.sbuf_top)

    def at(self, offset, name, shape, dtype):
        per_part = int(np.prod(shape[1:])) * mybir.dt.size(dtype)
        assert offset + per_part <= self.total, (name, offset, per_part)
        return self.nc.alloc_sbuf_tensor_at(name, shape, dtype,
                                            offset=self.base + offset)


def build_program(stage):
    nc = bass.Bass("TRN2", target_bir_lowering=False)

    def din(name, shape, dt=F32):
        return nc.dram_tensor(name, shape, dt, kind="ExternalInput")

    x_in = din("x_in", [D, TOK])
    w1t = din("w1t", [D, HID])
    w3t = din("w3t", [D, HID])
    w2t = din("w2t", [HID, D])
    wqt_s = din("wqt_s", [D, 256])
    wkv_at = din("wkv_at", [D, DL])
    wkvb_s = din("wkvb_s", [DL, 512])
    owt = din("owt", [D, D])
    c_id_d = din("c_id", [128, 128], BF16)
    c_h128_d = din("c_h128", [128, 128], BF16)
    c_h64_d = din("c_h64", [128, 128], BF16)
    c_idf_d = din("c_idf", [128, 128])
    c_ms_d = din("c_ms", [128, 128])
    c_mask_d = din("c_mask", [128, 4 * TT])
    c_cs_d = din("c_cs", [128, 4])
    c_ones_d = din("c_ones", [128, 1])
    c_onesrow_d = din("c_onesrow", [1, 128])
    c_wqb_d = din("c_wqb", [128, 2])
    c_outb_d = din("c_outb", [128, DCN])
    c_nw_d = din("c_nw", [128, LCN])
    c_neginf_d = din("c_neginf", [128, 1])
    c_eps_d = din("c_eps", [1, 2])

    w2a_d = nc.dram_tensor("w2a", [HID, D], BF16)
    w2eff_d = nc.dram_tensor("w2eff", [DCN, 128, HCN, 128], BF16)
    xspill_d = nc.dram_tensor("xspill", [D, TOK], F32)
    ghl_c = nc.dram_tensor("ghl_c", [D + DL, TOK], BF16)
    ghl_g = nc.dram_tensor("ghl_g", [NCORES * (D + DL), TOK], BF16, addr_space="Shared")
    oc_c = nc.dram_tensor("oc_c", [D, TOK], BF16)
    oa_r = nc.dram_tensor("oa_r", [D, TOK], BF16)
    out_x = nc.dram_tensor("out_x", [D, TOK], F32, kind="ExternalOutput")
    if stage == "attn":
        d_ghl = nc.dram_tensor("d_ghl", [NCORES * (D + DL), TOK], BF16,
                               kind="ExternalOutput")
        d_q0 = nc.dram_tensor("d_q0", [128, S], BF16, kind="ExternalOutput")
        d_K = nc.dram_tensor("d_K", [128, S], BF16, kind="ExternalOutput")
        d_V = nc.dram_tensor("d_V", [128, S], BF16, kind="ExternalOutput")
        d_oc = nc.dram_tensor("d_oc", [D, TOK], BF16, kind="ExternalOutput")
        d_oa = nc.dram_tensor("d_oa", [D, TOK], BF16, kind="ExternalOutput")

    st = ExitStack()
    with st:
        KB = 1024
        sm = SbufMap(nc, st, 206 * KB)
        o = 0

        def alloc(name, shape, dtype):
            nonlocal o
            h = sm.at(o, name, shape, dtype)
            o += (int(np.prod(shape[1:])) * mybir.dt.size(dtype) + 31) // 32 * 32
            return h

        c_id = alloc("s_id", [128, 128], BF16)
        c_h128 = alloc("s_h128", [128, 128], BF16)
        c_h64 = alloc("s_h64", [128, 128], BF16)
        c_idf = alloc("s_idf", [128, 128], F32)
        c_ms = alloc("s_ms", [128, 128], F32)
        c_mask = alloc("s_mask", [128, 4 * TT], F32)
        c_cs = alloc("s_cs", [128, 4], F32)
        c_ones = alloc("s_ones", [128, 1], F32)
        c_onesrow = alloc("s_onesrow", [1, 128], F32)
        c_wqb = alloc("s_wqb", [128, 2], F32)
        c_outb = alloc("s_outb", [128, DCN], F32)
        c_nw = alloc("s_nw", [128, LCN], F32)
        c_neginf = alloc("s_neginf", [128, 1], F32)
        c_eps = alloc("s_eps", [1, 2], F32)
        r_a = alloc("s_ra", [1, TOK], F32)
        r_b = alloc("s_rb", [1, TOK], F32)
        rB512 = alloc("s_rB512", [128, TT], F32)
        sc512 = [alloc(f"s_sc{i}", [128, TT], F32) for i in range(2)]
        stg = [alloc(f"s_stg{i}", [128, TT], BF16) for i in range(2)]
        R_sb = [alloc(f"s_R{i}", [128, 128], F32) for i in range(2)]
        ost = [alloc(f"s_ost{i}", [128, 256], BF16) for i in range(2)]
        m_t = [[alloc(f"s_m{sj}{pp}", [128, 1], F32) for pp in range(2)] for sj in range(2)]
        l_t = [alloc(f"s_l{sj}", [128, 1], F32) for sj in range(2)]
        negm_t = [alloc(f"s_negm{sj}", [128, 1], F32) for sj in range(2)]
        rl_t = [alloc(f"s_rl{sj}", [128, 1], F32) for sj in range(2)]
        assert o <= 30 * KB, o

        RX, R1, R2 = 30 * KB, 94 * KB, 158 * KB
        x_sb = sm.at(RX, "v_x", [128, DCN * TOK], F32)
        h2b = [sm.at(RX + i * 16 * KB, f"v_h2b{i}", [128, DCN * TT], BF16) for i in range(2)]
        s_sc = [sm.at(RX + i * 32 * KB, f"v_s{i}", [128, 16 * TT], F32) for i in range(2)]
        x2_sb = sm.at(RX, "v_x2", [128, DCN * TOK], F32)

        w2c_sb = [sm.at(R1 + i * 4 * KB, f"v_w2c{i}", [128, D], BF16) for i in range(2)]
        w2a_sb = [sm.at(R1 + 8 * KB + i * 4 * KB, f"v_w2a{i}", [128, D], BF16) for i in range(2)]
        wc_sb = [sm.at(R1 + 16 * KB + i * 4 * KB, f"v_wc{i}", [128, D], BF16) for i in range(2)]
        weff_sb = [sm.at(R1 + 24 * KB + i * 4 * KB, f"v_weff{i}", [128, D], BF16) for i in range(2)]
        g_sb = sm.at(R1, "v_g", [128, 16 * TOK], BF16)
        w2eb = [sm.at(R1 + 32 * KB + i * 4 * KB, f"v_w2eb{i}", [128, 16 * 128], BF16) for i in range(2)]
        qT = [sm.at(R1 + i * 16 * KB, f"v_qT{i}", [128, S], BF16) for i in range(2)]
        K_sb = sm.at(R1 + 32 * KB, "v_K", [128, S], BF16)
        V_sb = sm.at(R1 + 48 * KB, "v_V", [128, S], BF16)
        owt_sb = sm.at(R1, "v_owt", [128, DCN * TOK], BF16)
        oT_sb = sm.at(R1 + 32 * KB, "v_oT", [128, DCN * TOK], BF16)

        hT_sb = sm.at(R2, "v_hT", [128, DCN * TOK], BF16)
        w1b = [sm.at(R2 + 32 * KB + i * 4 * KB, f"v_w1b{i}", [128, DCN * 128], BF16) for i in range(2)]
        w3b = [sm.at(R2 + 40 * KB + i * 4 * KB, f"v_w3b{i}", [128, DCN * 128], BF16) for i in range(2)]
        watb = [sm.at(R2 + 32 * KB + i * 4 * KB, f"v_watb{i}", [128, DCN * 128], BF16) for i in range(2)]
        wql_sb = sm.at(R2, "v_wql", [128, DCN * 256], BF16)
        pr = [sm.at(R2 + i * 16 * KB, f"v_pr{i}", [128, 16 * TT], BF16) for i in range(2)]
        ln_db = [sm.at(R2 + 32 * KB + i * 4 * KB, f"v_lndb{i}", [128, LCN * TT], BF16) for i in range(2)]
        wkvb_sb = sm.at(R2 + 40 * KB, "v_wkvb", [128, LCN * 512], BF16)
        aT_sb = [sm.at(R2 + 44 * KB + i * KB, f"v_aT{i}", [128, TT], BF16) for i in range(2)]

        psf = [st.enter_context(nc.psum_tensor(f"psf{i}", [128, 512], F32)) for i in range(6)]
        pst = [st.enter_context(nc.psum_tensor(f"pst{i}", [128, 1024], BF16)) for i in range(2)]

        P = Plan(nc)
        for s in ("sS", "sG", "sSC", "sT", "sV", "sA", "sC",
                  "sS0", "sS1", "sG0", "sG1", "sSC0", "sSC1", "sW0", "sW1", "sP"):
            P.mksem(st, s)

        for dst, src in (
            (c_id, c_id_d), (c_h128, c_h128_d), (c_h64, c_h64_d), (c_idf, c_idf_d),
            (c_ms, c_ms_d), (c_mask, c_mask_d), (c_cs, c_cs_d), (c_ones, c_ones_d),
            (c_onesrow, c_onesrow_d), (c_wqb, c_wqb_d), (c_outb, c_outb_d),
            (c_nw, c_nw_d), (c_neginf, c_neginf_d), (c_eps, c_eps_d),
        ):
            P.dma("sync", dst[:, :], src[:, :], "sS")
        P.dma("sync", x_sb.ap().rearrange("p (dc t) -> p dc t", t=TOK),
              x_in.ap().rearrange("(dc p) t -> p dc t", p=128), "sS")
        P.barrier()

        pbank = {i: ("sV", 0) for i in range(6)}

        def load_w2c(c, wait_sT):
            par = c & 1
            P.wait("gpsimd", "sT", wait_sT)
            return P.dma("gpsimd", w2c_sb[par][:, :],
                         w2t[c * 128:(c + 1) * 128, :], f"sG{par}")

        w2c_ld = {0: load_w2c(0, 0), 1: load_w2c(1, 0)}
        w2c_done = {}
        w2a_free = [0, 0]
        rot = 0
        for c in range(HCN):
            par = c & 1
            P.wait("tensor", f"sG{par}", w2c_ld[c])
            banks = [(rot + s4) % 6 for s4 in range(4)]
            t_done = None
            for s4, bk in enumerate(banks):
                sname, v = pbank[bk]
                P.wait("tensor", sname, v)
                t_done = P.op("tensor", "matmul", psf[bk][:, :], c_h128[:, :],
                              w2c_sb[par][:, s4 * 512:(s4 + 1) * 512],
                              start=True, stop=True,
                              inc=("sT" if s4 == 3 else None))
            w2c_done[c] = t_done
            if c + 2 < HCN:
                w2c_ld[c + 2] = load_w2c(c + 2, t_done)
            P.wait("vector", "sT", t_done)
            P.wait("vector", f"sW{par}", w2a_free[par])
            for s4, bk in enumerate(banks):
                v_done = P.op("vector", "tensor_copy",
                              w2a_sb[par][:, s4 * 512:(s4 + 1) * 512], psf[bk][:, :],
                              inc="sV")
                pbank[bk] = ("sV", v_done)
            rot = (rot + 4) % 6
            P.wait("sync", "sV", v_done)
            w2a_free[par] = P.dma("sync", w2a_d[c * 128:(c + 1) * 128, :],
                                  w2a_sb[par][:, :], f"sW{par}")
        P.barrier()

        w2a_r = w2a_d.ap().rearrange("(c pz) d -> pz c d", pz=128)
        w2e_r = w2eff_d.ap().rearrange("dc hrow hc m -> hrow hc dc m")
        pbank = {i: ("sV", 0) for i in range(6)}

        def load_wc(pp, wait_sT):
            par = pp & 1
            P.wait("sync", "sT", wait_sT)
            P.dma("sync", wc_sb[par][0:64, :], w2a_r[2 * pp], f"sS{par}")
            return P.dma("sync", wc_sb[par][64:128, :], w2a_r[2 * pp + 1],
                         f"sS{par}")

        wc_ld = {0: load_wc(0, 0), 1: load_wc(1, 0)}
        weff_free = [0, 0]
        rot = 0
        for pp in range(64):
            par = pp & 1
            P.wait("tensor", f"sS{par}", wc_ld[pp])
            banks = [(rot + s4) % 6 for s4 in range(4)]
            t_done = None
            for s4, bk in enumerate(banks):
                sname, v = pbank[bk]
                P.wait("tensor", sname, v)
                t_done = P.op("tensor", "matmul", psf[bk][:, :], c_h64[:, :],
                              wc_sb[par][:, s4 * 512:(s4 + 1) * 512],
                              start=True, stop=True,
                              inc=("sT" if s4 == 3 else None))
            if pp + 2 < 64:
                wc_ld[pp + 2] = load_wc(pp + 2, t_done)
            P.wait("vector", "sT", t_done)
            P.wait("vector", f"sW{par}", weff_free[par])
            for s4, bk in enumerate(banks):
                v_done = P.op("vector", "tensor_scalar",
                              out=weff_sb[par][:, s4 * 512:(s4 + 1) * 512],
                              in0=psf[bk][:, :], scalar1=RSQN, scalar2=None,
                              op0=ALU.mult, inc="sV")
                pbank[bk] = ("sV", v_done)
            rot = (rot + 4) % 6
            P.wait("sync", "sV", v_done)
            P.dma("sync", w2e_r[2 * pp],
                  weff_sb[par][0:64, :].rearrange("c (dc m) -> c dc m", m=128),
                  f"sW{par}")
            weff_free[par] = P.dma(
                "sync", w2e_r[2 * pp + 1],
                weff_sb[par][64:128, :].rearrange("c (dc m) -> c dc m", m=128),
                f"sW{par}")
        P.barrier()

        for hl in range(2):
            P.op("vector", "tensor_scalar", out=rB512[:, 0:128], in0=c_idf[:, :],
                 scalar1=c_cs[:, 2 * hl:2 * hl + 1], scalar2=None, op0=ALU.mult)
            P.op("vector", "scalar_tensor_tensor", out=R_sb[hl][:, :], in0=c_ms[:, :],
                 scalar=c_cs[:, 2 * hl + 1:2 * hl + 2], in1=rB512[:, 0:128],
                 op0=ALU.mult, op1=ALU.add, inc="sV")
        P.barrier()

        def emit_ln(src_sb):
            sq_par = 0
            consumed = [0, 0]
            for tt in range(2):
                for dc in range(DCN):
                    xs = src_sb[:, dc * TOK + tt * TT: dc * TOK + (tt + 1) * TT]
                    P.wait("scalar", "sT", consumed[sq_par])
                    a_done = P.op("scalar", "activation", out=sc512[sq_par][:, :],
                                  in_=xs, func=AF.Square, inc="sA")
                    P.wait("tensor", "sA", a_done)
                    P.op("tensor", "matmul", psf[2 + tt][0:1, :], c_ones[:, :],
                         sc512[sq_par][:, :],
                         start=(dc == 0), stop=(dc == DCN - 1))
                    t_done = P.op("tensor", "matmul", psf[tt][0:1, :], c_ones[:, :], xs,
                                  start=(dc == 0), stop=(dc == DCN - 1), inc="sT")
                    consumed[sq_par] = t_done
                    sq_par ^= 1
            P.wait("vector", "sT", P.cnt["sT"])
            for tt in range(2):
                rsl = slice(tt * TT, (tt + 1) * TT)
                P.op("vector", "tensor_scalar", out=r_a[0:1, rsl], in0=psf[tt][0:1, :],
                     scalar1=1.0 / D, scalar2=None, op0=ALU.mult)
                P.op("vector", "tensor_scalar", out=r_b[0:1, rsl], in0=psf[2 + tt][0:1, :],
                     scalar1=1.0 / D, scalar2=None, op0=ALU.mult)
                P.op("vector", "tensor_mul", psf[4][0:1, :], r_a[0:1, rsl], r_a[0:1, rsl])
                v_done = P.op("vector", "tensor_sub", r_b[0:1, rsl], r_b[0:1, rsl],
                              psf[4][0:1, :], inc="sV")
                P.wait("scalar", "sV", v_done)
                a_done = P.op("scalar", "activation", out=psf[5][0:1, :], in_=r_b[0:1, rsl],
                              func=AF.Sqrt, bias=c_eps[0:1, 0:1], scale=1.0, inc="sA")
                P.wait("vector", "sA", a_done)
                P.op("vector", "reciprocal", r_b[0:1, rsl], psf[5][0:1, :], inc="sV")
            P.wait("tensor", "sV", P.cnt["sV"])
            bc_done = None
            for tt in range(2):
                rsl = slice(tt * TT, (tt + 1) * TT)
                P.op("tensor", "matmul", psf[2 + tt][:, :], c_onesrow[:, :], r_a[0:1, rsl],
                     start=True, stop=True)
                bc_done = P.op("tensor", "matmul", psf[4 + tt][:, :], c_onesrow[:, :],
                               r_b[0:1, rsl], start=True, stop=True,
                               inc=("sT" if tt == 1 else None))
            P.wait("vector", "sT", bc_done)
            par = 0
            for dc in range(DCN):
                for tt in range(2):
                    xs = src_sb[:, dc * TOK + tt * TT: dc * TOK + (tt + 1) * TT]
                    hs = hT_sb[:, dc * TOK + tt * TT: dc * TOK + (tt + 1) * TT]
                    P.op("vector", "tensor_sub", sc512[par][:, :], xs, psf[2 + tt][:, :])
                    P.op("vector", "tensor_mul", hs, sc512[par][:, :], psf[4 + tt][:, :],
                         inc="sV")
                    par ^= 1
            P.barrier()

        def emit_ffn(res_sb):
            for qtr in range(4):
                gbank = {i: ("sV", 0) for i in range(6)}

                def load_w13(i, wait_sT):
                    hc = qtr * 16 + i
                    par = i & 1
                    P.wait("gpsimd", "sT", wait_sT)
                    P.dma("gpsimd",
                          w1b[par].ap().rearrange("p (dc m) -> p dc m", m=128),
                          w1t[:, hc * 128:(hc + 1) * 128].rearrange("(dc p) m -> p dc m", p=128),
                          f"sG{par}")
                    return P.dma(
                        "gpsimd",
                        w3b[par].ap().rearrange("p (dc m) -> p dc m", m=128),
                        w3t[:, hc * 128:(hc + 1) * 128].rearrange("(dc p) m -> p dc m", p=128),
                        f"sG{par}")

                w_ld = {0: load_w13(0, 0), 1: load_w13(1, 0)}
                for i in range(16):
                    par = i & 1
                    b = [(4 * i) % 6, (4 * i + 1) % 6, (4 * i + 2) % 6, (4 * i + 3) % 6]
                    P.wait("tensor", f"sG{par}", w_ld[i])
                    for bb in b:
                        sname, v = gbank[bb]
                        P.wait("tensor", sname, v)
                    for k in range(DCN):
                        lhs = w1b[par][:, k * 128:(k + 1) * 128]
                        P.op("tensor", "matmul", psf[b[0]][:, :], lhs,
                             hT_sb[:, k * TOK:k * TOK + TT],
                             start=(k == 0), stop=(k == DCN - 1))
                        P.op("tensor", "matmul", psf[b[1]][:, :], lhs,
                             hT_sb[:, k * TOK + TT:k * TOK + 2 * TT],
                             start=(k == 0), stop=(k == DCN - 1))
                    t_done = None
                    for k in range(DCN):
                        lhs = w3b[par][:, k * 128:(k + 1) * 128]
                        P.op("tensor", "matmul", psf[b[2]][:, :], lhs,
                             hT_sb[:, k * TOK:k * TOK + TT],
                             start=(k == 0), stop=(k == DCN - 1))
                        t_done = P.op("tensor", "matmul", psf[b[3]][:, :], lhs,
                                      hT_sb[:, k * TOK + TT:k * TOK + 2 * TT],
                                      start=(k == 0), stop=(k == DCN - 1),
                                      inc=("sT" if k == DCN - 1 else None))
                    if i + 2 < 16:
                        w_ld[i + 2] = load_w13(i + 2, t_done)
                    P.wait("scalar", "sT", t_done)
                    P.wait("vector", "sT", t_done)
                    for tt in range(2):
                        spar = (2 * i + tt) & 1
                        a_done = P.op("scalar", "activation", out=sc512[spar][:, :],
                                      in_=psf[b[tt]][:, :], func=AF.Silu, inc="sA")
                        P.wait("vector", "sA", a_done)
                        v_done = P.op("vector", "tensor_mul",
                                      g_sb[:, i * TOK + tt * TT:i * TOK + (tt + 1) * TT],
                                      sc512[spar][:, :], psf[b[2 + tt]][:, :], inc="sV")
                        gbank[b[tt]] = ("sA", a_done)
                        gbank[b[2 + tt]] = ("sV", v_done)
                        P.wait("scalar", "sV", v_done)
                g_ready = P.cnt["sV"]

                def load_w2e(dc, wait_sT):
                    par = dc & 1
                    P.wait("sync", "sT", wait_sT)
                    return P.dma(
                        "sync", w2eb[par].ap().rearrange("p (hc m) -> p hc m", m=128),
                        w2eff_d[dc, :, qtr * 16:(qtr + 1) * 16, :], f"sS{par}")

                e_ld = {0: load_w2e(0, 0), 1: load_w2e(1, 0)}
                mb_free = [0, 0, 0, 0]
                P.wait("tensor", "sV", g_ready)
                for dc in range(DCN):
                    par = dc & 1
                    P.wait("tensor", f"sS{par}", e_ld[dc])
                    bk0, bk1 = (2 * dc) % 4, (2 * dc) % 4 + 1
                    P.wait("tensor", "sV", mb_free[bk0])
                    P.wait("tensor", "sV", mb_free[bk1])
                    t_done = None
                    for hcl in range(16):
                        lhs = w2eb[par][:, hcl * 128:(hcl + 1) * 128]
                        P.op("tensor", "matmul", psf[bk0][:, :], lhs,
                             g_sb[:, hcl * TOK:hcl * TOK + TT],
                             start=(hcl == 0), stop=(hcl == 15))
                        t_done = P.op("tensor", "matmul", psf[bk1][:, :], lhs,
                                      g_sb[:, hcl * TOK + TT:hcl * TOK + 2 * TT],
                                      start=(hcl == 0), stop=(hcl == 15),
                                      inc=("sT" if hcl == 15 else None))
                    P.wait("vector", "sT", t_done)
                    for tt, bk in ((0, bk0), (1, bk1)):
                        mb_free[bk] = P.op(
                            "vector", "scalar_tensor_tensor",
                            out=res_sb[:, dc * TOK + tt * TT:dc * TOK + (tt + 1) * TT],
                            in0=psf[bk][:, :], scalar=0.5,
                            in1=res_sb[:, dc * TOK + tt * TT:dc * TOK + (tt + 1) * TT],
                            op0=ALU.mult, op1=ALU.add, inc="sV")
                    if dc + 2 < DCN:
                        e_ld[dc + 2] = load_w2e(dc + 2, t_done)
                P.barrier()

        emit_ln(x_sb)
        emit_ffn(x_sb)
        if stage == "ffn1":
            fin = P.dma("sync", out_x.ap().rearrange("(dc p) t -> p dc t", p=128),
                        x_sb.ap().rearrange("p (dc t) -> p dc t", t=TOK), "sS")
            P.wait("sync", "sS", fin)
            _emit_block(nc, P)
            return nc

        emit_ln(x_sb)
        P.dma("scalar", ghl_c[0:D, :].rearrange("(dc p) t -> p dc t", p=128),
              hT_sb.ap().rearrange("p (dc t) -> p dc t", t=TOK), "sSC")
        st_free = [0, 0]
        for tt in range(2):
            def load_wat(lc, wait_sT):
                par = lc & 1
                P.wait("gpsimd", "sT", wait_sT)
                return P.dma(
                    "gpsimd", watb[par].ap().rearrange("p (dc m) -> p dc m", m=128),
                    wkv_at[:, lc * 128:(lc + 1) * 128].rearrange("(dc p) m -> p dc m", p=128),
                    f"sG{par}")

            at_ld = {0: load_wat(0, 0), 1: load_wat(1, 0)}
            for lc in range(LCN):
                par = lc & 1
                P.wait("tensor", f"sG{par}", at_ld[lc])
                t_done = None
                for k in range(DCN):
                    t_done = P.op("tensor", "matmul", psf[lc][:, :],
                                  watb[par][:, k * 128:(k + 1) * 128],
                                  hT_sb[:, k * TOK + tt * TT:k * TOK + (tt + 1) * TT],
                                  start=(k == 0), stop=(k == DCN - 1),
                                  inc=("sT" if k == DCN - 1 else None))
                if lc + 2 < LCN:
                    at_ld[lc + 2] = load_wat(lc + 2, t_done)
            lat_done = P.cnt["sT"]
            P.wait("scalar", "sT", lat_done)
            sq_done = {}
            for lc in range(LCN):
                spar = lc & 1
                P.wait("scalar", "sT", sq_done.get(lc - 2, 0))
                a_done = P.op("scalar", "activation", out=sc512[spar][:, :],
                              in_=psf[lc][:, :], func=AF.Square, inc="sA")
                P.wait("tensor", "sA", a_done)
                sq_done[lc] = P.op("tensor", "matmul", psf[4][0:1, :], c_ones[:, :],
                                   sc512[spar][:, :], start=(lc == 0),
                                   stop=(lc == LCN - 1), inc="sT")
            P.wait("vector", "sT", P.cnt["sT"])
            v_done = P.op("vector", "tensor_scalar", out=r_b[0:1, 0:TT],
                          in0=psf[4][0:1, :], scalar1=1.0 / DL, scalar2=None,
                          op0=ALU.mult, inc="sV")
            P.wait("scalar", "sV", v_done)
            a_done = P.op("scalar", "activation", out=psf[5][0:1, :], in_=r_b[0:1, 0:TT],
                          func=AF.Sqrt, bias=c_eps[0:1, 1:2], scale=1.0, inc="sA")
            P.wait("vector", "sA", a_done)
            v_done = P.op("vector", "reciprocal", r_b[0:1, 0:TT], psf[5][0:1, :], inc="sV")
            P.wait("tensor", "sV", v_done)
            t_done = P.op("tensor", "matmul", psf[5][:, :], c_onesrow[:, :],
                          r_b[0:1, 0:TT], start=True, stop=True, inc="sT")
            P.wait("vector", "sT", t_done)
            P.op("vector", "tensor_copy", rB512[:, :], psf[5][:, :])
            for lc in range(LCN):
                spar = lc & 1
                P.wait("vector", f"sSC{spar}", st_free[spar])
                v_done = P.op("vector", "scalar_tensor_tensor", out=stg[spar][:, :],
                              in0=psf[lc][:, :], scalar=c_nw[:, lc:lc + 1],
                              in1=rB512[:, :], op0=ALU.mult, op1=ALU.mult, inc="sV")
                P.wait("scalar", "sV", v_done)
                st_free[spar] = P.dma(
                    "scalar",
                    ghl_c[D + lc * 128:D + (lc + 1) * 128, tt * TT:(tt + 1) * TT],
                    stg[spar][:, :], f"sSC{spar}")
            P.barrier()
        P.dma("scalar", xspill_d.ap().rearrange("(dc p) t -> p dc t", p=128),
              x_sb.ap().rearrange("p (dc t) -> p dc t", t=TOK), "sSC")
        P.barrier()
        P.op("gpsimd", "collective_compute", "AllGather", ALU.bypass,
             replica_groups=[list(range(NCORES))],
             ins=[ghl_c.ap().opt()], outs=[ghl_g.ap().opt()], inc="sC")
        P.barrier()

        ghl_r = ghl_g.ap().rearrange("(rk f20 p) t -> rk p f20 t", rk=8, f20=20, p=128)

        P.dma("gpsimd", wql_sb.ap().rearrange("p (dc m) -> p dc m", m=256),
              wqt_s.ap().rearrange("(dc p) m -> p dc m", p=128), "sG")
        P.dma("gpsimd", wkvb_sb.ap().rearrange("p (lc f) -> p lc f", f=512),
              wkvb_s.ap().rearrange("(lc p) f -> p lc f", p=128), "sG")
        P.barrier()

        def load_h2(t16, wait_sT):
            par = t16 & 1
            rk, slot = t16 // 2, t16 % 2
            P.wait("sync", "sT", wait_sT)
            return P.dma(
                "sync", h2b[par].ap().rearrange("p (dc t) -> p dc t", t=TT),
                ghl_r[rk, :, 0:16, slot * TT:(slot + 1) * TT], f"sS{par}")

        h2_ld = {0: load_h2(0, 0), 1: load_h2(1, 0)}
        sc_free = [0, 0]
        bfree = {}
        for t16 in range(16):
            par = t16 & 1
            P.wait("tensor", f"sS{par}", h2_ld[t16])
            t_last = None
            for hl in range(2):
                bq = (2 * t16 + hl) % 3
                bq2 = 3 + (2 * t16 + hl) % 3
                sname, v = bfree.get(bq, ("sV", 0))
                P.wait("tensor", sname, v)
                t_done = None
                for k in range(DCN):
                    t_done = P.op("tensor", "matmul", psf[bq][:, :],
                                  wql_sb[:, k * 256 + hl * 128:k * 256 + (hl + 1) * 128],
                                  h2b[par][:, k * TT:(k + 1) * TT],
                                  start=(k == 0), stop=(k == DCN - 1),
                                  inc=("sT" if k == DCN - 1 else None))
                spar = (2 * t16 + hl) & 1
                P.wait("vector", "sT", t_done)
                P.wait("vector", "sT", sc_free[spar])
                v_done = P.op("vector", "tensor_scalar", out=sc512[spar][:, :],
                              in0=psf[bq][:, :], scalar1=c_wqb[:, hl:hl + 1],
                              scalar2=None, op0=ALU.add, inc="sV")
                bfree[bq] = ("sV", v_done)
                sname, v = bfree.get(bq2, ("sV", 0))
                P.wait("tensor", sname, v)
                P.wait("tensor", "sV", v_done)
                t2 = P.op("tensor", "matmul", psf[bq2][:, :], R_sb[hl][:, :],
                          sc512[spar][:, :], start=True, stop=True, inc="sT")
                sc_free[spar] = t2
                t_last = t2
                P.wait("vector", "sT", t2)
                v2 = P.op("vector", "tensor_copy",
                          qT[hl][:, t16 * TT:(t16 + 1) * TT], psf[bq2][:, :], inc="sV")
                bfree[bq2] = ("sV", v2)
            if t16 + 2 < 16:
                h2_ld[t16 + 2] = load_h2(t16 + 2, t_last)
        P.barrier()

        if stage == "attn":
            P.dma("sync", d_ghl.ap(), ghl_g.ap(), "sS")
            P.dma("sync", d_q0[:, :], qT[0][:, :], "sS")
            P.barrier()

        for hl in range(2):
            def load_ln(t16, wait_sT):
                par = t16 & 1
                rk, slot = t16 // 2, t16 % 2
                P.wait("sync", "sT", wait_sT)
                return P.dma(
                    "sync", ln_db[par].ap().rearrange("p (lc t) -> p lc t", t=TT),
                    ghl_r[rk, :, 16:20, slot * TT:(slot + 1) * TT], f"sS{par}")

            ln_ld = {0: load_ln(0, 0), 1: load_ln(1, 0)}
            scp = 0
            vtp = 0
            bfree = {}
            tfree = {0: ("sV", 0), 1: ("sV", 0)}
            for t16 in range(16):
                par = t16 & 1
                P.wait("tensor", f"sS{par}", ln_ld[t16])
                bk, bv, br = (3 * t16) % 6, (3 * t16 + 1) % 6, (3 * t16 + 2) % 6
                for bb in (bk, bv, br):
                    sname, v = bfree.get(bb, ("sV", 0))
                    P.wait("tensor", sname, v)
                for lc in range(LCN):
                    P.op("tensor", "matmul", psf[bk][:, :],
                         wkvb_sb[:, lc * 512 + hl * 128:lc * 512 + (hl + 1) * 128],
                         ln_db[par][:, lc * TT:(lc + 1) * TT],
                         start=(lc == 0), stop=(lc == LCN - 1))
                t_done = None
                for lc in range(LCN):
                    t_done = P.op(
                        "tensor", "matmul", psf[bv][:, :],
                        wkvb_sb[:, lc * 512 + 256 + hl * 128:lc * 512 + 256 + (hl + 1) * 128],
                        ln_db[par][:, lc * TT:(lc + 1) * TT],
                        start=(lc == 0), stop=(lc == LCN - 1),
                        inc=("sT" if lc == LCN - 1 else None))
                if t16 + 2 < 16:
                    ln_ld[t16 + 2] = load_ln(t16 + 2, t_done)
                P.wait("vector", "sT", t_done)
                P.wait("vector", "sT", sc_free[scp])
                vk = P.op("vector", "tensor_copy", sc512[scp][:, :], psf[bk][:, :],
                          inc="sV")
                bfree[bk] = ("sV", vk)
                P.wait("tensor", "sV", vk)
                tk = P.op("tensor", "matmul", psf[br][:, :], R_sb[hl][:, :],
                          sc512[scp][:, :], start=True, stop=True, inc="sT")
                sc_free[scp] = tk
                scp ^= 1
                P.wait("vector", "sT", tk)
                vk2 = P.op("vector", "tensor_copy",
                           K_sb[:, t16 * TT:(t16 + 1) * TT], psf[br][:, :], inc="sV")
                bfree[br] = ("sV", vk2)
                vv = P.op("vector", "tensor_copy", stg[vtp][:, :], psf[bv][:, :],
                          inc="sV")
                bfree[bv] = ("sV", vv)
                tb = t16 & 1
                sname, v = tfree[tb]
                P.wait("tensor", sname, v)
                P.wait("tensor", "sV", vv)
                tv = None
                for j in range(4):
                    tv = P.op("tensor", "transpose", pst[tb][:, j * 128:(j + 1) * 128],
                              stg[vtp][:, j * 128:(j + 1) * 128], c_id[:, :],
                              inc=("sT" if j == 3 else None))
                vtp ^= 1
                P.wait("vector", "sT", tv)
                vv2 = P.op("vector", "tensor_copy",
                           V_sb[:, t16 * TT:(t16 + 1) * TT], pst[tb][:, 0:512], inc="sV")
                tfree[tb] = ("sV", vv2)
            P.barrier()
            if stage == "attn" and hl == 0:
                P.dma("sync", d_K[:, :], K_sb[:, :], "sS")
                P.dma("sync", d_V[:, :], V_sb[:, :], "sS")
                P.barrier()

            st_c = {
                "s_free": [0, 0], "pr_free": [0, 0], "l_free": [0, 0],
                "aT_free": [0, 0], "ost_free": [0, 0],
                "sbank": {i: ("sV", 0) for i in range(4)},
                "tpar": {0: ("sV", 0), 1: ("sV", 0)},
                "oQ": ("sV", 0),
                "scale_done": [0, 0],
                "rl_free": [0, 0],
            }

            def emit_A_step(p, kc):
                cov = p // 2 + 1
                for sj in range(2):
                    js = 2 * p + sj
                    spar = (2 * kc + sj) & 3
                    sname, v = st_c["sbank"][spar]
                    P.wait("tensor", sname, v)
                    tsc = P.op("tensor", "matmul", psf[spar][:, :],
                               qT[hl][:, js * 128:(js + 1) * 128],
                               K_sb[:, kc * TT:(kc + 1) * TT],
                               start=True, stop=True, inc="sT")
                    P.wait("vector", "sT", tsc)
                    if kc == 0:
                        P.wait("vector", "sA", st_c["s_free"][sj])
                    mprev = c_neginf if kc == 0 else m_t[sj][(kc - 1) & 1]
                    ssl = s_sc[sj][:, kc * TT:(kc + 1) * TT]
                    if kc == cov - 1:
                        mv = (js % 4) * TT
                        P.op("vector", "scalar_tensor_tensor", out=ssl,
                             in0=psf[spar][:, :], scalar=SCALE_QK,
                             in1=c_mask[:, mv:mv + TT], op0=ALU.mult, op1=ALU.add)
                        vdone = P.op("vector", "tensor_scalar", out=ssl, in0=ssl,
                                     scalar1=1.0, scalar2=mprev[:, :],
                                     op0=ALU.mult, op1=ALU.max,
                                     accum_out=m_t[sj][kc & 1][:, :], inc="sV")
                    else:
                        vdone = P.op("vector", "tensor_scalar", out=ssl,
                                     in0=psf[spar][:, :], scalar1=SCALE_QK,
                                     scalar2=mprev[:, :], op0=ALU.mult, op1=ALU.max,
                                     accum_out=m_t[sj][kc & 1][:, :], inc="sV")
                    st_c["sbank"][spar] = ("sV", vdone)

            def emit_B(p):
                cov = p // 2 + 1
                for sj in range(2):
                    mfin = m_t[sj][(cov - 1) & 1]
                    nm = P.op("vector", "tensor_scalar", out=negm_t[sj][:, :],
                              in0=mfin[:, :], scalar1=-1.0, scalar2=None,
                              op0=ALU.mult, inc="sV")
                    P.wait("scalar", "sV", nm)
                    P.wait("scalar", "sT", st_c["pr_free"][sj])
                    P.wait("scalar", "sP", st_c["scale_done"][sj])
                    adone = P.op("scalar", "activation",
                                 out=pr[sj][:, 0:cov * TT],
                                 in_=s_sc[sj][:, 0:cov * TT],
                                 func=AF.Exp, bias=negm_t[sj][:, :], scale=1.0,
                                 accum_out=l_t[sj][:, :], inc="sA")
                    st_c["s_free"][sj] = adone
                    P.wait("vector", "sA", adone)
                    P.wait("vector", "sP", st_c["rl_free"][sj])
                    rdone = P.op("vector", "reciprocal", rl_t[sj][:, :],
                                 l_t[sj][:, :], inc="sV")
                    P.wait("gpsimd", "sA", adone)
                    P.wait("gpsimd", "sV", rdone)
                    st_c["scale_done"][sj] = P.op(
                        "gpsimd", "tensor_scalar", out=pr[sj][:, 0:cov * TT],
                        in0=pr[sj][:, 0:cov * TT], scalar1=rl_t[sj][:, :],
                        scalar2=None, op0=ALU.mult, inc="sP")
                    st_c["rl_free"][sj] = st_c["scale_done"][sj]

            def make_C_steps(p):
                cov = p // 2 + 1
                opar = p & 1
                steps = []

                def pre():
                    P.wait("tensor", "sP", max(st_c["scale_done"]))
                    sname, v = st_c["oQ"]
                    P.wait("tensor", sname, v)
                steps.append(pre)

                def upstep(up, last):
                    u0, u1 = 2 * up, 2 * up + 1
                    tb = up & 1
                    sname, v = st_c["tpar"][tb]
                    P.wait("tensor", sname, v)
                    ttr = None
                    for pos, (sj, u) in enumerate(((0, u0), (1, u0), (0, u1), (1, u1))):
                        ttr = P.op("tensor", "transpose",
                                   pst[tb][:, pos * 128:(pos + 1) * 128],
                                   pr[sj][:, u * 128:(u + 1) * 128], c_id[:, :],
                                   inc=("sT" if pos == 3 else None))
                    P.wait("vector", "sT", ttr)
                    P.wait("vector", "sT", st_c["aT_free"][tb])
                    vev = P.op("vector", "tensor_copy", aT_sb[tb][:, :],
                               pst[tb][:, 0:512], inc="sV")
                    st_c["tpar"][tb] = ("sV", vev)
                    P.wait("tensor", "sV", vev)
                    P.op("tensor", "matmul", psf[4][:, 0:256],
                         V_sb[:, u0 * 128:(u0 + 1) * 128], aT_sb[tb][:, 0:256],
                         start=(up == 0), stop=False)
                    last_av = P.op("tensor", "matmul", psf[4][:, 0:256],
                                   V_sb[:, u1 * 128:(u1 + 1) * 128],
                                   aT_sb[tb][:, 256:512],
                                   start=False, stop=last, inc="sT")
                    st_c["aT_free"][tb] = last_av
                    if last:
                        st_c["pr_free"][0] = last_av
                        st_c["pr_free"][1] = last_av
                        P.wait("vector", "sT", last_av)
                        P.wait("vector", f"sSC{opar}", st_c["ost_free"][opar])
                        vo = P.op("vector", "tensor_copy", ost[opar][:, :],
                                  psf[4][:, 0:256], inc="sV")
                        st_c["oQ"] = ("sV", vo)
                        P.wait("scalar", "sV", vo)
                        st_c["ost_free"][opar] = P.dma(
                            "scalar",
                            oc_c[(p // 4) * 256 + hl * 128:
                                 (p // 4) * 256 + (hl + 1) * 128,
                                 (p % 4) * 256:(p % 4 + 1) * 256],
                            ost[opar][:, :], f"sSC{opar}")

                for up in range(cov * 2):
                    steps.append((lambda u=up, lst=(up == cov * 2 - 1):
                                  upstep(u, lst)))
                return steps

            cq = []
            for p in range(32):
                cov = p // 2 + 1
                for kc in range(cov):
                    emit_A_step(p, kc)
                    for _ in range(3):
                        if cq:
                            cq.pop(0)()
                while cq:
                    cq.pop(0)()
                emit_B(p)
                cq = make_C_steps(p)
            while cq:
                cq.pop(0)()
            P.barrier()

        if stage == "attn":
            P.dma("sync", d_oc.ap(), oc_c.ap(), "sS")
            P.barrier()

        P.op("gpsimd", "collective_compute", "AllToAll", ALU.bypass,
             replica_groups=[list(range(NCORES))],
             ins=[oc_c.ap().opt()], outs=[oa_r.ap().opt()], inc="sC")
        P.barrier()

        if stage == "attn":
            P.dma("sync", d_oa.ap(), oa_r.ap(), "sS")
            P.barrier()

        P.dma("sync", x2_sb.ap().rearrange("p (dc t) -> p dc t", t=TOK),
              xspill_d.ap().rearrange("(dc p) t -> p dc t", p=128), "sS")
        for fc in range(DCN):
            P.dma("sync", oT_sb[:, fc * TOK:(fc + 1) * TOK],
                  oa_r[fc * 128:(fc + 1) * 128, :], "sS")
        P.barrier()
        for dhalf in range(2):
            wl = P.dma("gpsimd", owt_sb.ap().rearrange("p (fc d) -> p fc d", d=TOK),
                       owt[:, dhalf * TOK:(dhalf + 1) * TOK].rearrange("(fc p) d -> p fc d", p=128),
                       "sG")
            P.wait("tensor", "sG", wl)
            obank = {0: ("sV", 0), 1: ("sV", 0)}
            step = 0
            for dc8 in range(8):
                dc = dhalf * 8 + dc8
                for tt in range(2):
                    bk = step & 1
                    sname, v = obank[bk]
                    P.wait("tensor", sname, v)
                    t_done = None
                    for fc in range(DCN):
                        t_done = P.op(
                            "tensor", "matmul", psf[bk][:, :],
                            owt_sb[:, fc * TOK + dc8 * 128:fc * TOK + (dc8 + 1) * 128],
                            oT_sb[:, fc * TOK + tt * TT:fc * TOK + (tt + 1) * TT],
                            start=(fc == 0), stop=(fc == DCN - 1),
                            inc=("sT" if fc == DCN - 1 else None))
                    P.wait("vector", "sT", t_done)
                    vdone = P.op("vector", "scalar_tensor_tensor",
                                 out=x2_sb[:, dc * TOK + tt * TT:dc * TOK + (tt + 1) * TT],
                                 in0=psf[bk][:, :], scalar=c_outb[:, dc:dc + 1],
                                 in1=x2_sb[:, dc * TOK + tt * TT:dc * TOK + (tt + 1) * TT],
                                 op0=ALU.add, op1=ALU.add, inc="sV")
                    obank[bk] = ("sV", vdone)
                    step += 1
            P.barrier()

        if stage == "attn":
            fin = P.dma("sync", out_x.ap().rearrange("(dc p) t -> p dc t", p=128),
                        x2_sb.ap().rearrange("p (dc t) -> p dc t", t=TOK), "sS")
            P.wait("sync", "sS", fin)
            _emit_block(nc, P)
            return nc

        emit_ln(x2_sb)
        emit_ffn(x2_sb)

        fin = P.dma("sync", out_x.ap().rearrange("(dc p) t -> p dc t", p=128),
                    x2_sb.ap().rearrange("p (dc t) -> p dc t", t=TOK), "sS")
        P.wait("sync", "sS", fin)
        _emit_block(nc, P)
    return nc


def _emit_block(nc, P):
    with nc.Block() as blk:
        @blk.tensor
        def _(eng):
            P.run(eng, "tensor")

        @blk.vector
        def _(eng):
            P.run(eng, "vector")

        @blk.scalar
        def _(eng):
            P.run(eng, "scalar")

        @blk.sync
        def _(eng):
            P.run(eng, "sync")

        @blk.gpsimd
        def _(eng):
            P.run(eng, "gpsimd")


def _host_prepare(inputs):
    x = np.asarray(inputs["L0_attn_x"], np.float32)[0]
    freqs = np.asarray(inputs["L0_attn_freqs"], np.float32)
    wq = np.asarray(inputs["L0_attn_wq_w"], np.float32)
    wqb = np.asarray(inputs["L0_attn_wq_b"], np.float32)
    wkva = np.asarray(inputs["L0_attn_wkv_a_w"], np.float32)
    wkvb = np.asarray(inputs["L0_attn_wkv_b_w"], np.float32)
    nw = np.asarray(inputs["L0_attn_norm_w"], np.float32)
    ow = np.asarray(inputs["L0_attn_out_w"], np.float32)
    ob = np.asarray(inputs["L0_attn_out_b"], np.float32)
    W1 = np.asarray(inputs["L0_ffn_W1"], np.float32)
    W2 = np.asarray(inputs["L0_ffn_W2"], np.float32)
    W3 = np.asarray(inputs["L0_ffn_W3"], np.float32)

    xT = np.ascontiguousarray(x.T)
    w1t = np.ascontiguousarray(W1.T)
    w3t = np.ascontiguousarray(W3.T)
    w2t = np.ascontiguousarray(W2.T)
    wqt = np.ascontiguousarray(wq.T)
    wkv_at = np.ascontiguousarray(wkva.T)
    wkv_bt = np.ascontiguousarray(wkvb.T)
    owt = np.ascontiguousarray(ow.T)

    h128 = _hadamard(128).astype(BF16NP)
    h64 = _hadamard(64)
    h64bd = np.zeros((128, 128), np.float32)
    h64bd[:64, :64] = h64
    h64bd[64:, 64:] = h64
    h64bd = h64bd.astype(BF16NP)
    ident = np.eye(128, dtype=np.float32).astype(BF16NP)
    identf = np.eye(128, dtype=np.float32)
    ms = np.zeros((128, 128), np.float32)
    for i in range(32):
        e, od = 64 + 2 * i, 64 + 2 * i + 1
        ms[e, od] = 1.0
        ms[od, e] = -1.0
    mask = np.zeros((128, 4 * TT), np.float32)
    q_i = np.arange(128)[:, None]
    kt_i = np.arange(TT)[None, :]
    for v in range(4):
        mask[:, v * TT:(v + 1) * TT] = np.where(kt_i <= v * 128 + q_i, 0.0, MASKVAL)
    ones = np.ones((128, 1), np.float32)
    onesrow = np.ones((1, 128), np.float32)
    outb = np.ascontiguousarray(ob.reshape(DCN, 128).T)
    nwt = np.ascontiguousarray(nw.reshape(LCN, 128).T)
    neginf = np.full((128, 1), -3.0e38, np.float32)
    epsc = np.array([[EPS_LN, EPS_RMS]], np.float32)

    ims = []
    for r in range(NCORES):
        g0, g1 = 2 * r, 2 * r + 1
        cs = np.zeros((128, 4), np.float32)
        for li, g in enumerate((g0, g1)):
            cvec = np.ones(128, np.float32)
            svec = np.zeros(128, np.float32)
            cvec[64 + 2 * np.arange(32)] = freqs[g, :, 0]
            cvec[64 + 2 * np.arange(32) + 1] = freqs[g, :, 0]
            svec[64 + 2 * np.arange(32)] = freqs[g, :, 1]
            svec[64 + 2 * np.arange(32) + 1] = freqs[g, :, 1]
            cs[:, 2 * li] = cvec
            cs[:, 2 * li + 1] = svec
        wkvb_sl = np.ascontiguousarray(np.concatenate(
            [wkv_bt[:, g0 * 128:(g0 + 1) * 128], wkv_bt[:, g1 * 128:(g1 + 1) * 128],
             wkv_bt[:, 2048 + g0 * 128:2048 + (g0 + 1) * 128],
             wkv_bt[:, 2048 + g1 * 128:2048 + (g1 + 1) * 128]], axis=1))
        wq_sl = np.ascontiguousarray(wqt[:, g0 * 128:(g1 + 1) * 128])
        wqb_sl = np.ascontiguousarray(
            np.stack([wqb[g0 * 128:(g0 + 1) * 128], wqb[g1 * 128:(g1 + 1) * 128]],
                     axis=1))
        ims.append(dict(
            x_in=np.ascontiguousarray(xT[:, r * TOK:(r + 1) * TOK]),
            w1t=w1t, w3t=w3t, w2t=w2t, wqt_s=wq_sl, wkv_at=wkv_at,
            wkvb_s=wkvb_sl, owt=owt,
            c_id=ident, c_h128=h128, c_h64=h64bd, c_idf=identf, c_ms=ms,
            c_mask=mask, c_cs=cs, c_ones=ones, c_onesrow=onesrow,
            c_wqb=wqb_sl, c_outb=outb, c_nw=nwt, c_neginf=neginf, c_eps=epsc,
        ))
    return ims


def kernel(**inputs):
    key = _STAGE
    if key not in _CACHE:
        _CACHE[key] = build_program(key)
    nc = _CACHE[key]
    ims = _host_prepare(inputs)
    res = run_bass_kernel_spmd(nc, ims, core_ids=list(range(NCORES)))
    out = np.concatenate([np.asarray(res.results[r]["out_x"]) for r in range(NCORES)],
                         axis=1)
    return np.ascontiguousarray(out.T).reshape(1, S, D).astype(np.float32)


# version 03
# speedup vs baseline: 1.7554x; 1.7554x over previous
import math
import os
import numpy as np
import ml_dtypes
import concourse.bass as bass
import concourse.mybir as mybir
from concourse.bass_utils import run_bass_kernel_spmd
from contextlib import ExitStack

os.environ.setdefault("TRN_ALLOW_ICI_ALL_TO_ALL", "1")

F32 = mybir.dt.float32
BF16 = mybir.dt.bfloat16
AF = mybir.ActivationFunctionType
ALU = mybir.AluOpType
BF16NP = ml_dtypes.bfloat16

S, D, NCORES = 8192, 2048, 8
TOK = S // NCORES
TT = 512
DCN = 16
HID, HCN = 8192, 64
NH, DH, DL, LCN = 16, 128, 512, 4
SCALE_QK = 1.0 / math.sqrt(128.0)
RSQN = 1.0 / math.sqrt(8192.0)
EPS_LN, EPS_RMS = 1e-5, 1e-6
MASKVAL = -1.0e30

ENGS = ("tensor", "vector", "scalar", "sync", "gpsimd")

_STAGE = "full"
_CACHE = {}


def _hadamard(n):
    h = np.array([[1.0]], np.float32)
    while h.shape[0] < n:
        h = np.block([[h, h], [h, -h]])
    return h.astype(np.float32)


class Plan:
    SELF_SEM = {"vector": "sV", "scalar": "sA"}

    def __init__(self, nc):
        self.nc = nc
        self.ops = {e: [] for e in ENGS}
        self.sems = {}
        self.cnt = {}
        self.last_w = {e: {} for e in ENGS}
        self.last_r = {e: {} for e in ENGS}
        self.guar = {e: {} for e in ENGS}

    def mksem(self, st, name):
        h = st.enter_context(self.nc.semaphore(name))
        self.sems[name] = h
        self.cnt[name] = 0

    def wait(self, eng, sem, val):
        if val is None or val <= 0:
            return
        if self.guar[eng].get(sem, 0) >= val:
            return
        self.guar[eng][sem] = val
        self.ops[eng].append(("wait", self.sems[sem], val))

    @staticmethod
    def _apname(v):
        t = getattr(v, "tensor", None)
        if t is not None:
            return getattr(t, "name", None)
        if hasattr(v, "name") and hasattr(v, "dtype"):
            return v.name
        return None

    def _classify(self, args, kw):
        writes, reads = [], []
        for i, a in enumerate(args):
            nm = self._apname(a)
            if nm is not None:
                (writes if i == 0 else reads).append(nm)
        for k, v in kw.items():
            nm = self._apname(v)
            if nm is not None:
                (writes if k in ("out", "accum_out") else reads).append(nm)
        return writes, reads

    def op(self, eng, method, *args, inc=None, incv=1, **kw):
        if eng in self.SELF_SEM and method != "dma_start":
            ssem = self.SELF_SEM[eng]
            writes, reads = self._classify(args, kw)
            need = 0
            for nm in writes + reads:
                need = max(need, self.last_w[eng].get(nm, 0))
            for nm in writes:
                need = max(need, self.last_r[eng].get(nm, 0))
            self.wait(eng, ssem, need)
            if inc is None:
                inc = ssem
            assert inc == ssem, (eng, inc)
            self.cnt[inc] += incv
            tot = self.cnt[inc]
            for nm in writes:
                self.last_w[eng][nm] = tot
            for nm in reads:
                self.last_r[eng][nm] = tot
            self.ops[eng].append(("op", method, args, kw, self.sems[inc], incv))
            return tot
        sem = self.sems[inc] if inc is not None else None
        tot = None
        if inc is not None:
            self.cnt[inc] += incv
            tot = self.cnt[inc]
        self.ops[eng].append(("op", method, args, kw, sem, incv))
        return tot

    def dma(self, eng, out, in_, inc, **kw):
        return self.op(eng, "dma_start", out=out, in_=in_, inc=inc, incv=16, **kw)

    def barrier(self):
        snap = dict(self.cnt)
        for e in ENGS:
            for s, v in snap.items():
                self.wait(e, s, v)

    def run(self, eng_obj, eng_name):
        for rec in self.ops[eng_name]:
            if rec[0] == "wait":
                eng_obj.wait_ge(rec[1], rec[2])
            else:
                _, method, args, kw, sem, iv = rec
                inst = getattr(eng_obj, method)(*args, **kw)
                if sem is not None:
                    inst.then_inc(sem, iv)


class SbufMap:
    def __init__(self, nc, st, total_bytes):
        self.nc = nc
        self.total = total_bytes
        self.base = (nc.sbuf_base + 31) // 32 * 32
        assert self.base + total_bytes <= nc.sbuf_top, (self.base, nc.sbuf_top)

    def at(self, offset, name, shape, dtype):
        per_part = int(np.prod(shape[1:])) * mybir.dt.size(dtype)
        assert offset + per_part <= self.total, (name, offset, per_part)
        return self.nc.alloc_sbuf_tensor_at(name, shape, dtype,
                                            offset=self.base + offset)


def build_program(stage):
    nc = bass.Bass("TRN2", target_bir_lowering=False)

    def din(name, shape, dt=F32):
        return nc.dram_tensor(name, shape, dt, kind="ExternalInput")

    x_in = din("x_in", [D, TOK])
    w1t = din("w1t", [D, HID])
    w3t = din("w3t", [D, HID])
    w2t = din("w2t", [HID, D])
    wqt_s = din("wqt_s", [D, 256])
    wkv_at = din("wkv_at", [D, DL])
    wkvb_s = din("wkvb_s", [DL, 512])
    owt = din("owt", [D, D])
    c_id_d = din("c_id", [128, 128], BF16)
    c_h128_d = din("c_h128", [128, 128], BF16)
    c_h64_d = din("c_h64", [128, 128], BF16)
    c_idf_d = din("c_idf", [128, 128])
    c_ms_d = din("c_ms", [128, 128])
    c_mask_d = din("c_mask", [128, 4 * TT])
    c_cs_d = din("c_cs", [128, 4])
    c_ones_d = din("c_ones", [128, 1])
    c_onesrow_d = din("c_onesrow", [1, 128])
    c_wqb_d = din("c_wqb", [128, 2])
    c_outb_d = din("c_outb", [128, DCN])
    c_nw_d = din("c_nw", [128, LCN])
    c_neginf_d = din("c_neginf", [128, 1])
    c_eps_d = din("c_eps", [1, 2])

    w2a_d = nc.dram_tensor("w2a", [HID, D], BF16)
    w2eff_d = nc.dram_tensor("w2eff", [DCN, 128, HCN, 128], BF16)
    xspill_d = nc.dram_tensor("xspill", [D, TOK], F32)
    ghl_c = nc.dram_tensor("ghl_c", [D + DL, TOK], BF16)
    ghl_g = nc.dram_tensor("ghl_g", [NCORES * (D + DL), TOK], BF16, addr_space="Shared")
    oc_c = nc.dram_tensor("oc_c", [D, TOK], BF16)
    oa_r = nc.dram_tensor("oa_r", [D, TOK], BF16)
    out_x = nc.dram_tensor("out_x", [D, TOK], F32, kind="ExternalOutput")
    if stage == "attn":
        d_ghl = nc.dram_tensor("d_ghl", [NCORES * (D + DL), TOK], BF16,
                               kind="ExternalOutput")
        d_q0 = nc.dram_tensor("d_q0", [128, S], BF16, kind="ExternalOutput")
        d_K = nc.dram_tensor("d_K", [128, S], BF16, kind="ExternalOutput")
        d_V = nc.dram_tensor("d_V", [128, S], BF16, kind="ExternalOutput")
        d_oc = nc.dram_tensor("d_oc", [D, TOK], BF16, kind="ExternalOutput")
        d_oa = nc.dram_tensor("d_oa", [D, TOK], BF16, kind="ExternalOutput")

    st = ExitStack()
    with st:
        KB = 1024
        sm = SbufMap(nc, st, 206 * KB)
        o = 0

        def alloc(name, shape, dtype):
            nonlocal o
            h = sm.at(o, name, shape, dtype)
            o += (int(np.prod(shape[1:])) * mybir.dt.size(dtype) + 31) // 32 * 32
            return h

        c_id = alloc("s_id", [128, 128], BF16)
        c_h128 = alloc("s_h128", [128, 128], BF16)
        c_h64 = alloc("s_h64", [128, 128], BF16)
        c_idf = alloc("s_idf", [128, 128], F32)
        c_ms = alloc("s_ms", [128, 128], F32)
        c_mask = alloc("s_mask", [128, 4 * TT], F32)
        c_cs = alloc("s_cs", [128, 4], F32)
        c_ones = alloc("s_ones", [128, 1], F32)
        c_onesrow = alloc("s_onesrow", [1, 128], F32)
        c_wqb = alloc("s_wqb", [128, 2], F32)
        c_outb = alloc("s_outb", [128, DCN], F32)
        c_nw = alloc("s_nw", [128, LCN], F32)
        c_neginf = alloc("s_neginf", [128, 1], F32)
        c_eps = alloc("s_eps", [1, 2], F32)
        r_a = alloc("s_ra", [1, TOK], F32)
        r_b = alloc("s_rb", [1, TOK], F32)
        rB512 = alloc("s_rB512", [128, TT], F32)
        sc512 = [alloc(f"s_sc{i}", [128, TT], F32) for i in range(2)]
        stg = [alloc(f"s_stg{i}", [128, TT], BF16) for i in range(2)]
        R_sb = [alloc(f"s_R{i}", [128, 128], F32) for i in range(2)]
        ost = [alloc(f"s_ost{i}", [128, 256], BF16) for i in range(2)]
        m_t = [[alloc(f"s_m{sj}{pp}", [128, 1], F32) for pp in range(2)] for sj in range(2)]
        l_t = [alloc(f"s_l{sj}", [128, 1], F32) for sj in range(2)]
        negm_t = [alloc(f"s_negm{sj}", [128, 1], F32) for sj in range(2)]
        rl_t = [alloc(f"s_rl{sj}", [128, 1], F32) for sj in range(2)]
        assert o <= 30 * KB, o

        RX, R1, R2 = 30 * KB, 94 * KB, 158 * KB
        x_sb = sm.at(RX, "v_x", [128, DCN * TOK], F32)
        h2b = [sm.at(RX + i * 16 * KB, f"v_h2b{i}", [128, DCN * TT], BF16) for i in range(2)]
        s_sc = [sm.at(RX + i * 32 * KB, f"v_s{i}", [128, 16 * TT], F32) for i in range(2)]
        x2_sb = sm.at(RX, "v_x2", [128, DCN * TOK], F32)

        w2c_sb = [sm.at(R1 + i * 4 * KB, f"v_w2c{i}", [128, D], BF16) for i in range(2)]
        w2a_sb = [sm.at(R1 + 8 * KB + i * 4 * KB, f"v_w2a{i}", [128, D], BF16) for i in range(2)]
        wc_sb = [sm.at(R1 + 16 * KB + i * 4 * KB, f"v_wc{i}", [128, D], BF16) for i in range(2)]
        weff_sb = [sm.at(R1 + 24 * KB + i * 4 * KB, f"v_weff{i}", [128, D], BF16) for i in range(2)]
        g_sb = sm.at(R1, "v_g", [128, 16 * TOK], BF16)
        w2eb = [sm.at(R1 + 32 * KB + i * 4 * KB, f"v_w2eb{i}", [128, 16 * 128], BF16) for i in range(2)]
        qT = [sm.at(R1 + i * 16 * KB, f"v_qT{i}", [128, S], BF16) for i in range(2)]
        K_sb = sm.at(R1 + 32 * KB, "v_K", [128, S], BF16)
        V_sb = sm.at(R1 + 48 * KB, "v_V", [128, S], BF16)
        owt_sb = sm.at(R1, "v_owt", [128, DCN * TOK], BF16)
        oT_sb = sm.at(R1 + 32 * KB, "v_oT", [128, DCN * TOK], BF16)

        hT_sb = sm.at(R2, "v_hT", [128, DCN * TOK], BF16)
        w1b = [sm.at(R2 + 32 * KB + i * 4 * KB, f"v_w1b{i}", [128, DCN * 128], BF16) for i in range(2)]
        w3b = [sm.at(R2 + 40 * KB + i * 4 * KB, f"v_w3b{i}", [128, DCN * 128], BF16) for i in range(2)]
        watb = [sm.at(R2 + 32 * KB + i * 4 * KB, f"v_watb{i}", [128, DCN * 128], BF16) for i in range(2)]
        wql_sb = sm.at(R2, "v_wql", [128, DCN * 256], BF16)
        pr = [sm.at(R2 + i * 16 * KB, f"v_pr{i}", [128, 16 * TT], BF16) for i in range(2)]
        ln_db = [sm.at(R2 + 32 * KB + i * 4 * KB, f"v_lndb{i}", [128, LCN * TT], BF16) for i in range(2)]
        wkvb_sb = sm.at(R2 + 40 * KB, "v_wkvb", [128, LCN * 512], BF16)
        aT_sb = [sm.at(R2 + 44 * KB + i * KB, f"v_aT{i}", [128, TT], BF16) for i in range(2)]

        psf = [st.enter_context(nc.psum_tensor(f"psf{i}", [128, 512], F32)) for i in range(6)]
        pst = [st.enter_context(nc.psum_tensor(f"pst{i}", [128, 1024], BF16)) for i in range(2)]

        P = Plan(nc)
        for s in ("sS", "sG", "sSC", "sT", "sV", "sA", "sC",
                  "sS0", "sS1", "sG0", "sG1", "sSC0", "sSC1", "sW0", "sW1", "sP"):
            P.mksem(st, s)

        for dst, src in (
            (c_id, c_id_d), (c_h128, c_h128_d), (c_h64, c_h64_d), (c_idf, c_idf_d),
            (c_ms, c_ms_d), (c_mask, c_mask_d), (c_cs, c_cs_d), (c_ones, c_ones_d),
            (c_onesrow, c_onesrow_d), (c_wqb, c_wqb_d), (c_outb, c_outb_d),
            (c_nw, c_nw_d), (c_neginf, c_neginf_d), (c_eps, c_eps_d),
        ):
            P.dma("sync", dst[:, :], src[:, :], "sS")
        P.dma("sync", x_sb.ap().rearrange("p (dc t) -> p dc t", t=TOK),
              x_in.ap().rearrange("(dc p) t -> p dc t", p=128), "sS")
        P.barrier()

        pbank = {i: ("sV", 0) for i in range(6)}

        def load_w2c(c, wait_sT):
            par = c & 1
            P.wait("gpsimd", "sT", wait_sT)
            return P.dma("gpsimd", w2c_sb[par][:, :],
                         w2t[c * 128:(c + 1) * 128, :], f"sG{par}")

        w2c_ld = {0: load_w2c(0, 0), 1: load_w2c(1, 0)}
        w2c_done = {}
        w2a_free = [0, 0]
        rot = 0
        for c in range(HCN):
            par = c & 1
            P.wait("tensor", f"sG{par}", w2c_ld[c])
            banks = [(rot + s4) % 6 for s4 in range(4)]
            t_done = None
            for s4, bk in enumerate(banks):
                sname, v = pbank[bk]
                P.wait("tensor", sname, v)
                t_done = P.op("tensor", "matmul", psf[bk][:, :], c_h128[:, :],
                              w2c_sb[par][:, s4 * 512:(s4 + 1) * 512],
                              start=True, stop=True,
                              inc=("sT" if s4 == 3 else None))
            w2c_done[c] = t_done
            if c + 2 < HCN:
                w2c_ld[c + 2] = load_w2c(c + 2, t_done)
            P.wait("vector", "sT", t_done)
            P.wait("vector", f"sW{par}", w2a_free[par])
            for s4, bk in enumerate(banks):
                v_done = P.op("vector", "tensor_copy",
                              w2a_sb[par][:, s4 * 512:(s4 + 1) * 512], psf[bk][:, :],
                              inc="sV")
                pbank[bk] = ("sV", v_done)
            rot = (rot + 4) % 6
            P.wait("sync", "sV", v_done)
            w2a_free[par] = P.dma("sync", w2a_d[c * 128:(c + 1) * 128, :],
                                  w2a_sb[par][:, :], f"sW{par}")
        P.barrier()

        w2a_r = w2a_d.ap().rearrange("(c pz) d -> pz c d", pz=128)
        w2e_r = w2eff_d.ap().rearrange("dc hrow hc m -> hrow hc dc m")
        pbank = {i: ("sV", 0) for i in range(6)}

        def load_wc(pp, wait_sT):
            par = pp & 1
            P.wait("sync", "sT", wait_sT)
            P.dma("sync", wc_sb[par][0:64, :], w2a_r[2 * pp], f"sS{par}")
            return P.dma("sync", wc_sb[par][64:128, :], w2a_r[2 * pp + 1],
                         f"sS{par}")

        wc_ld = {0: load_wc(0, 0), 1: load_wc(1, 0)}
        weff_free = [0, 0]
        rot = 0
        for pp in range(64):
            par = pp & 1
            P.wait("tensor", f"sS{par}", wc_ld[pp])
            banks = [(rot + s4) % 6 for s4 in range(4)]
            t_done = None
            for s4, bk in enumerate(banks):
                sname, v = pbank[bk]
                P.wait("tensor", sname, v)
                t_done = P.op("tensor", "matmul", psf[bk][:, :], c_h64[:, :],
                              wc_sb[par][:, s4 * 512:(s4 + 1) * 512],
                              start=True, stop=True,
                              inc=("sT" if s4 == 3 else None))
            if pp + 2 < 64:
                wc_ld[pp + 2] = load_wc(pp + 2, t_done)
            P.wait("vector", "sT", t_done)
            P.wait("vector", f"sW{par}", weff_free[par])
            for s4, bk in enumerate(banks):
                v_done = P.op("vector", "tensor_scalar",
                              out=weff_sb[par][:, s4 * 512:(s4 + 1) * 512],
                              in0=psf[bk][:, :], scalar1=RSQN, scalar2=None,
                              op0=ALU.mult, inc="sV")
                pbank[bk] = ("sV", v_done)
            rot = (rot + 4) % 6
            P.wait("sync", "sV", v_done)
            P.dma("sync", w2e_r[2 * pp],
                  weff_sb[par][0:64, :].rearrange("c (dc m) -> c dc m", m=128),
                  f"sW{par}")
            weff_free[par] = P.dma(
                "sync", w2e_r[2 * pp + 1],
                weff_sb[par][64:128, :].rearrange("c (dc m) -> c dc m", m=128),
                f"sW{par}")
        P.barrier()

        for hl in range(2):
            P.op("vector", "tensor_scalar", out=rB512[:, 0:128], in0=c_idf[:, :],
                 scalar1=c_cs[:, 2 * hl:2 * hl + 1], scalar2=None, op0=ALU.mult)
            P.op("vector", "scalar_tensor_tensor", out=R_sb[hl][:, :], in0=c_ms[:, :],
                 scalar=c_cs[:, 2 * hl + 1:2 * hl + 2], in1=rB512[:, 0:128],
                 op0=ALU.mult, op1=ALU.add, inc="sV")
        P.barrier()

        def emit_ln(src_sb):
            sq_par = 0
            consumed = [0, 0]
            for tt in range(2):
                for dc in range(DCN):
                    xs = src_sb[:, dc * TOK + tt * TT: dc * TOK + (tt + 1) * TT]
                    P.wait("scalar", "sT", consumed[sq_par])
                    a_done = P.op("scalar", "activation", out=sc512[sq_par][:, :],
                                  in_=xs, func=AF.Square, inc="sA")
                    P.wait("tensor", "sA", a_done)
                    P.op("tensor", "matmul", psf[2 + tt][0:1, :], c_ones[:, :],
                         sc512[sq_par][:, :],
                         start=(dc == 0), stop=(dc == DCN - 1))
                    t_done = P.op("tensor", "matmul", psf[tt][0:1, :], c_ones[:, :], xs,
                                  start=(dc == 0), stop=(dc == DCN - 1), inc="sT")
                    consumed[sq_par] = t_done
                    sq_par ^= 1
            P.wait("vector", "sT", P.cnt["sT"])
            for tt in range(2):
                rsl = slice(tt * TT, (tt + 1) * TT)
                P.op("vector", "tensor_scalar", out=r_a[0:1, rsl], in0=psf[tt][0:1, :],
                     scalar1=1.0 / D, scalar2=None, op0=ALU.mult)
                P.op("vector", "tensor_scalar", out=r_b[0:1, rsl], in0=psf[2 + tt][0:1, :],
                     scalar1=1.0 / D, scalar2=None, op0=ALU.mult)
                P.op("vector", "tensor_mul", psf[4][0:1, :], r_a[0:1, rsl], r_a[0:1, rsl])
                v_done = P.op("vector", "tensor_sub", r_b[0:1, rsl], r_b[0:1, rsl],
                              psf[4][0:1, :], inc="sV")
                P.wait("scalar", "sV", v_done)
                a_done = P.op("scalar", "activation", out=psf[5][0:1, :], in_=r_b[0:1, rsl],
                              func=AF.Sqrt, bias=c_eps[0:1, 0:1], scale=1.0, inc="sA")
                P.wait("vector", "sA", a_done)
                P.op("vector", "reciprocal", r_b[0:1, rsl], psf[5][0:1, :], inc="sV")
            P.wait("tensor", "sV", P.cnt["sV"])
            bc_done = None
            for tt in range(2):
                rsl = slice(tt * TT, (tt + 1) * TT)
                P.op("tensor", "matmul", psf[2 + tt][:, :], c_onesrow[:, :], r_a[0:1, rsl],
                     start=True, stop=True)
                bc_done = P.op("tensor", "matmul", psf[4 + tt][:, :], c_onesrow[:, :],
                               r_b[0:1, rsl], start=True, stop=True,
                               inc=("sT" if tt == 1 else None))
            P.wait("vector", "sT", bc_done)
            par = 0
            for dc in range(DCN):
                for tt in range(2):
                    xs = src_sb[:, dc * TOK + tt * TT: dc * TOK + (tt + 1) * TT]
                    hs = hT_sb[:, dc * TOK + tt * TT: dc * TOK + (tt + 1) * TT]
                    P.op("vector", "tensor_sub", sc512[par][:, :], xs, psf[2 + tt][:, :])
                    P.op("vector", "tensor_mul", hs, sc512[par][:, :], psf[4 + tt][:, :],
                         inc="sV")
                    par ^= 1
            P.barrier()

        def emit_ffn(res_sb):
            for qtr in range(4):
                gbank = {i: ("sV", 0) for i in range(6)}

                def load_w13(i, wait_sT):
                    hc = qtr * 16 + i
                    par = i & 1
                    P.wait("gpsimd", "sT", wait_sT)
                    P.dma("gpsimd",
                          w1b[par].ap().rearrange("p (dc m) -> p dc m", m=128),
                          w1t[:, hc * 128:(hc + 1) * 128].rearrange("(dc p) m -> p dc m", p=128),
                          f"sG{par}")
                    return P.dma(
                        "gpsimd",
                        w3b[par].ap().rearrange("p (dc m) -> p dc m", m=128),
                        w3t[:, hc * 128:(hc + 1) * 128].rearrange("(dc p) m -> p dc m", p=128),
                        f"sG{par}")

                w_ld = {0: load_w13(0, 0), 1: load_w13(1, 0)}
                for i in range(16):
                    par = i & 1
                    b = [(4 * i) % 6, (4 * i + 1) % 6, (4 * i + 2) % 6, (4 * i + 3) % 6]
                    P.wait("tensor", f"sG{par}", w_ld[i])
                    for bb in b:
                        sname, v = gbank[bb]
                        P.wait("tensor", sname, v)
                    for k in range(DCN):
                        lhs = w1b[par][:, k * 128:(k + 1) * 128]
                        P.op("tensor", "matmul", psf[b[0]][:, :], lhs,
                             hT_sb[:, k * TOK:k * TOK + TT],
                             start=(k == 0), stop=(k == DCN - 1))
                        P.op("tensor", "matmul", psf[b[1]][:, :], lhs,
                             hT_sb[:, k * TOK + TT:k * TOK + 2 * TT],
                             start=(k == 0), stop=(k == DCN - 1))
                    t_done = None
                    for k in range(DCN):
                        lhs = w3b[par][:, k * 128:(k + 1) * 128]
                        P.op("tensor", "matmul", psf[b[2]][:, :], lhs,
                             hT_sb[:, k * TOK:k * TOK + TT],
                             start=(k == 0), stop=(k == DCN - 1))
                        t_done = P.op("tensor", "matmul", psf[b[3]][:, :], lhs,
                                      hT_sb[:, k * TOK + TT:k * TOK + 2 * TT],
                                      start=(k == 0), stop=(k == DCN - 1),
                                      inc=("sT" if k == DCN - 1 else None))
                    if i + 2 < 16:
                        w_ld[i + 2] = load_w13(i + 2, t_done)
                    P.wait("scalar", "sT", t_done)
                    P.wait("vector", "sT", t_done)
                    for tt in range(2):
                        spar = (2 * i + tt) & 1
                        a_done = P.op("scalar", "activation", out=sc512[spar][:, :],
                                      in_=psf[b[tt]][:, :], func=AF.Silu, inc="sA")
                        P.wait("vector", "sA", a_done)
                        v_done = P.op("vector", "tensor_mul",
                                      g_sb[:, i * TOK + tt * TT:i * TOK + (tt + 1) * TT],
                                      sc512[spar][:, :], psf[b[2 + tt]][:, :], inc="sV")
                        gbank[b[tt]] = ("sA", a_done)
                        gbank[b[2 + tt]] = ("sV", v_done)
                        P.wait("scalar", "sV", v_done)
                g_ready = P.cnt["sV"]

                def load_w2e(dc, wait_sT):
                    par = dc & 1
                    P.wait("sync", "sT", wait_sT)
                    return P.dma(
                        "sync", w2eb[par].ap().rearrange("p (hc m) -> p hc m", m=128),
                        w2eff_d[dc, :, qtr * 16:(qtr + 1) * 16, :], f"sS{par}")

                e_ld = {0: load_w2e(0, 0), 1: load_w2e(1, 0)}
                mb_free = [0, 0, 0, 0]
                P.wait("tensor", "sV", g_ready)
                for dc in range(DCN):
                    par = dc & 1
                    P.wait("tensor", f"sS{par}", e_ld[dc])
                    bk0, bk1 = (2 * dc) % 4, (2 * dc) % 4 + 1
                    P.wait("tensor", "sV", mb_free[bk0])
                    P.wait("tensor", "sV", mb_free[bk1])
                    t_done = None
                    for hcl in range(16):
                        lhs = w2eb[par][:, hcl * 128:(hcl + 1) * 128]
                        P.op("tensor", "matmul", psf[bk0][:, :], lhs,
                             g_sb[:, hcl * TOK:hcl * TOK + TT],
                             start=(hcl == 0), stop=(hcl == 15))
                        t_done = P.op("tensor", "matmul", psf[bk1][:, :], lhs,
                                      g_sb[:, hcl * TOK + TT:hcl * TOK + 2 * TT],
                                      start=(hcl == 0), stop=(hcl == 15),
                                      inc=("sT" if hcl == 15 else None))
                    P.wait("vector", "sT", t_done)
                    for tt, bk in ((0, bk0), (1, bk1)):
                        mb_free[bk] = P.op(
                            "vector", "scalar_tensor_tensor",
                            out=res_sb[:, dc * TOK + tt * TT:dc * TOK + (tt + 1) * TT],
                            in0=psf[bk][:, :], scalar=0.5,
                            in1=res_sb[:, dc * TOK + tt * TT:dc * TOK + (tt + 1) * TT],
                            op0=ALU.mult, op1=ALU.add, inc="sV")
                    if dc + 2 < DCN:
                        e_ld[dc + 2] = load_w2e(dc + 2, t_done)
                P.barrier()

        emit_ln(x_sb)
        emit_ffn(x_sb)
        if stage == "ffn1":
            fin = P.dma("sync", out_x.ap().rearrange("(dc p) t -> p dc t", p=128),
                        x_sb.ap().rearrange("p (dc t) -> p dc t", t=TOK), "sS")
            P.wait("sync", "sS", fin)
            _emit_block(nc, P)
            return nc

        emit_ln(x_sb)
        P.dma("scalar", ghl_c[0:D, :].rearrange("(dc p) t -> p dc t", p=128),
              hT_sb.ap().rearrange("p (dc t) -> p dc t", t=TOK), "sSC")
        st_free = [0, 0]
        for tt in range(2):
            def load_wat(lc, wait_sT):
                par = lc & 1
                P.wait("gpsimd", "sT", wait_sT)
                return P.dma(
                    "gpsimd", watb[par].ap().rearrange("p (dc m) -> p dc m", m=128),
                    wkv_at[:, lc * 128:(lc + 1) * 128].rearrange("(dc p) m -> p dc m", p=128),
                    f"sG{par}")

            at_ld = {0: load_wat(0, 0), 1: load_wat(1, 0)}
            for lc in range(LCN):
                par = lc & 1
                P.wait("tensor", f"sG{par}", at_ld[lc])
                t_done = None
                for k in range(DCN):
                    t_done = P.op("tensor", "matmul", psf[lc][:, :],
                                  watb[par][:, k * 128:(k + 1) * 128],
                                  hT_sb[:, k * TOK + tt * TT:k * TOK + (tt + 1) * TT],
                                  start=(k == 0), stop=(k == DCN - 1),
                                  inc=("sT" if k == DCN - 1 else None))
                if lc + 2 < LCN:
                    at_ld[lc + 2] = load_wat(lc + 2, t_done)
            lat_done = P.cnt["sT"]
            P.wait("scalar", "sT", lat_done)
            sq_done = {}
            for lc in range(LCN):
                spar = lc & 1
                P.wait("scalar", "sT", sq_done.get(lc - 2, 0))
                a_done = P.op("scalar", "activation", out=sc512[spar][:, :],
                              in_=psf[lc][:, :], func=AF.Square, inc="sA")
                P.wait("tensor", "sA", a_done)
                sq_done[lc] = P.op("tensor", "matmul", psf[4][0:1, :], c_ones[:, :],
                                   sc512[spar][:, :], start=(lc == 0),
                                   stop=(lc == LCN - 1), inc="sT")
            P.wait("vector", "sT", P.cnt["sT"])
            v_done = P.op("vector", "tensor_scalar", out=r_b[0:1, 0:TT],
                          in0=psf[4][0:1, :], scalar1=1.0 / DL, scalar2=None,
                          op0=ALU.mult, inc="sV")
            P.wait("scalar", "sV", v_done)
            a_done = P.op("scalar", "activation", out=psf[5][0:1, :], in_=r_b[0:1, 0:TT],
                          func=AF.Sqrt, bias=c_eps[0:1, 1:2], scale=1.0, inc="sA")
            P.wait("vector", "sA", a_done)
            v_done = P.op("vector", "reciprocal", r_b[0:1, 0:TT], psf[5][0:1, :], inc="sV")
            P.wait("tensor", "sV", v_done)
            t_done = P.op("tensor", "matmul", psf[5][:, :], c_onesrow[:, :],
                          r_b[0:1, 0:TT], start=True, stop=True, inc="sT")
            P.wait("vector", "sT", t_done)
            P.op("vector", "tensor_copy", rB512[:, :], psf[5][:, :])
            for lc in range(LCN):
                spar = lc & 1
                P.wait("vector", f"sSC{spar}", st_free[spar])
                v_done = P.op("vector", "scalar_tensor_tensor", out=stg[spar][:, :],
                              in0=psf[lc][:, :], scalar=c_nw[:, lc:lc + 1],
                              in1=rB512[:, :], op0=ALU.mult, op1=ALU.mult, inc="sV")
                P.wait("scalar", "sV", v_done)
                st_free[spar] = P.dma(
                    "scalar",
                    ghl_c[D + lc * 128:D + (lc + 1) * 128, tt * TT:(tt + 1) * TT],
                    stg[spar][:, :], f"sSC{spar}")
            P.barrier()
        P.dma("scalar", xspill_d.ap().rearrange("(dc p) t -> p dc t", p=128),
              x_sb.ap().rearrange("p (dc t) -> p dc t", t=TOK), "sSC")
        P.barrier()
        P.op("gpsimd", "collective_compute", "AllGather", ALU.bypass,
             replica_groups=[list(range(NCORES))],
             ins=[ghl_c.ap().opt()], outs=[ghl_g.ap().opt()], inc="sC")
        P.barrier()

        ghl_r = ghl_g.ap().rearrange("(rk f20 p) t -> rk p f20 t", rk=8, f20=20, p=128)

        P.dma("gpsimd", wql_sb.ap().rearrange("p (dc m) -> p dc m", m=256),
              wqt_s.ap().rearrange("(dc p) m -> p dc m", p=128), "sG")
        P.dma("gpsimd", wkvb_sb.ap().rearrange("p (lc f) -> p lc f", f=512),
              wkvb_s.ap().rearrange("(lc p) f -> p lc f", p=128), "sG")
        P.barrier()

        def load_h2(t16, wait_sT):
            par = t16 & 1
            rk, slot = t16 // 2, t16 % 2
            P.wait("sync", "sT", wait_sT)
            return P.dma(
                "sync", h2b[par].ap().rearrange("p (dc t) -> p dc t", t=TT),
                ghl_r[rk, :, 0:16, slot * TT:(slot + 1) * TT], f"sS{par}")

        h2_ld = {0: load_h2(0, 0), 1: load_h2(1, 0)}
        sc_free = [0, 0]
        bfree = {}
        for t16 in range(16):
            par = t16 & 1
            P.wait("tensor", f"sS{par}", h2_ld[t16])
            t_last = None
            for hl in range(2):
                bq = (2 * t16 + hl) % 3
                bq2 = 3 + (2 * t16 + hl) % 3
                sname, v = bfree.get(bq, ("sV", 0))
                P.wait("tensor", sname, v)
                t_done = None
                for k in range(DCN):
                    t_done = P.op("tensor", "matmul", psf[bq][:, :],
                                  wql_sb[:, k * 256 + hl * 128:k * 256 + (hl + 1) * 128],
                                  h2b[par][:, k * TT:(k + 1) * TT],
                                  start=(k == 0), stop=(k == DCN - 1),
                                  inc=("sT" if k == DCN - 1 else None))
                spar = (2 * t16 + hl) & 1
                P.wait("vector", "sT", t_done)
                P.wait("vector", "sT", sc_free[spar])
                v_done = P.op("vector", "tensor_scalar", out=sc512[spar][:, :],
                              in0=psf[bq][:, :], scalar1=c_wqb[:, hl:hl + 1],
                              scalar2=None, op0=ALU.add, inc="sV")
                bfree[bq] = ("sV", v_done)
                sname, v = bfree.get(bq2, ("sV", 0))
                P.wait("tensor", sname, v)
                P.wait("tensor", "sV", v_done)
                t2 = P.op("tensor", "matmul", psf[bq2][:, :], R_sb[hl][:, :],
                          sc512[spar][:, :], start=True, stop=True, inc="sT")
                sc_free[spar] = t2
                t_last = t2
                P.wait("vector", "sT", t2)
                v2 = P.op("vector", "tensor_copy",
                          qT[hl][:, t16 * TT:(t16 + 1) * TT], psf[bq2][:, :], inc="sV")
                bfree[bq2] = ("sV", v2)
            if t16 + 2 < 16:
                h2_ld[t16 + 2] = load_h2(t16 + 2, t_last)
        P.barrier()

        if stage == "attn":
            P.dma("sync", d_ghl.ap(), ghl_g.ap(), "sS")
            P.dma("sync", d_q0[:, :], qT[0][:, :], "sS")
            P.barrier()

        for hl in range(2):
            def load_ln(t16, wait_sT):
                par = t16 & 1
                rk, slot = t16 // 2, t16 % 2
                P.wait("sync", "sT", wait_sT)
                return P.dma(
                    "sync", ln_db[par].ap().rearrange("p (lc t) -> p lc t", t=TT),
                    ghl_r[rk, :, 16:20, slot * TT:(slot + 1) * TT], f"sS{par}")

            ln_ld = {0: load_ln(0, 0), 1: load_ln(1, 0)}
            scp = 0
            vtp = 0
            bfree = {}
            tfree = {0: ("sV", 0), 1: ("sV", 0)}
            for t16 in range(16):
                par = t16 & 1
                P.wait("tensor", f"sS{par}", ln_ld[t16])
                bk, bv, br = (3 * t16) % 6, (3 * t16 + 1) % 6, (3 * t16 + 2) % 6
                for bb in (bk, bv, br):
                    sname, v = bfree.get(bb, ("sV", 0))
                    P.wait("tensor", sname, v)
                for lc in range(LCN):
                    P.op("tensor", "matmul", psf[bk][:, :],
                         wkvb_sb[:, lc * 512 + hl * 128:lc * 512 + (hl + 1) * 128],
                         ln_db[par][:, lc * TT:(lc + 1) * TT],
                         start=(lc == 0), stop=(lc == LCN - 1))
                t_done = None
                for lc in range(LCN):
                    t_done = P.op(
                        "tensor", "matmul", psf[bv][:, :],
                        wkvb_sb[:, lc * 512 + 256 + hl * 128:lc * 512 + 256 + (hl + 1) * 128],
                        ln_db[par][:, lc * TT:(lc + 1) * TT],
                        start=(lc == 0), stop=(lc == LCN - 1),
                        inc=("sT" if lc == LCN - 1 else None))
                if t16 + 2 < 16:
                    ln_ld[t16 + 2] = load_ln(t16 + 2, t_done)
                P.wait("vector", "sT", t_done)
                P.wait("vector", "sT", sc_free[scp])
                vk = P.op("vector", "tensor_copy", sc512[scp][:, :], psf[bk][:, :],
                          inc="sV")
                bfree[bk] = ("sV", vk)
                P.wait("tensor", "sV", vk)
                tk = P.op("tensor", "matmul", psf[br][:, :], R_sb[hl][:, :],
                          sc512[scp][:, :], start=True, stop=True, inc="sT")
                sc_free[scp] = tk
                scp ^= 1
                P.wait("vector", "sT", tk)
                vk2 = P.op("vector", "tensor_copy",
                           K_sb[:, t16 * TT:(t16 + 1) * TT], psf[br][:, :], inc="sV")
                bfree[br] = ("sV", vk2)
                vv = P.op("vector", "tensor_copy", stg[vtp][:, :], psf[bv][:, :],
                          inc="sV")
                bfree[bv] = ("sV", vv)
                tb = t16 & 1
                sname, v = tfree[tb]
                P.wait("tensor", sname, v)
                P.wait("tensor", "sV", vv)
                tv = None
                for j in range(4):
                    tv = P.op("tensor", "transpose", pst[tb][:, j * 128:(j + 1) * 128],
                              stg[vtp][:, j * 128:(j + 1) * 128], c_id[:, :],
                              inc=("sT" if j == 3 else None))
                vtp ^= 1
                P.wait("vector", "sT", tv)
                vv2 = P.op("vector", "tensor_copy",
                           V_sb[:, t16 * TT:(t16 + 1) * TT], pst[tb][:, 0:512], inc="sV")
                tfree[tb] = ("sV", vv2)
            P.barrier()
            if stage == "attn" and hl == 0:
                P.dma("sync", d_K[:, :], K_sb[:, :], "sS")
                P.dma("sync", d_V[:, :], V_sb[:, :], "sS")
                P.barrier()

            st_c = {
                "s_free": [0, 0], "pr_free": [0, 0], "l_free": [0, 0],
                "aT_free": [0, 0], "ost_free": [0, 0],
                "sbank": {i: ("sV", 0) for i in range(4)},
                "tpar": {0: ("sV", 0), 1: ("sV", 0)},
                "oQ": ("sV", 0),
                "scale_done": [0, 0],
                "rl_free": [0, 0],
            }

            def emit_A_step(p, kc):
                cov = p // 2 + 1
                for sj in range(2):
                    js = 2 * p + sj
                    spar = (2 * kc + sj) & 3
                    sname, v = st_c["sbank"][spar]
                    P.wait("tensor", sname, v)
                    tsc = P.op("tensor", "matmul", psf[spar][:, :],
                               qT[hl][:, js * 128:(js + 1) * 128],
                               K_sb[:, kc * TT:(kc + 1) * TT],
                               start=True, stop=True, inc="sT")
                    P.wait("vector", "sT", tsc)
                    if kc == 0:
                        P.wait("vector", "sA", st_c["s_free"][sj])
                    mprev = c_neginf if kc == 0 else m_t[sj][(kc - 1) & 1]
                    ssl = s_sc[sj][:, kc * TT:(kc + 1) * TT]
                    if kc == cov - 1:
                        mv = (js % 4) * TT
                        P.op("vector", "scalar_tensor_tensor", out=ssl,
                             in0=psf[spar][:, :], scalar=SCALE_QK,
                             in1=c_mask[:, mv:mv + TT], op0=ALU.mult, op1=ALU.add)
                        vdone = P.op("vector", "tensor_scalar", out=ssl, in0=ssl,
                                     scalar1=1.0, scalar2=mprev[:, :],
                                     op0=ALU.mult, op1=ALU.max,
                                     accum_out=m_t[sj][kc & 1][:, :], inc="sV")
                    else:
                        vdone = P.op("vector", "tensor_scalar", out=ssl,
                                     in0=psf[spar][:, :], scalar1=SCALE_QK,
                                     scalar2=mprev[:, :], op0=ALU.mult, op1=ALU.max,
                                     accum_out=m_t[sj][kc & 1][:, :], inc="sV")
                    st_c["sbank"][spar] = ("sV", vdone)

            def emit_B(p):
                cov = p // 2 + 1
                for sj in range(2):
                    mfin = m_t[sj][(cov - 1) & 1]
                    nm = P.op("vector", "tensor_scalar", out=negm_t[sj][:, :],
                              in0=mfin[:, :], scalar1=-1.0, scalar2=None,
                              op0=ALU.mult, inc="sV")
                    P.wait("scalar", "sV", nm)
                    P.wait("scalar", "sT", st_c["pr_free"][sj])
                    adone = P.op("scalar", "activation",
                                 out=pr[sj][:, 0:cov * TT],
                                 in_=s_sc[sj][:, 0:cov * TT],
                                 func=AF.Exp, bias=negm_t[sj][:, :], scale=1.0,
                                 accum_out=l_t[sj][:, :], inc="sA")
                    st_c["s_free"][sj] = adone
                    P.wait("vector", "sA", adone)
                    P.op("vector", "reciprocal", rl_t[sj][:, :], l_t[sj][:, :],
                         inc="sV")
                    st_c["scale_done"][sj] = P.op(
                        "vector", "tensor_scalar", out=pr[sj][:, 0:cov * TT],
                        in0=pr[sj][:, 0:cov * TT], scalar1=rl_t[sj][:, :],
                        scalar2=None, op0=ALU.mult, inc="sV")

            def make_C_steps(p):
                cov = p // 2 + 1
                opar = p & 1
                steps = []

                def pre():
                    P.wait("tensor", "sV", max(st_c["scale_done"]))
                    sname, v = st_c["oQ"]
                    P.wait("tensor", sname, v)
                steps.append(pre)

                def upstep(up, last):
                    u0, u1 = 2 * up, 2 * up + 1
                    tb = up & 1
                    sname, v = st_c["tpar"][tb]
                    P.wait("tensor", sname, v)
                    ttr = None
                    for pos, (sj, u) in enumerate(((0, u0), (1, u0), (0, u1), (1, u1))):
                        ttr = P.op("tensor", "transpose",
                                   pst[tb][:, pos * 128:(pos + 1) * 128],
                                   pr[sj][:, u * 128:(u + 1) * 128], c_id[:, :],
                                   inc=("sT" if pos == 3 else None))
                    P.wait("vector", "sT", ttr)
                    P.wait("vector", "sT", st_c["aT_free"][tb])
                    vev = P.op("vector", "tensor_copy", aT_sb[tb][:, :],
                               pst[tb][:, 0:512], inc="sV")
                    st_c["tpar"][tb] = ("sV", vev)
                    P.wait("tensor", "sV", vev)
                    P.op("tensor", "matmul", psf[4][:, 0:256],
                         V_sb[:, u0 * 128:(u0 + 1) * 128], aT_sb[tb][:, 0:256],
                         start=(up == 0), stop=False)
                    last_av = P.op("tensor", "matmul", psf[4][:, 0:256],
                                   V_sb[:, u1 * 128:(u1 + 1) * 128],
                                   aT_sb[tb][:, 256:512],
                                   start=False, stop=last, inc="sT")
                    st_c["aT_free"][tb] = last_av
                    if last:
                        st_c["pr_free"][0] = last_av
                        st_c["pr_free"][1] = last_av
                        P.wait("vector", "sT", last_av)
                        P.wait("vector", f"sSC{opar}", st_c["ost_free"][opar])
                        vo = P.op("vector", "tensor_copy", ost[opar][:, :],
                                  psf[4][:, 0:256], inc="sV")
                        st_c["oQ"] = ("sV", vo)
                        P.wait("scalar", "sV", vo)
                        st_c["ost_free"][opar] = P.dma(
                            "scalar",
                            oc_c[(p // 4) * 256 + hl * 128:
                                 (p // 4) * 256 + (hl + 1) * 128,
                                 (p % 4) * 256:(p % 4 + 1) * 256],
                            ost[opar][:, :], f"sSC{opar}")

                for up in range(cov * 2):
                    steps.append((lambda u=up, lst=(up == cov * 2 - 1):
                                  upstep(u, lst)))
                return steps

            cq = []
            for p in range(32):
                cov = p // 2 + 1
                for kc in range(cov):
                    emit_A_step(p, kc)
                    for _ in range(3):
                        if cq:
                            cq.pop(0)()
                while cq:
                    cq.pop(0)()
                emit_B(p)
                cq = make_C_steps(p)
            while cq:
                cq.pop(0)()
            P.barrier()

        if stage == "attn":
            P.dma("sync", d_oc.ap(), oc_c.ap(), "sS")
            P.barrier()

        P.op("gpsimd", "collective_compute", "AllToAll", ALU.bypass,
             replica_groups=[list(range(NCORES))],
             ins=[oc_c.ap().opt()], outs=[oa_r.ap().opt()], inc="sC")
        P.barrier()

        if stage == "attn":
            P.dma("sync", d_oa.ap(), oa_r.ap(), "sS")
            P.barrier()

        P.dma("sync", x2_sb.ap().rearrange("p (dc t) -> p dc t", t=TOK),
              xspill_d.ap().rearrange("(dc p) t -> p dc t", p=128), "sS")
        for fc in range(DCN):
            P.dma("sync", oT_sb[:, fc * TOK:(fc + 1) * TOK],
                  oa_r[fc * 128:(fc + 1) * 128, :], "sS")
        P.barrier()
        for dhalf in range(2):
            wl = P.dma("gpsimd", owt_sb.ap().rearrange("p (fc d) -> p fc d", d=TOK),
                       owt[:, dhalf * TOK:(dhalf + 1) * TOK].rearrange("(fc p) d -> p fc d", p=128),
                       "sG")
            P.wait("tensor", "sG", wl)
            obank = {0: ("sV", 0), 1: ("sV", 0)}
            step = 0
            for dc8 in range(8):
                dc = dhalf * 8 + dc8
                for tt in range(2):
                    bk = step & 1
                    sname, v = obank[bk]
                    P.wait("tensor", sname, v)
                    t_done = None
                    for fc in range(DCN):
                        t_done = P.op(
                            "tensor", "matmul", psf[bk][:, :],
                            owt_sb[:, fc * TOK + dc8 * 128:fc * TOK + (dc8 + 1) * 128],
                            oT_sb[:, fc * TOK + tt * TT:fc * TOK + (tt + 1) * TT],
                            start=(fc == 0), stop=(fc == DCN - 1),
                            inc=("sT" if fc == DCN - 1 else None))
                    P.wait("vector", "sT", t_done)
                    vdone = P.op("vector", "scalar_tensor_tensor",
                                 out=x2_sb[:, dc * TOK + tt * TT:dc * TOK + (tt + 1) * TT],
                                 in0=psf[bk][:, :], scalar=c_outb[:, dc:dc + 1],
                                 in1=x2_sb[:, dc * TOK + tt * TT:dc * TOK + (tt + 1) * TT],
                                 op0=ALU.add, op1=ALU.add, inc="sV")
                    obank[bk] = ("sV", vdone)
                    step += 1
            P.barrier()

        if stage == "attn":
            fin = P.dma("sync", out_x.ap().rearrange("(dc p) t -> p dc t", p=128),
                        x2_sb.ap().rearrange("p (dc t) -> p dc t", t=TOK), "sS")
            P.wait("sync", "sS", fin)
            _emit_block(nc, P)
            return nc

        emit_ln(x2_sb)
        emit_ffn(x2_sb)

        fin = P.dma("sync", out_x.ap().rearrange("(dc p) t -> p dc t", p=128),
                    x2_sb.ap().rearrange("p (dc t) -> p dc t", t=TOK), "sS")
        P.wait("sync", "sS", fin)
        _emit_block(nc, P)
    return nc


def _emit_block(nc, P):
    with nc.Block() as blk:
        @blk.tensor
        def _(eng):
            P.run(eng, "tensor")

        @blk.vector
        def _(eng):
            P.run(eng, "vector")

        @blk.scalar
        def _(eng):
            P.run(eng, "scalar")

        @blk.sync
        def _(eng):
            P.run(eng, "sync")

        @blk.gpsimd
        def _(eng):
            P.run(eng, "gpsimd")


def _host_prepare(inputs):
    x = np.asarray(inputs["L0_attn_x"], np.float32)[0]
    freqs = np.asarray(inputs["L0_attn_freqs"], np.float32)
    wq = np.asarray(inputs["L0_attn_wq_w"], np.float32)
    wqb = np.asarray(inputs["L0_attn_wq_b"], np.float32)
    wkva = np.asarray(inputs["L0_attn_wkv_a_w"], np.float32)
    wkvb = np.asarray(inputs["L0_attn_wkv_b_w"], np.float32)
    nw = np.asarray(inputs["L0_attn_norm_w"], np.float32)
    ow = np.asarray(inputs["L0_attn_out_w"], np.float32)
    ob = np.asarray(inputs["L0_attn_out_b"], np.float32)
    W1 = np.asarray(inputs["L0_ffn_W1"], np.float32)
    W2 = np.asarray(inputs["L0_ffn_W2"], np.float32)
    W3 = np.asarray(inputs["L0_ffn_W3"], np.float32)

    xT = np.ascontiguousarray(x.T)
    w1t = np.ascontiguousarray(W1.T)
    w3t = np.ascontiguousarray(W3.T)
    w2t = np.ascontiguousarray(W2.T)
    wqt = np.ascontiguousarray(wq.T)
    wkv_at = np.ascontiguousarray(wkva.T)
    wkv_bt = np.ascontiguousarray(wkvb.T)
    owt = np.ascontiguousarray(ow.T)

    h128 = _hadamard(128).astype(BF16NP)
    h64 = _hadamard(64)
    h64bd = np.zeros((128, 128), np.float32)
    h64bd[:64, :64] = h64
    h64bd[64:, 64:] = h64
    h64bd = h64bd.astype(BF16NP)
    ident = np.eye(128, dtype=np.float32).astype(BF16NP)
    identf = np.eye(128, dtype=np.float32)
    ms = np.zeros((128, 128), np.float32)
    for i in range(32):
        e, od = 64 + 2 * i, 64 + 2 * i + 1
        ms[e, od] = 1.0
        ms[od, e] = -1.0
    mask = np.zeros((128, 4 * TT), np.float32)
    q_i = np.arange(128)[:, None]
    kt_i = np.arange(TT)[None, :]
    for v in range(4):
        mask[:, v * TT:(v + 1) * TT] = np.where(kt_i <= v * 128 + q_i, 0.0, MASKVAL)
    ones = np.ones((128, 1), np.float32)
    onesrow = np.ones((1, 128), np.float32)
    outb = np.ascontiguousarray(ob.reshape(DCN, 128).T)
    nwt = np.ascontiguousarray(nw.reshape(LCN, 128).T)
    neginf = np.full((128, 1), -3.0e38, np.float32)
    epsc = np.array([[EPS_LN, EPS_RMS]], np.float32)

    ims = []
    for r in range(NCORES):
        g0, g1 = 2 * r, 2 * r + 1
        cs = np.zeros((128, 4), np.float32)
        for li, g in enumerate((g0, g1)):
            cvec = np.ones(128, np.float32)
            svec = np.zeros(128, np.float32)
            cvec[64 + 2 * np.arange(32)] = freqs[g, :, 0]
            cvec[64 + 2 * np.arange(32) + 1] = freqs[g, :, 0]
            svec[64 + 2 * np.arange(32)] = freqs[g, :, 1]
            svec[64 + 2 * np.arange(32) + 1] = freqs[g, :, 1]
            cs[:, 2 * li] = cvec
            cs[:, 2 * li + 1] = svec
        wkvb_sl = np.ascontiguousarray(np.concatenate(
            [wkv_bt[:, g0 * 128:(g0 + 1) * 128], wkv_bt[:, g1 * 128:(g1 + 1) * 128],
             wkv_bt[:, 2048 + g0 * 128:2048 + (g0 + 1) * 128],
             wkv_bt[:, 2048 + g1 * 128:2048 + (g1 + 1) * 128]], axis=1))
        wq_sl = np.ascontiguousarray(wqt[:, g0 * 128:(g1 + 1) * 128])
        wqb_sl = np.ascontiguousarray(
            np.stack([wqb[g0 * 128:(g0 + 1) * 128], wqb[g1 * 128:(g1 + 1) * 128]],
                     axis=1))
        ims.append(dict(
            x_in=np.ascontiguousarray(xT[:, r * TOK:(r + 1) * TOK]),
            w1t=w1t, w3t=w3t, w2t=w2t, wqt_s=wq_sl, wkv_at=wkv_at,
            wkvb_s=wkvb_sl, owt=owt,
            c_id=ident, c_h128=h128, c_h64=h64bd, c_idf=identf, c_ms=ms,
            c_mask=mask, c_cs=cs, c_ones=ones, c_onesrow=onesrow,
            c_wqb=wqb_sl, c_outb=outb, c_nw=nwt, c_neginf=neginf, c_eps=epsc,
        ))
    return ims


def kernel(**inputs):
    key = _STAGE
    if key not in _CACHE:
        _CACHE[key] = build_program(key)
    nc = _CACHE[key]
    ims = _host_prepare(inputs)
    res = run_bass_kernel_spmd(nc, ims, core_ids=list(range(NCORES)))
    out = np.concatenate([np.asarray(res.results[r]["out_x"]) for r in range(NCORES)],
                         axis=1)
    return np.ascontiguousarray(out.T).reshape(1, S, D).astype(np.float32)


# version 04
# speedup vs baseline: 1.7692x; 1.0079x over previous
import math
import os
import numpy as np
import ml_dtypes
import concourse.bass as bass
import concourse.mybir as mybir
from concourse.bass_utils import run_bass_kernel_spmd
from contextlib import ExitStack

os.environ.setdefault("TRN_ALLOW_ICI_ALL_TO_ALL", "1")

F32 = mybir.dt.float32
BF16 = mybir.dt.bfloat16
AF = mybir.ActivationFunctionType
ALU = mybir.AluOpType
BF16NP = ml_dtypes.bfloat16

S, D, NCORES = 8192, 2048, 8
TOK = S // NCORES
TT = 512
DCN = 16
HID, HCN = 8192, 64
NH, DH, DL, LCN = 16, 128, 512, 4
SCALE_QK = 1.0 / math.sqrt(128.0)
RSQN = 1.0 / math.sqrt(8192.0)
EPS_LN, EPS_RMS = 1e-5, 1e-6
MASKVAL = -1.0e30

ENGS = ("tensor", "vector", "scalar", "sync", "gpsimd")

_STAGE = "full"
_CACHE = {}


def _hadamard(n):
    h = np.array([[1.0]], np.float32)
    while h.shape[0] < n:
        h = np.block([[h, h], [h, -h]])
    return h.astype(np.float32)


class Plan:
    SELF_SEM = {"vector": "sV", "scalar": "sA"}

    def __init__(self, nc):
        self.nc = nc
        self.ops = {e: [] for e in ENGS}
        self.sems = {}
        self.cnt = {}
        self.last_w = {e: {} for e in ENGS}
        self.last_r = {e: {} for e in ENGS}
        self.guar = {e: {} for e in ENGS}

    def mksem(self, st, name):
        h = st.enter_context(self.nc.semaphore(name))
        self.sems[name] = h
        self.cnt[name] = 0

    def wait(self, eng, sem, val):
        if val is None or val <= 0:
            return
        if self.guar[eng].get(sem, 0) >= val:
            return
        self.guar[eng][sem] = val
        self.ops[eng].append(("wait", self.sems[sem], val))

    @staticmethod
    def _apname(v):
        t = getattr(v, "tensor", None)
        if t is not None:
            return getattr(t, "name", None)
        if hasattr(v, "name") and hasattr(v, "dtype"):
            return v.name
        return None

    def _classify(self, args, kw):
        writes, reads = [], []
        for i, a in enumerate(args):
            nm = self._apname(a)
            if nm is not None:
                (writes if i == 0 else reads).append(nm)
        for k, v in kw.items():
            nm = self._apname(v)
            if nm is not None:
                (writes if k in ("out", "accum_out") else reads).append(nm)
        return writes, reads

    def op(self, eng, method, *args, inc=None, incv=1, **kw):
        if eng in self.SELF_SEM and method != "dma_start":
            ssem = self.SELF_SEM[eng]
            writes, reads = self._classify(args, kw)
            need = 0
            for nm in writes + reads:
                need = max(need, self.last_w[eng].get(nm, 0))
            for nm in writes:
                need = max(need, self.last_r[eng].get(nm, 0))
            self.wait(eng, ssem, need)
            if inc is None:
                inc = ssem
            assert inc == ssem, (eng, inc)
            self.cnt[inc] += incv
            tot = self.cnt[inc]
            for nm in writes:
                self.last_w[eng][nm] = tot
            for nm in reads:
                self.last_r[eng][nm] = tot
            self.ops[eng].append(("op", method, args, kw, self.sems[inc], incv))
            return tot
        sem = self.sems[inc] if inc is not None else None
        tot = None
        if inc is not None:
            self.cnt[inc] += incv
            tot = self.cnt[inc]
        self.ops[eng].append(("op", method, args, kw, sem, incv))
        return tot

    def dma(self, eng, out, in_, inc, **kw):
        return self.op(eng, "dma_start", out=out, in_=in_, inc=inc, incv=16, **kw)

    def barrier(self):
        snap = dict(self.cnt)
        for e in ENGS:
            for s, v in snap.items():
                self.wait(e, s, v)

    def run(self, eng_obj, eng_name):
        for rec in self.ops[eng_name]:
            if rec[0] == "wait":
                eng_obj.wait_ge(rec[1], rec[2])
            else:
                _, method, args, kw, sem, iv = rec
                inst = getattr(eng_obj, method)(*args, **kw)
                if sem is not None:
                    inst.then_inc(sem, iv)


class SbufMap:
    def __init__(self, nc, st, total_bytes):
        self.nc = nc
        self.total = total_bytes
        self.base = (nc.sbuf_base + 31) // 32 * 32
        assert self.base + total_bytes <= nc.sbuf_top, (self.base, nc.sbuf_top)

    def at(self, offset, name, shape, dtype):
        per_part = int(np.prod(shape[1:])) * mybir.dt.size(dtype)
        assert offset + per_part <= self.total, (name, offset, per_part)
        return self.nc.alloc_sbuf_tensor_at(name, shape, dtype,
                                            offset=self.base + offset)


def build_program(stage):
    nc = bass.Bass("TRN2", target_bir_lowering=False)

    def din(name, shape, dt=F32):
        return nc.dram_tensor(name, shape, dt, kind="ExternalInput")

    x_in = din("x_in", [D, TOK])
    w1t = din("w1t", [HCN, 128, DCN, 128])
    w3t = din("w3t", [HCN, 128, DCN, 128])
    w2t = din("w2t", [HID, D])
    wqt_s = din("wqt_s", [D, 256])
    wkv_at = din("wkv_at", [LCN, 128, DCN, 128])
    wkvb_s = din("wkvb_s", [DL, 512])
    owt = din("owt", [D, D])
    c_id_d = din("c_id", [128, 128], BF16)
    c_h128_d = din("c_h128", [128, 128], BF16)
    c_h64_d = din("c_h64", [128, 128], BF16)
    c_idf_d = din("c_idf", [128, 128])
    c_ms_d = din("c_ms", [128, 128])
    c_mask_d = din("c_mask", [128, 4 * TT])
    c_cs_d = din("c_cs", [128, 4])
    c_ones_d = din("c_ones", [128, 1])
    c_onesrow_d = din("c_onesrow", [1, 128])
    c_wqb_d = din("c_wqb", [128, 2])
    c_outb_d = din("c_outb", [128, DCN])
    c_nw_d = din("c_nw", [128, LCN])
    c_neginf_d = din("c_neginf", [128, 1])
    c_eps_d = din("c_eps", [1, 2])

    w2a_d = nc.dram_tensor("w2a", [HID, D], BF16)
    w2eff_d = nc.dram_tensor("w2eff", [DCN, 128, HCN, 128], BF16)
    xspill_d = nc.dram_tensor("xspill", [D, TOK], F32)
    ghl_c = nc.dram_tensor("ghl_c", [D + DL, TOK], BF16)
    ghl_g = nc.dram_tensor("ghl_g", [NCORES * (D + DL), TOK], BF16, addr_space="Shared")
    oc_c = nc.dram_tensor("oc_c", [D, TOK], BF16)
    oa_r = nc.dram_tensor("oa_r", [D, TOK], BF16)
    out_x = nc.dram_tensor("out_x", [D, TOK], F32, kind="ExternalOutput")
    if stage == "attn":
        d_ghl = nc.dram_tensor("d_ghl", [NCORES * (D + DL), TOK], BF16,
                               kind="ExternalOutput")
        d_q0 = nc.dram_tensor("d_q0", [128, S], BF16, kind="ExternalOutput")
        d_K = nc.dram_tensor("d_K", [128, S], BF16, kind="ExternalOutput")
        d_V = nc.dram_tensor("d_V", [128, S], BF16, kind="ExternalOutput")
        d_oc = nc.dram_tensor("d_oc", [D, TOK], BF16, kind="ExternalOutput")
        d_oa = nc.dram_tensor("d_oa", [D, TOK], BF16, kind="ExternalOutput")

    st = ExitStack()
    with st:
        KB = 1024
        sm = SbufMap(nc, st, 206 * KB)
        o = 0

        def alloc(name, shape, dtype):
            nonlocal o
            h = sm.at(o, name, shape, dtype)
            o += (int(np.prod(shape[1:])) * mybir.dt.size(dtype) + 31) // 32 * 32
            return h

        c_id = alloc("s_id", [128, 128], BF16)
        c_h128 = alloc("s_h128", [128, 128], BF16)
        c_h64 = alloc("s_h64", [128, 128], BF16)
        c_idf = alloc("s_idf", [128, 128], F32)
        c_ms = alloc("s_ms", [128, 128], F32)
        c_mask = alloc("s_mask", [128, 4 * TT], F32)
        c_cs = alloc("s_cs", [128, 4], F32)
        c_ones = alloc("s_ones", [128, 1], F32)
        c_onesrow = alloc("s_onesrow", [1, 128], F32)
        c_wqb = alloc("s_wqb", [128, 2], F32)
        c_outb = alloc("s_outb", [128, DCN], F32)
        c_nw = alloc("s_nw", [128, LCN], F32)
        c_neginf = alloc("s_neginf", [128, 1], F32)
        c_eps = alloc("s_eps", [1, 2], F32)
        r_a = alloc("s_ra", [1, TOK], F32)
        r_b = alloc("s_rb", [1, TOK], F32)
        rB512 = alloc("s_rB512", [128, TT], F32)
        sc512 = [alloc(f"s_sc{i}", [128, TT], F32) for i in range(2)]
        stg = [alloc(f"s_stg{i}", [128, TT], BF16) for i in range(2)]
        R_sb = [alloc(f"s_R{i}", [128, 128], F32) for i in range(2)]
        ost = [alloc(f"s_ost{i}", [128, 256], BF16) for i in range(2)]
        m_t = [[alloc(f"s_m{sj}{pp}", [128, 1], F32) for pp in range(2)] for sj in range(2)]
        l_t = [alloc(f"s_l{sj}", [128, 1], F32) for sj in range(2)]
        negm_t = [alloc(f"s_negm{sj}", [128, 1], F32) for sj in range(2)]
        rl_t = [alloc(f"s_rl{sj}", [128, 1], F32) for sj in range(2)]
        assert o <= 30 * KB, o

        RX, R1, R2 = 30 * KB, 94 * KB, 158 * KB
        x_sb = sm.at(RX, "v_x", [128, DCN * TOK], F32)
        h2b = [sm.at(RX + i * 16 * KB, f"v_h2b{i}", [128, DCN * TT], BF16) for i in range(2)]
        s_sc = [sm.at(RX + i * 32 * KB, f"v_s{i}", [128, 16 * TT], F32) for i in range(2)]
        x2_sb = sm.at(RX, "v_x2", [128, DCN * TOK], F32)

        w2c_sb = [sm.at(R1 + i * 4 * KB, f"v_w2c{i}", [128, D], BF16) for i in range(2)]
        w2a_sb = [sm.at(R1 + 8 * KB + i * 4 * KB, f"v_w2a{i}", [128, D], BF16) for i in range(2)]
        wc_sb = [sm.at(R1 + 16 * KB + i * 4 * KB, f"v_wc{i}", [128, D], BF16) for i in range(2)]
        weff_sb = [sm.at(R1 + 24 * KB + i * 4 * KB, f"v_weff{i}", [128, D], BF16) for i in range(2)]
        g_sb = sm.at(R1, "v_g", [128, 16 * TOK], BF16)
        w2eb = [sm.at(R1 + 32 * KB + i * 4 * KB, f"v_w2eb{i}", [128, 16 * 128], BF16) for i in range(2)]
        qT = [sm.at(R1 + i * 16 * KB, f"v_qT{i}", [128, S], BF16) for i in range(2)]
        K_sb = sm.at(R1 + 32 * KB, "v_K", [128, S], BF16)
        V_sb = sm.at(R1 + 48 * KB, "v_V", [128, S], BF16)
        owt_sb = sm.at(R1, "v_owt", [128, DCN * TOK], BF16)
        oT_sb = sm.at(R1 + 32 * KB, "v_oT", [128, DCN * TOK], BF16)

        hT_sb = sm.at(R2, "v_hT", [128, DCN * TOK], BF16)
        w1b = [sm.at(R2 + 32 * KB + i * 4 * KB, f"v_w1b{i}", [128, DCN * 128], BF16) for i in range(2)]
        w3b = [sm.at(R2 + 40 * KB + i * 4 * KB, f"v_w3b{i}", [128, DCN * 128], BF16) for i in range(2)]
        watb = [sm.at(R2 + 32 * KB + i * 4 * KB, f"v_watb{i}", [128, DCN * 128], BF16) for i in range(2)]
        wql_sb = sm.at(R2, "v_wql", [128, DCN * 256], BF16)
        pr = [sm.at(R2 + i * 16 * KB, f"v_pr{i}", [128, 16 * TT], BF16) for i in range(2)]
        ln_db = [sm.at(R2 + 32 * KB + i * 4 * KB, f"v_lndb{i}", [128, LCN * TT], BF16) for i in range(2)]
        wkvb_sb = sm.at(R2 + 40 * KB, "v_wkvb", [128, LCN * 512], BF16)
        aT_sb = [sm.at(R2 + 44 * KB + i * KB, f"v_aT{i}", [128, TT], BF16) for i in range(2)]

        psf = [st.enter_context(nc.psum_tensor(f"psf{i}", [128, 512], F32)) for i in range(6)]
        pst = [st.enter_context(nc.psum_tensor(f"pst{i}", [128, 1024], BF16)) for i in range(2)]

        P = Plan(nc)
        for s in ("sS", "sG", "sSC", "sT", "sV", "sA", "sC",
                  "sS0", "sS1", "sG0", "sG1", "sSC0", "sSC1", "sW0", "sW1", "sP"):
            P.mksem(st, s)

        for dst, src in (
            (c_id, c_id_d), (c_h128, c_h128_d), (c_h64, c_h64_d), (c_idf, c_idf_d),
            (c_ms, c_ms_d), (c_mask, c_mask_d), (c_cs, c_cs_d), (c_ones, c_ones_d),
            (c_onesrow, c_onesrow_d), (c_wqb, c_wqb_d), (c_outb, c_outb_d),
            (c_nw, c_nw_d), (c_neginf, c_neginf_d), (c_eps, c_eps_d),
        ):
            P.dma("sync", dst[:, :], src[:, :], "sS")
        P.dma("sync", x_sb.ap().rearrange("p (dc t) -> p dc t", t=TOK),
              x_in.ap().rearrange("(dc p) t -> p dc t", p=128), "sS")
        P.barrier()

        pbank = {i: ("sV", 0) for i in range(6)}

        def load_w2c(c, wait_sT):
            par = c & 1
            P.wait("gpsimd", "sT", wait_sT)
            return P.dma("gpsimd", w2c_sb[par][:, :],
                         w2t[c * 128:(c + 1) * 128, :], f"sG{par}")

        w2c_ld = {0: load_w2c(0, 0), 1: load_w2c(1, 0)}
        w2c_done = {}
        w2a_free = [0, 0]
        rot = 0
        for c in range(HCN):
            par = c & 1
            P.wait("tensor", f"sG{par}", w2c_ld[c])
            banks = [(rot + s4) % 6 for s4 in range(4)]
            t_done = None
            for s4, bk in enumerate(banks):
                sname, v = pbank[bk]
                P.wait("tensor", sname, v)
                t_done = P.op("tensor", "matmul", psf[bk][:, :], c_h128[:, :],
                              w2c_sb[par][:, s4 * 512:(s4 + 1) * 512],
                              start=True, stop=True,
                              inc=("sT" if s4 == 3 else None))
            w2c_done[c] = t_done
            if c + 2 < HCN:
                w2c_ld[c + 2] = load_w2c(c + 2, t_done)
            P.wait("vector", "sT", t_done)
            P.wait("vector", f"sW{par}", w2a_free[par])
            for s4, bk in enumerate(banks):
                v_done = P.op("vector", "tensor_copy",
                              w2a_sb[par][:, s4 * 512:(s4 + 1) * 512], psf[bk][:, :],
                              inc="sV")
                pbank[bk] = ("sV", v_done)
            rot = (rot + 4) % 6
            P.wait("sync", "sV", v_done)
            w2a_free[par] = P.dma("sync", w2a_d[c * 128:(c + 1) * 128, :],
                                  w2a_sb[par][:, :], f"sW{par}")
        P.barrier()

        w2a_r = w2a_d.ap().rearrange("(c pz) d -> pz c d", pz=128)
        w2e_r = w2eff_d.ap().rearrange("dc hrow hc m -> hrow hc dc m")
        pbank = {i: ("sV", 0) for i in range(6)}

        def load_wc(pp, wait_sT):
            par = pp & 1
            P.wait("sync", "sT", wait_sT)
            P.dma("sync", wc_sb[par][0:64, :], w2a_r[2 * pp], f"sS{par}")
            return P.dma("sync", wc_sb[par][64:128, :], w2a_r[2 * pp + 1],
                         f"sS{par}")

        wc_ld = {0: load_wc(0, 0), 1: load_wc(1, 0)}
        weff_free = [0, 0]
        rot = 0
        for pp in range(64):
            par = pp & 1
            P.wait("tensor", f"sS{par}", wc_ld[pp])
            banks = [(rot + s4) % 6 for s4 in range(4)]
            t_done = None
            for s4, bk in enumerate(banks):
                sname, v = pbank[bk]
                P.wait("tensor", sname, v)
                t_done = P.op("tensor", "matmul", psf[bk][:, :], c_h64[:, :],
                              wc_sb[par][:, s4 * 512:(s4 + 1) * 512],
                              start=True, stop=True,
                              inc=("sT" if s4 == 3 else None))
            if pp + 2 < 64:
                wc_ld[pp + 2] = load_wc(pp + 2, t_done)
            P.wait("vector", "sT", t_done)
            P.wait("vector", f"sW{par}", weff_free[par])
            for s4, bk in enumerate(banks):
                v_done = P.op("vector", "tensor_scalar",
                              out=weff_sb[par][:, s4 * 512:(s4 + 1) * 512],
                              in0=psf[bk][:, :], scalar1=RSQN, scalar2=None,
                              op0=ALU.mult, inc="sV")
                pbank[bk] = ("sV", v_done)
            rot = (rot + 4) % 6
            P.wait("sync", "sV", v_done)
            P.dma("sync", w2e_r[2 * pp],
                  weff_sb[par][0:64, :].rearrange("c (dc m) -> c dc m", m=128),
                  f"sW{par}")
            weff_free[par] = P.dma(
                "sync", w2e_r[2 * pp + 1],
                weff_sb[par][64:128, :].rearrange("c (dc m) -> c dc m", m=128),
                f"sW{par}")
        P.barrier()

        for hl in range(2):
            P.op("vector", "tensor_scalar", out=rB512[:, 0:128], in0=c_idf[:, :],
                 scalar1=c_cs[:, 2 * hl:2 * hl + 1], scalar2=None, op0=ALU.mult)
            P.op("vector", "scalar_tensor_tensor", out=R_sb[hl][:, :], in0=c_ms[:, :],
                 scalar=c_cs[:, 2 * hl + 1:2 * hl + 2], in1=rB512[:, 0:128],
                 op0=ALU.mult, op1=ALU.add, inc="sV")
        P.barrier()

        def emit_ln(src_sb):
            sq_par = 0
            consumed = [0, 0]
            for tt in range(2):
                for dc in range(DCN):
                    xs = src_sb[:, dc * TOK + tt * TT: dc * TOK + (tt + 1) * TT]
                    P.wait("scalar", "sT", consumed[sq_par])
                    a_done = P.op("scalar", "activation", out=sc512[sq_par][:, :],
                                  in_=xs, func=AF.Square, inc="sA")
                    P.wait("tensor", "sA", a_done)
                    P.op("tensor", "matmul", psf[2 + tt][0:1, :], c_ones[:, :],
                         sc512[sq_par][:, :],
                         start=(dc == 0), stop=(dc == DCN - 1))
                    t_done = P.op("tensor", "matmul", psf[tt][0:1, :], c_ones[:, :], xs,
                                  start=(dc == 0), stop=(dc == DCN - 1), inc="sT")
                    consumed[sq_par] = t_done
                    sq_par ^= 1
            P.wait("vector", "sT", P.cnt["sT"])
            for tt in range(2):
                rsl = slice(tt * TT, (tt + 1) * TT)
                P.op("vector", "tensor_scalar", out=r_a[0:1, rsl], in0=psf[tt][0:1, :],
                     scalar1=1.0 / D, scalar2=None, op0=ALU.mult)
                P.op("vector", "tensor_scalar", out=r_b[0:1, rsl], in0=psf[2 + tt][0:1, :],
                     scalar1=1.0 / D, scalar2=None, op0=ALU.mult)
                P.op("vector", "tensor_mul", psf[4][0:1, :], r_a[0:1, rsl], r_a[0:1, rsl])
                v_done = P.op("vector", "tensor_sub", r_b[0:1, rsl], r_b[0:1, rsl],
                              psf[4][0:1, :], inc="sV")
                P.wait("scalar", "sV", v_done)
                a_done = P.op("scalar", "activation", out=psf[5][0:1, :], in_=r_b[0:1, rsl],
                              func=AF.Sqrt, bias=c_eps[0:1, 0:1], scale=1.0, inc="sA")
                P.wait("vector", "sA", a_done)
                P.op("vector", "reciprocal", r_b[0:1, rsl], psf[5][0:1, :], inc="sV")
            P.wait("tensor", "sV", P.cnt["sV"])
            bc_done = None
            for tt in range(2):
                rsl = slice(tt * TT, (tt + 1) * TT)
                P.op("tensor", "matmul", psf[2 + tt][:, :], c_onesrow[:, :], r_a[0:1, rsl],
                     start=True, stop=True)
                bc_done = P.op("tensor", "matmul", psf[4 + tt][:, :], c_onesrow[:, :],
                               r_b[0:1, rsl], start=True, stop=True,
                               inc=("sT" if tt == 1 else None))
            P.wait("vector", "sT", bc_done)
            par = 0
            for dc in range(DCN):
                for tt in range(2):
                    xs = src_sb[:, dc * TOK + tt * TT: dc * TOK + (tt + 1) * TT]
                    hs = hT_sb[:, dc * TOK + tt * TT: dc * TOK + (tt + 1) * TT]
                    P.op("vector", "tensor_sub", sc512[par][:, :], xs, psf[2 + tt][:, :])
                    P.op("vector", "tensor_mul", hs, sc512[par][:, :], psf[4 + tt][:, :],
                         inc="sV")
                    par ^= 1
            P.barrier()

        def emit_ffn(res_sb):
            for qtr in range(4):
                gbank = {i: ("sV", 0) for i in range(6)}

                def load_w13(i, wait_sT):
                    hc = qtr * 16 + i
                    par = i & 1
                    P.wait("gpsimd", "sT", wait_sT)
                    P.dma("gpsimd",
                          w1b[par].ap().rearrange("p (dc m) -> p dc m", m=128),
                          w1t[hc], f"sG{par}")
                    return P.dma(
                        "gpsimd",
                        w3b[par].ap().rearrange("p (dc m) -> p dc m", m=128),
                        w3t[hc], f"sG{par}")

                w_ld = {0: load_w13(0, 0), 1: load_w13(1, 0)}
                for i in range(16):
                    par = i & 1
                    b = [(4 * i) % 6, (4 * i + 1) % 6, (4 * i + 2) % 6, (4 * i + 3) % 6]
                    P.wait("tensor", f"sG{par}", w_ld[i])
                    for bb in b:
                        sname, v = gbank[bb]
                        P.wait("tensor", sname, v)
                    for k in range(DCN):
                        lhs = w1b[par][:, k * 128:(k + 1) * 128]
                        P.op("tensor", "matmul", psf[b[0]][:, :], lhs,
                             hT_sb[:, k * TOK:k * TOK + TT],
                             start=(k == 0), stop=(k == DCN - 1))
                        P.op("tensor", "matmul", psf[b[1]][:, :], lhs,
                             hT_sb[:, k * TOK + TT:k * TOK + 2 * TT],
                             start=(k == 0), stop=(k == DCN - 1))
                    t_done = None
                    for k in range(DCN):
                        lhs = w3b[par][:, k * 128:(k + 1) * 128]
                        P.op("tensor", "matmul", psf[b[2]][:, :], lhs,
                             hT_sb[:, k * TOK:k * TOK + TT],
                             start=(k == 0), stop=(k == DCN - 1))
                        t_done = P.op("tensor", "matmul", psf[b[3]][:, :], lhs,
                                      hT_sb[:, k * TOK + TT:k * TOK + 2 * TT],
                                      start=(k == 0), stop=(k == DCN - 1),
                                      inc=("sT" if k == DCN - 1 else None))
                    if i + 2 < 16:
                        w_ld[i + 2] = load_w13(i + 2, t_done)
                    P.wait("scalar", "sT", t_done)
                    P.wait("vector", "sT", t_done)
                    for tt in range(2):
                        spar = (2 * i + tt) & 1
                        a_done = P.op("scalar", "activation", out=sc512[spar][:, :],
                                      in_=psf[b[tt]][:, :], func=AF.Silu, inc="sA")
                        P.wait("vector", "sA", a_done)
                        v_done = P.op("vector", "tensor_mul",
                                      g_sb[:, i * TOK + tt * TT:i * TOK + (tt + 1) * TT],
                                      sc512[spar][:, :], psf[b[2 + tt]][:, :], inc="sV")
                        gbank[b[tt]] = ("sA", a_done)
                        gbank[b[2 + tt]] = ("sV", v_done)
                        P.wait("scalar", "sV", v_done)
                g_ready = P.cnt["sV"]

                def load_w2e(dc, wait_sT):
                    par = dc & 1
                    P.wait("sync", "sT", wait_sT)
                    return P.dma(
                        "sync", w2eb[par].ap().rearrange("p (hc m) -> p hc m", m=128),
                        w2eff_d[dc, :, qtr * 16:(qtr + 1) * 16, :], f"sS{par}")

                e_ld = {0: load_w2e(0, 0), 1: load_w2e(1, 0)}
                mb_free = [0, 0, 0, 0]
                P.wait("tensor", "sV", g_ready)
                for dc in range(DCN):
                    par = dc & 1
                    P.wait("tensor", f"sS{par}", e_ld[dc])
                    bk0, bk1 = (2 * dc) % 4, (2 * dc) % 4 + 1
                    P.wait("tensor", "sV", mb_free[bk0])
                    P.wait("tensor", "sV", mb_free[bk1])
                    t_done = None
                    for hcl in range(16):
                        lhs = w2eb[par][:, hcl * 128:(hcl + 1) * 128]
                        P.op("tensor", "matmul", psf[bk0][:, :], lhs,
                             g_sb[:, hcl * TOK:hcl * TOK + TT],
                             start=(hcl == 0), stop=(hcl == 15))
                        t_done = P.op("tensor", "matmul", psf[bk1][:, :], lhs,
                                      g_sb[:, hcl * TOK + TT:hcl * TOK + 2 * TT],
                                      start=(hcl == 0), stop=(hcl == 15),
                                      inc=("sT" if hcl == 15 else None))
                    P.wait("vector", "sT", t_done)
                    for tt, bk in ((0, bk0), (1, bk1)):
                        mb_free[bk] = P.op(
                            "vector", "scalar_tensor_tensor",
                            out=res_sb[:, dc * TOK + tt * TT:dc * TOK + (tt + 1) * TT],
                            in0=psf[bk][:, :], scalar=0.5,
                            in1=res_sb[:, dc * TOK + tt * TT:dc * TOK + (tt + 1) * TT],
                            op0=ALU.mult, op1=ALU.add, inc="sV")
                    if dc + 2 < DCN:
                        e_ld[dc + 2] = load_w2e(dc + 2, t_done)
                P.barrier()

        emit_ln(x_sb)
        emit_ffn(x_sb)
        if stage == "ffn1":
            fin = P.dma("sync", out_x.ap().rearrange("(dc p) t -> p dc t", p=128),
                        x_sb.ap().rearrange("p (dc t) -> p dc t", t=TOK), "sS")
            P.wait("sync", "sS", fin)
            _emit_block(nc, P)
            return nc

        emit_ln(x_sb)
        P.dma("scalar", ghl_c[0:D, :].rearrange("(dc p) t -> p dc t", p=128),
              hT_sb.ap().rearrange("p (dc t) -> p dc t", t=TOK), "sSC")
        st_free = [0, 0]
        for tt in range(2):
            def load_wat(lc, wait_sT):
                par = lc & 1
                P.wait("gpsimd", "sT", wait_sT)
                return P.dma(
                    "gpsimd", watb[par].ap().rearrange("p (dc m) -> p dc m", m=128),
                    wkv_at[lc], f"sG{par}")

            at_ld = {0: load_wat(0, 0), 1: load_wat(1, 0)}
            for lc in range(LCN):
                par = lc & 1
                P.wait("tensor", f"sG{par}", at_ld[lc])
                t_done = None
                for k in range(DCN):
                    t_done = P.op("tensor", "matmul", psf[lc][:, :],
                                  watb[par][:, k * 128:(k + 1) * 128],
                                  hT_sb[:, k * TOK + tt * TT:k * TOK + (tt + 1) * TT],
                                  start=(k == 0), stop=(k == DCN - 1),
                                  inc=("sT" if k == DCN - 1 else None))
                if lc + 2 < LCN:
                    at_ld[lc + 2] = load_wat(lc + 2, t_done)
            lat_done = P.cnt["sT"]
            P.wait("scalar", "sT", lat_done)
            sq_done = {}
            for lc in range(LCN):
                spar = lc & 1
                P.wait("scalar", "sT", sq_done.get(lc - 2, 0))
                a_done = P.op("scalar", "activation", out=sc512[spar][:, :],
                              in_=psf[lc][:, :], func=AF.Square, inc="sA")
                P.wait("tensor", "sA", a_done)
                sq_done[lc] = P.op("tensor", "matmul", psf[4][0:1, :], c_ones[:, :],
                                   sc512[spar][:, :], start=(lc == 0),
                                   stop=(lc == LCN - 1), inc="sT")
            P.wait("vector", "sT", P.cnt["sT"])
            v_done = P.op("vector", "tensor_scalar", out=r_b[0:1, 0:TT],
                          in0=psf[4][0:1, :], scalar1=1.0 / DL, scalar2=None,
                          op0=ALU.mult, inc="sV")
            P.wait("scalar", "sV", v_done)
            a_done = P.op("scalar", "activation", out=psf[5][0:1, :], in_=r_b[0:1, 0:TT],
                          func=AF.Sqrt, bias=c_eps[0:1, 1:2], scale=1.0, inc="sA")
            P.wait("vector", "sA", a_done)
            v_done = P.op("vector", "reciprocal", r_b[0:1, 0:TT], psf[5][0:1, :], inc="sV")
            P.wait("tensor", "sV", v_done)
            t_done = P.op("tensor", "matmul", psf[5][:, :], c_onesrow[:, :],
                          r_b[0:1, 0:TT], start=True, stop=True, inc="sT")
            P.wait("vector", "sT", t_done)
            P.op("vector", "tensor_copy", rB512[:, :], psf[5][:, :])
            for lc in range(LCN):
                spar = lc & 1
                P.wait("vector", f"sSC{spar}", st_free[spar])
                v_done = P.op("vector", "scalar_tensor_tensor", out=stg[spar][:, :],
                              in0=psf[lc][:, :], scalar=c_nw[:, lc:lc + 1],
                              in1=rB512[:, :], op0=ALU.mult, op1=ALU.mult, inc="sV")
                P.wait("scalar", "sV", v_done)
                st_free[spar] = P.dma(
                    "scalar",
                    ghl_c[D + lc * 128:D + (lc + 1) * 128, tt * TT:(tt + 1) * TT],
                    stg[spar][:, :], f"sSC{spar}")
            P.barrier()
        P.dma("scalar", xspill_d.ap().rearrange("(dc p) t -> p dc t", p=128),
              x_sb.ap().rearrange("p (dc t) -> p dc t", t=TOK), "sSC")
        P.barrier()
        P.op("gpsimd", "collective_compute", "AllGather", ALU.bypass,
             replica_groups=[list(range(NCORES))],
             ins=[ghl_c.ap().opt()], outs=[ghl_g.ap().opt()], inc="sC")
        P.barrier()

        ghl_r = ghl_g.ap().rearrange("(rk f20 p) t -> rk p f20 t", rk=8, f20=20, p=128)

        P.dma("gpsimd", wql_sb.ap().rearrange("p (dc m) -> p dc m", m=256),
              wqt_s.ap().rearrange("(dc p) m -> p dc m", p=128), "sG")
        P.dma("gpsimd", wkvb_sb.ap().rearrange("p (lc f) -> p lc f", f=512),
              wkvb_s.ap().rearrange("(lc p) f -> p lc f", p=128), "sG")
        P.barrier()

        def load_h2(t16, wait_sT):
            par = t16 & 1
            rk, slot = t16 // 2, t16 % 2
            P.wait("sync", "sT", wait_sT)
            return P.dma(
                "sync", h2b[par].ap().rearrange("p (dc t) -> p dc t", t=TT),
                ghl_r[rk, :, 0:16, slot * TT:(slot + 1) * TT], f"sS{par}")

        h2_ld = {0: load_h2(0, 0), 1: load_h2(1, 0)}
        sc_free = [0, 0]
        bfree = {}
        for t16 in range(16):
            par = t16 & 1
            P.wait("tensor", f"sS{par}", h2_ld[t16])
            t_last = None
            for hl in range(2):
                bq = (2 * t16 + hl) % 3
                bq2 = 3 + (2 * t16 + hl) % 3
                sname, v = bfree.get(bq, ("sV", 0))
                P.wait("tensor", sname, v)
                t_done = None
                for k in range(DCN):
                    t_done = P.op("tensor", "matmul", psf[bq][:, :],
                                  wql_sb[:, k * 256 + hl * 128:k * 256 + (hl + 1) * 128],
                                  h2b[par][:, k * TT:(k + 1) * TT],
                                  start=(k == 0), stop=(k == DCN - 1),
                                  inc=("sT" if k == DCN - 1 else None))
                spar = (2 * t16 + hl) & 1
                P.wait("vector", "sT", t_done)
                P.wait("vector", "sT", sc_free[spar])
                v_done = P.op("vector", "tensor_scalar", out=sc512[spar][:, :],
                              in0=psf[bq][:, :], scalar1=c_wqb[:, hl:hl + 1],
                              scalar2=None, op0=ALU.add, inc="sV")
                bfree[bq] = ("sV", v_done)
                sname, v = bfree.get(bq2, ("sV", 0))
                P.wait("tensor", sname, v)
                P.wait("tensor", "sV", v_done)
                t2 = P.op("tensor", "matmul", psf[bq2][:, :], R_sb[hl][:, :],
                          sc512[spar][:, :], start=True, stop=True, inc="sT")
                sc_free[spar] = t2
                t_last = t2
                P.wait("vector", "sT", t2)
                v2 = P.op("vector", "tensor_copy",
                          qT[hl][:, t16 * TT:(t16 + 1) * TT], psf[bq2][:, :], inc="sV")
                bfree[bq2] = ("sV", v2)
            if t16 + 2 < 16:
                h2_ld[t16 + 2] = load_h2(t16 + 2, t_last)
        P.barrier()

        if stage == "attn":
            P.dma("sync", d_ghl.ap(), ghl_g.ap(), "sS")
            P.dma("sync", d_q0[:, :], qT[0][:, :], "sS")
            P.barrier()

        for hl in range(2):
            def load_ln(t16, wait_sT):
                par = t16 & 1
                rk, slot = t16 // 2, t16 % 2
                P.wait("sync", "sT", wait_sT)
                return P.dma(
                    "sync", ln_db[par].ap().rearrange("p (lc t) -> p lc t", t=TT),
                    ghl_r[rk, :, 16:20, slot * TT:(slot + 1) * TT], f"sS{par}")

            ln_ld = {0: load_ln(0, 0), 1: load_ln(1, 0)}
            scp = 0
            vtp = 0
            bfree = {}
            tfree = {0: ("sV", 0), 1: ("sV", 0)}
            for t16 in range(16):
                par = t16 & 1
                P.wait("tensor", f"sS{par}", ln_ld[t16])
                bk, bv, br = (3 * t16) % 6, (3 * t16 + 1) % 6, (3 * t16 + 2) % 6
                for bb in (bk, bv, br):
                    sname, v = bfree.get(bb, ("sV", 0))
                    P.wait("tensor", sname, v)
                for lc in range(LCN):
                    P.op("tensor", "matmul", psf[bk][:, :],
                         wkvb_sb[:, lc * 512 + hl * 128:lc * 512 + (hl + 1) * 128],
                         ln_db[par][:, lc * TT:(lc + 1) * TT],
                         start=(lc == 0), stop=(lc == LCN - 1))
                t_done = None
                for lc in range(LCN):
                    t_done = P.op(
                        "tensor", "matmul", psf[bv][:, :],
                        wkvb_sb[:, lc * 512 + 256 + hl * 128:lc * 512 + 256 + (hl + 1) * 128],
                        ln_db[par][:, lc * TT:(lc + 1) * TT],
                        start=(lc == 0), stop=(lc == LCN - 1),
                        inc=("sT" if lc == LCN - 1 else None))
                if t16 + 2 < 16:
                    ln_ld[t16 + 2] = load_ln(t16 + 2, t_done)
                P.wait("vector", "sT", t_done)
                P.wait("vector", "sT", sc_free[scp])
                vk = P.op("vector", "tensor_copy", sc512[scp][:, :], psf[bk][:, :],
                          inc="sV")
                bfree[bk] = ("sV", vk)
                P.wait("tensor", "sV", vk)
                tk = P.op("tensor", "matmul", psf[br][:, :], R_sb[hl][:, :],
                          sc512[scp][:, :], start=True, stop=True, inc="sT")
                sc_free[scp] = tk
                scp ^= 1
                P.wait("vector", "sT", tk)
                vk2 = P.op("vector", "tensor_copy",
                           K_sb[:, t16 * TT:(t16 + 1) * TT], psf[br][:, :], inc="sV")
                bfree[br] = ("sV", vk2)
                vv = P.op("vector", "tensor_copy", stg[vtp][:, :], psf[bv][:, :],
                          inc="sV")
                bfree[bv] = ("sV", vv)
                tb = t16 & 1
                sname, v = tfree[tb]
                P.wait("tensor", sname, v)
                P.wait("tensor", "sV", vv)
                tv = None
                for j in range(4):
                    tv = P.op("tensor", "transpose", pst[tb][:, j * 128:(j + 1) * 128],
                              stg[vtp][:, j * 128:(j + 1) * 128], c_id[:, :],
                              inc=("sT" if j == 3 else None))
                vtp ^= 1
                P.wait("vector", "sT", tv)
                vv2 = P.op("vector", "tensor_copy",
                           V_sb[:, t16 * TT:(t16 + 1) * TT], pst[tb][:, 0:512], inc="sV")
                tfree[tb] = ("sV", vv2)
            P.barrier()
            if stage == "attn" and hl == 0:
                P.dma("sync", d_K[:, :], K_sb[:, :], "sS")
                P.dma("sync", d_V[:, :], V_sb[:, :], "sS")
                P.barrier()

            st_c = {
                "s_free": [0, 0], "pr_free": [0, 0], "l_free": [0, 0],
                "aT_free": [0, 0], "ost_free": [0, 0],
                "sbank": {i: ("sV", 0) for i in range(4)},
                "tpar": {0: ("sV", 0), 1: ("sV", 0)},
                "oQ": ("sV", 0),
                "scale_done": [0, 0],
                "rl_free": [0, 0],
            }

            def emit_A_step(p, kc):
                cov = p // 2 + 1
                for sj in range(2):
                    js = 2 * p + sj
                    spar = (2 * kc + sj) & 3
                    sname, v = st_c["sbank"][spar]
                    P.wait("tensor", sname, v)
                    tsc = P.op("tensor", "matmul", psf[spar][:, :],
                               qT[hl][:, js * 128:(js + 1) * 128],
                               K_sb[:, kc * TT:(kc + 1) * TT],
                               start=True, stop=True, inc="sT")
                    P.wait("vector", "sT", tsc)
                    if kc == 0:
                        P.wait("vector", "sA", st_c["s_free"][sj])
                    mprev = c_neginf if kc == 0 else m_t[sj][(kc - 1) & 1]
                    ssl = s_sc[sj][:, kc * TT:(kc + 1) * TT]
                    if kc == cov - 1:
                        mv = (js % 4) * TT
                        P.op("vector", "scalar_tensor_tensor", out=ssl,
                             in0=psf[spar][:, :], scalar=SCALE_QK,
                             in1=c_mask[:, mv:mv + TT], op0=ALU.mult, op1=ALU.add)
                        vdone = P.op("vector", "tensor_scalar", out=ssl, in0=ssl,
                                     scalar1=1.0, scalar2=mprev[:, :],
                                     op0=ALU.mult, op1=ALU.max,
                                     accum_out=m_t[sj][kc & 1][:, :], inc="sV")
                    else:
                        vdone = P.op("vector", "tensor_scalar", out=ssl,
                                     in0=psf[spar][:, :], scalar1=SCALE_QK,
                                     scalar2=mprev[:, :], op0=ALU.mult, op1=ALU.max,
                                     accum_out=m_t[sj][kc & 1][:, :], inc="sV")
                    st_c["sbank"][spar] = ("sV", vdone)

            def emit_B(p):
                cov = p // 2 + 1
                for sj in range(2):
                    mfin = m_t[sj][(cov - 1) & 1]
                    nm = P.op("vector", "tensor_scalar", out=negm_t[sj][:, :],
                              in0=mfin[:, :], scalar1=-1.0, scalar2=None,
                              op0=ALU.mult, inc="sV")
                    P.wait("scalar", "sV", nm)
                    P.wait("scalar", "sT", st_c["pr_free"][sj])
                    adone = P.op("scalar", "activation",
                                 out=pr[sj][:, 0:cov * TT],
                                 in_=s_sc[sj][:, 0:cov * TT],
                                 func=AF.Exp, bias=negm_t[sj][:, :], scale=1.0,
                                 accum_out=l_t[sj][:, :], inc="sA")
                    st_c["s_free"][sj] = adone
                    P.wait("vector", "sA", adone)
                    P.op("vector", "reciprocal", rl_t[sj][:, :], l_t[sj][:, :],
                         inc="sV")
                    st_c["scale_done"][sj] = P.op(
                        "vector", "tensor_scalar", out=pr[sj][:, 0:cov * TT],
                        in0=pr[sj][:, 0:cov * TT], scalar1=rl_t[sj][:, :],
                        scalar2=None, op0=ALU.mult, inc="sV")

            def make_C_steps(p):
                cov = p // 2 + 1
                opar = p & 1
                steps = []

                def pre():
                    P.wait("tensor", "sV", max(st_c["scale_done"]))
                    sname, v = st_c["oQ"]
                    P.wait("tensor", sname, v)
                steps.append(pre)

                def upstep(up, last):
                    u0, u1 = 2 * up, 2 * up + 1
                    tb = up & 1
                    sname, v = st_c["tpar"][tb]
                    P.wait("tensor", sname, v)
                    ttr = None
                    for pos, (sj, u) in enumerate(((0, u0), (1, u0), (0, u1), (1, u1))):
                        ttr = P.op("tensor", "transpose",
                                   pst[tb][:, pos * 128:(pos + 1) * 128],
                                   pr[sj][:, u * 128:(u + 1) * 128], c_id[:, :],
                                   inc=("sT" if pos == 3 else None))
                    P.wait("vector", "sT", ttr)
                    P.wait("vector", "sT", st_c["aT_free"][tb])
                    vev = P.op("vector", "tensor_copy", aT_sb[tb][:, :],
                               pst[tb][:, 0:512], inc="sV")
                    st_c["tpar"][tb] = ("sV", vev)
                    P.wait("tensor", "sV", vev)
                    P.op("tensor", "matmul", psf[4][:, 0:256],
                         V_sb[:, u0 * 128:(u0 + 1) * 128], aT_sb[tb][:, 0:256],
                         start=(up == 0), stop=False)
                    last_av = P.op("tensor", "matmul", psf[4][:, 0:256],
                                   V_sb[:, u1 * 128:(u1 + 1) * 128],
                                   aT_sb[tb][:, 256:512],
                                   start=False, stop=last, inc="sT")
                    st_c["aT_free"][tb] = last_av
                    if last:
                        st_c["pr_free"][0] = last_av
                        st_c["pr_free"][1] = last_av
                        P.wait("vector", "sT", last_av)
                        P.wait("vector", f"sSC{opar}", st_c["ost_free"][opar])
                        vo = P.op("vector", "tensor_copy", ost[opar][:, :],
                                  psf[4][:, 0:256], inc="sV")
                        st_c["oQ"] = ("sV", vo)
                        P.wait("scalar", "sV", vo)
                        st_c["ost_free"][opar] = P.dma(
                            "scalar",
                            oc_c[(p // 4) * 256 + hl * 128:
                                 (p // 4) * 256 + (hl + 1) * 128,
                                 (p % 4) * 256:(p % 4 + 1) * 256],
                            ost[opar][:, :], f"sSC{opar}")

                for up in range(cov * 2):
                    steps.append((lambda u=up, lst=(up == cov * 2 - 1):
                                  upstep(u, lst)))
                return steps

            cq = []
            for p in range(32):
                cov = p // 2 + 1
                for kc in range(cov):
                    emit_A_step(p, kc)
                    for _ in range(3):
                        if cq:
                            cq.pop(0)()
                while cq:
                    cq.pop(0)()
                emit_B(p)
                cq = make_C_steps(p)
            while cq:
                cq.pop(0)()
            P.barrier()

        if stage == "attn":
            P.dma("sync", d_oc.ap(), oc_c.ap(), "sS")
            P.barrier()

        P.op("gpsimd", "collective_compute", "AllToAll", ALU.bypass,
             replica_groups=[list(range(NCORES))],
             ins=[oc_c.ap().opt()], outs=[oa_r.ap().opt()], inc="sC")
        P.barrier()

        if stage == "attn":
            P.dma("sync", d_oa.ap(), oa_r.ap(), "sS")
            P.barrier()

        P.dma("sync", x2_sb.ap().rearrange("p (dc t) -> p dc t", t=TOK),
              xspill_d.ap().rearrange("(dc p) t -> p dc t", p=128), "sS")
        for fc in range(DCN):
            P.dma("sync", oT_sb[:, fc * TOK:(fc + 1) * TOK],
                  oa_r[fc * 128:(fc + 1) * 128, :], "sS")
        P.barrier()
        for dhalf in range(2):
            wl = P.dma("gpsimd", owt_sb.ap().rearrange("p (fc d) -> p fc d", d=TOK),
                       owt[:, dhalf * TOK:(dhalf + 1) * TOK].rearrange("(fc p) d -> p fc d", p=128),
                       "sG")
            P.wait("tensor", "sG", wl)
            obank = {0: ("sV", 0), 1: ("sV", 0)}
            step = 0
            for dc8 in range(8):
                dc = dhalf * 8 + dc8
                for tt in range(2):
                    bk = step & 1
                    sname, v = obank[bk]
                    P.wait("tensor", sname, v)
                    t_done = None
                    for fc in range(DCN):
                        t_done = P.op(
                            "tensor", "matmul", psf[bk][:, :],
                            owt_sb[:, fc * TOK + dc8 * 128:fc * TOK + (dc8 + 1) * 128],
                            oT_sb[:, fc * TOK + tt * TT:fc * TOK + (tt + 1) * TT],
                            start=(fc == 0), stop=(fc == DCN - 1),
                            inc=("sT" if fc == DCN - 1 else None))
                    P.wait("vector", "sT", t_done)
                    vdone = P.op("vector", "scalar_tensor_tensor",
                                 out=x2_sb[:, dc * TOK + tt * TT:dc * TOK + (tt + 1) * TT],
                                 in0=psf[bk][:, :], scalar=c_outb[:, dc:dc + 1],
                                 in1=x2_sb[:, dc * TOK + tt * TT:dc * TOK + (tt + 1) * TT],
                                 op0=ALU.add, op1=ALU.add, inc="sV")
                    obank[bk] = ("sV", vdone)
                    step += 1
            P.barrier()

        if stage == "attn":
            fin = P.dma("sync", out_x.ap().rearrange("(dc p) t -> p dc t", p=128),
                        x2_sb.ap().rearrange("p (dc t) -> p dc t", t=TOK), "sS")
            P.wait("sync", "sS", fin)
            _emit_block(nc, P)
            return nc

        emit_ln(x2_sb)
        emit_ffn(x2_sb)

        fin = P.dma("sync", out_x.ap().rearrange("(dc p) t -> p dc t", p=128),
                    x2_sb.ap().rearrange("p (dc t) -> p dc t", t=TOK), "sS")
        P.wait("sync", "sS", fin)
        _emit_block(nc, P)
    return nc


def _emit_block(nc, P):
    with nc.Block() as blk:
        @blk.tensor
        def _(eng):
            P.run(eng, "tensor")

        @blk.vector
        def _(eng):
            P.run(eng, "vector")

        @blk.scalar
        def _(eng):
            P.run(eng, "scalar")

        @blk.sync
        def _(eng):
            P.run(eng, "sync")

        @blk.gpsimd
        def _(eng):
            P.run(eng, "gpsimd")


def _host_prepare(inputs):
    x = np.asarray(inputs["L0_attn_x"], np.float32)[0]
    freqs = np.asarray(inputs["L0_attn_freqs"], np.float32)
    wq = np.asarray(inputs["L0_attn_wq_w"], np.float32)
    wqb = np.asarray(inputs["L0_attn_wq_b"], np.float32)
    wkva = np.asarray(inputs["L0_attn_wkv_a_w"], np.float32)
    wkvb = np.asarray(inputs["L0_attn_wkv_b_w"], np.float32)
    nw = np.asarray(inputs["L0_attn_norm_w"], np.float32)
    ow = np.asarray(inputs["L0_attn_out_w"], np.float32)
    ob = np.asarray(inputs["L0_attn_out_b"], np.float32)
    W1 = np.asarray(inputs["L0_ffn_W1"], np.float32)
    W2 = np.asarray(inputs["L0_ffn_W2"], np.float32)
    W3 = np.asarray(inputs["L0_ffn_W3"], np.float32)

    xT = np.ascontiguousarray(x.T)
    w1t = np.ascontiguousarray(
        W1.T.reshape(DCN, 128, HCN, 128).transpose(2, 1, 0, 3))
    w3t = np.ascontiguousarray(
        W3.T.reshape(DCN, 128, HCN, 128).transpose(2, 1, 0, 3))
    w2t = np.ascontiguousarray(W2.T)
    wqt = np.ascontiguousarray(wq.T)
    wkv_at = np.ascontiguousarray(
        wkva.T.reshape(DCN, 128, LCN, 128).transpose(2, 1, 0, 3))
    wkv_bt = np.ascontiguousarray(wkvb.T)
    owt = np.ascontiguousarray(ow.T)

    h128 = _hadamard(128).astype(BF16NP)
    h64 = _hadamard(64)
    h64bd = np.zeros((128, 128), np.float32)
    h64bd[:64, :64] = h64
    h64bd[64:, 64:] = h64
    h64bd = h64bd.astype(BF16NP)
    ident = np.eye(128, dtype=np.float32).astype(BF16NP)
    identf = np.eye(128, dtype=np.float32)
    ms = np.zeros((128, 128), np.float32)
    for i in range(32):
        e, od = 64 + 2 * i, 64 + 2 * i + 1
        ms[e, od] = 1.0
        ms[od, e] = -1.0
    mask = np.zeros((128, 4 * TT), np.float32)
    q_i = np.arange(128)[:, None]
    kt_i = np.arange(TT)[None, :]
    for v in range(4):
        mask[:, v * TT:(v + 1) * TT] = np.where(kt_i <= v * 128 + q_i, 0.0, MASKVAL)
    ones = np.ones((128, 1), np.float32)
    onesrow = np.ones((1, 128), np.float32)
    outb = np.ascontiguousarray(ob.reshape(DCN, 128).T)
    nwt = np.ascontiguousarray(nw.reshape(LCN, 128).T)
    neginf = np.full((128, 1), -3.0e38, np.float32)
    epsc = np.array([[EPS_LN, EPS_RMS]], np.float32)

    ims = []
    for r in range(NCORES):
        g0, g1 = 2 * r, 2 * r + 1
        cs = np.zeros((128, 4), np.float32)
        for li, g in enumerate((g0, g1)):
            cvec = np.ones(128, np.float32)
            svec = np.zeros(128, np.float32)
            cvec[64 + 2 * np.arange(32)] = freqs[g, :, 0]
            cvec[64 + 2 * np.arange(32) + 1] = freqs[g, :, 0]
            svec[64 + 2 * np.arange(32)] = freqs[g, :, 1]
            svec[64 + 2 * np.arange(32) + 1] = freqs[g, :, 1]
            cs[:, 2 * li] = cvec
            cs[:, 2 * li + 1] = svec
        wkvb_sl = np.ascontiguousarray(np.concatenate(
            [wkv_bt[:, g0 * 128:(g0 + 1) * 128], wkv_bt[:, g1 * 128:(g1 + 1) * 128],
             wkv_bt[:, 2048 + g0 * 128:2048 + (g0 + 1) * 128],
             wkv_bt[:, 2048 + g1 * 128:2048 + (g1 + 1) * 128]], axis=1))
        wq_sl = np.ascontiguousarray(wqt[:, g0 * 128:(g1 + 1) * 128])
        wqb_sl = np.ascontiguousarray(
            np.stack([wqb[g0 * 128:(g0 + 1) * 128], wqb[g1 * 128:(g1 + 1) * 128]],
                     axis=1))
        ims.append(dict(
            x_in=np.ascontiguousarray(xT[:, r * TOK:(r + 1) * TOK]),
            w1t=w1t, w3t=w3t, w2t=w2t, wqt_s=wq_sl, wkv_at=wkv_at,
            wkvb_s=wkvb_sl, owt=owt,
            c_id=ident, c_h128=h128, c_h64=h64bd, c_idf=identf, c_ms=ms,
            c_mask=mask, c_cs=cs, c_ones=ones, c_onesrow=onesrow,
            c_wqb=wqb_sl, c_outb=outb, c_nw=nwt, c_neginf=neginf, c_eps=epsc,
        ))
    return ims


def kernel(**inputs):
    key = _STAGE
    if key not in _CACHE:
        _CACHE[key] = build_program(key)
    nc = _CACHE[key]
    ims = _host_prepare(inputs)
    res = run_bass_kernel_spmd(nc, ims, core_ids=list(range(NCORES)))
    out = np.concatenate([np.asarray(res.results[r]["out_x"]) for r in range(NCORES)],
                         axis=1)
    return np.ascontiguousarray(out.T).reshape(1, S, D).astype(np.float32)


# version 05
# speedup vs baseline: 1.7739x; 1.0026x over previous
import math
import os
import numpy as np
import ml_dtypes
import concourse.bass as bass
import concourse.mybir as mybir
from concourse.bass_utils import run_bass_kernel_spmd
from contextlib import ExitStack

os.environ.setdefault("TRN_ALLOW_ICI_ALL_TO_ALL", "1")

F32 = mybir.dt.float32
BF16 = mybir.dt.bfloat16
AF = mybir.ActivationFunctionType
ALU = mybir.AluOpType
BF16NP = ml_dtypes.bfloat16

S, D, NCORES = 8192, 2048, 8
TOK = S // NCORES
TT = 512
DCN = 16
HID, HCN = 8192, 64
NH, DH, DL, LCN = 16, 128, 512, 4
SCALE_QK = 1.0 / math.sqrt(128.0)
RSQN = 1.0 / math.sqrt(8192.0)
EPS_LN, EPS_RMS = 1e-5, 1e-6
MASKVAL = -1.0e30

ENGS = ("tensor", "vector", "scalar", "sync", "gpsimd")

_STAGE = "full"
_CACHE = {}


def _hadamard(n):
    h = np.array([[1.0]], np.float32)
    while h.shape[0] < n:
        h = np.block([[h, h], [h, -h]])
    return h.astype(np.float32)


class Plan:
    SELF_SEM = {"vector": "sV", "scalar": "sA"}

    def __init__(self, nc):
        self.nc = nc
        self.ops = {e: [] for e in ENGS}
        self.sems = {}
        self.cnt = {}
        self.last_w = {e: {} for e in ENGS}
        self.last_r = {e: {} for e in ENGS}
        self.guar = {e: {} for e in ENGS}

    def mksem(self, st, name):
        h = st.enter_context(self.nc.semaphore(name))
        self.sems[name] = h
        self.cnt[name] = 0

    def wait(self, eng, sem, val):
        if val is None or val <= 0:
            return
        if self.guar[eng].get(sem, 0) >= val:
            return
        self.guar[eng][sem] = val
        self.ops[eng].append(("wait", self.sems[sem], val))

    @staticmethod
    def _apname(v):
        t = getattr(v, "tensor", None)
        if t is not None:
            return getattr(t, "name", None)
        if hasattr(v, "name") and hasattr(v, "dtype"):
            return v.name
        return None

    def _classify(self, args, kw):
        writes, reads = [], []
        for i, a in enumerate(args):
            nm = self._apname(a)
            if nm is not None:
                (writes if i == 0 else reads).append(nm)
        for k, v in kw.items():
            nm = self._apname(v)
            if nm is not None:
                (writes if k in ("out", "accum_out") else reads).append(nm)
        return writes, reads

    def op(self, eng, method, *args, inc=None, incv=1, **kw):
        if eng in self.SELF_SEM and method != "dma_start":
            ssem = self.SELF_SEM[eng]
            writes, reads = self._classify(args, kw)
            need = 0
            for nm in writes + reads:
                need = max(need, self.last_w[eng].get(nm, 0))
            for nm in writes:
                need = max(need, self.last_r[eng].get(nm, 0))
            self.wait(eng, ssem, need)
            if inc is None:
                inc = ssem
            assert inc == ssem, (eng, inc)
            self.cnt[inc] += incv
            tot = self.cnt[inc]
            for nm in writes:
                self.last_w[eng][nm] = tot
            for nm in reads:
                self.last_r[eng][nm] = tot
            self.ops[eng].append(("op", method, args, kw, self.sems[inc], incv))
            return tot
        sem = self.sems[inc] if inc is not None else None
        tot = None
        if inc is not None:
            self.cnt[inc] += incv
            tot = self.cnt[inc]
        self.ops[eng].append(("op", method, args, kw, sem, incv))
        return tot

    def dma(self, eng, out, in_, inc, **kw):
        return self.op(eng, "dma_start", out=out, in_=in_, inc=inc, incv=16, **kw)

    def barrier(self):
        snap = dict(self.cnt)
        for e in ENGS:
            for s, v in snap.items():
                self.wait(e, s, v)

    def run(self, eng_obj, eng_name):
        for rec in self.ops[eng_name]:
            if rec[0] == "wait":
                eng_obj.wait_ge(rec[1], rec[2])
            else:
                _, method, args, kw, sem, iv = rec
                inst = getattr(eng_obj, method)(*args, **kw)
                if sem is not None:
                    inst.then_inc(sem, iv)


class SbufMap:
    def __init__(self, nc, st, total_bytes):
        self.nc = nc
        self.total = total_bytes
        self.base = (nc.sbuf_base + 31) // 32 * 32
        assert self.base + total_bytes <= nc.sbuf_top, (self.base, nc.sbuf_top)

    def at(self, offset, name, shape, dtype):
        per_part = int(np.prod(shape[1:])) * mybir.dt.size(dtype)
        assert offset + per_part <= self.total, (name, offset, per_part)
        return self.nc.alloc_sbuf_tensor_at(name, shape, dtype,
                                            offset=self.base + offset)


def build_program(stage):
    nc = bass.Bass("TRN2", target_bir_lowering=False)

    def din(name, shape, dt=F32):
        return nc.dram_tensor(name, shape, dt, kind="ExternalInput")

    x_in = din("x_in", [D, TOK])
    w1t = din("w1t", [HCN, 128, DCN, 128])
    w3t = din("w3t", [HCN, 128, DCN, 128])
    w2t = din("w2t", [HID, D])
    wqt_s = din("wqt_s", [D, 256])
    wkv_at = din("wkv_at", [LCN, 128, DCN, 128])
    wkvb_s = din("wkvb_s", [DL, 512])
    owt = din("owt", [D, D])
    c_id_d = din("c_id", [128, 128], BF16)
    c_h128_d = din("c_h128", [128, 128], BF16)
    c_h64_d = din("c_h64", [128, 128], BF16)
    c_idf_d = din("c_idf", [128, 128])
    c_ms_d = din("c_ms", [128, 128])
    c_mask_d = din("c_mask", [128, 4 * TT])
    c_cs_d = din("c_cs", [128, 4])
    c_ones_d = din("c_ones", [128, 1])
    c_onesrow_d = din("c_onesrow", [1, 128])
    c_wqb_d = din("c_wqb", [128, 2])
    c_outb_d = din("c_outb", [128, DCN])
    c_nw_d = din("c_nw", [128, LCN])
    c_neginf_d = din("c_neginf", [128, 1])
    c_eps_d = din("c_eps", [1, 2])

    w2a_d = nc.dram_tensor("w2a", [HID, D], BF16)
    w2eff_d = nc.dram_tensor("w2eff", [DCN, 128, HCN, 128], BF16)
    xspill_d = nc.dram_tensor("xspill", [D, TOK], F32)
    ghl_c = nc.dram_tensor("ghl_c", [D + DL, TOK], BF16)
    ghl_g = nc.dram_tensor("ghl_g", [NCORES * (D + DL), TOK], BF16, addr_space="Shared")
    oc_c = nc.dram_tensor("oc_c", [D, TOK], BF16)
    oa_r = nc.dram_tensor("oa_r", [D, TOK], BF16)
    out_x = nc.dram_tensor("out_x", [D, TOK], F32, kind="ExternalOutput")
    if stage == "attn":
        d_ghl = nc.dram_tensor("d_ghl", [NCORES * (D + DL), TOK], BF16,
                               kind="ExternalOutput")
        d_q0 = nc.dram_tensor("d_q0", [128, S], BF16, kind="ExternalOutput")
        d_K = nc.dram_tensor("d_K", [128, S], BF16, kind="ExternalOutput")
        d_V = nc.dram_tensor("d_V", [128, S], BF16, kind="ExternalOutput")
        d_oc = nc.dram_tensor("d_oc", [D, TOK], BF16, kind="ExternalOutput")
        d_oa = nc.dram_tensor("d_oa", [D, TOK], BF16, kind="ExternalOutput")

    st = ExitStack()
    with st:
        KB = 1024
        sm = SbufMap(nc, st, 206 * KB)
        o = 0

        def alloc(name, shape, dtype):
            nonlocal o
            h = sm.at(o, name, shape, dtype)
            o += (int(np.prod(shape[1:])) * mybir.dt.size(dtype) + 31) // 32 * 32
            return h

        c_id = alloc("s_id", [128, 128], BF16)
        c_h128 = alloc("s_h128", [128, 128], BF16)
        c_h64 = alloc("s_h64", [128, 128], BF16)
        c_idf = alloc("s_idf", [128, 128], F32)
        c_ms = alloc("s_ms", [128, 128], F32)
        c_mask = alloc("s_mask", [128, 4 * TT], F32)
        c_cs = alloc("s_cs", [128, 4], F32)
        c_ones = alloc("s_ones", [128, 1], F32)
        c_onesrow = alloc("s_onesrow", [1, 128], F32)
        c_wqb = alloc("s_wqb", [128, 2], F32)
        c_outb = alloc("s_outb", [128, DCN], F32)
        c_nw = alloc("s_nw", [128, LCN], F32)
        c_neginf = alloc("s_neginf", [128, 1], F32)
        c_eps = alloc("s_eps", [1, 2], F32)
        r_a = alloc("s_ra", [1, TOK], F32)
        r_b = alloc("s_rb", [1, TOK], F32)
        rB512 = alloc("s_rB512", [128, TT], F32)
        sc512 = [alloc(f"s_sc{i}", [128, TT], F32) for i in range(2)]
        stg = [alloc(f"s_stg{i}", [128, TT], BF16) for i in range(2)]
        R_sb = [alloc(f"s_R{i}", [128, 128], F32) for i in range(2)]
        ost = [alloc(f"s_ost{i}", [128, 256], BF16) for i in range(2)]
        m_t = [[alloc(f"s_m{sj}{pp}", [128, 1], F32) for pp in range(2)] for sj in range(2)]
        l_t = [alloc(f"s_l{sj}", [128, 1], F32) for sj in range(2)]
        negm_t = [alloc(f"s_negm{sj}", [128, 1], F32) for sj in range(2)]
        rl_t = [alloc(f"s_rl{sj}", [128, 1], F32) for sj in range(2)]
        assert o <= 30 * KB, o

        RX, R1, R2 = 30 * KB, 94 * KB, 158 * KB
        x_sb = sm.at(RX, "v_x", [128, DCN * TOK], F32)
        h2b = [sm.at(RX + i * 16 * KB, f"v_h2b{i}", [128, DCN * TT], BF16) for i in range(2)]
        s_sc = [sm.at(RX + i * 32 * KB, f"v_s{i}", [128, 16 * TT], F32) for i in range(2)]
        x2_sb = sm.at(RX, "v_x2", [128, DCN * TOK], F32)

        w2c_sb = [sm.at(R1 + i * 4 * KB, f"v_w2c{i}", [128, D], BF16) for i in range(2)]
        w2a_sb = [sm.at(R1 + 8 * KB + i * 4 * KB, f"v_w2a{i}", [128, D], BF16) for i in range(2)]
        wc_sb = [sm.at(R1 + 16 * KB + i * 4 * KB, f"v_wc{i}", [128, D], BF16) for i in range(2)]
        weff_sb = [sm.at(R1 + 24 * KB + i * 4 * KB, f"v_weff{i}", [128, D], BF16) for i in range(2)]
        g_sb = sm.at(R1, "v_g", [128, 16 * TOK], BF16)
        w2eb = [sm.at(R1 + 32 * KB + i * 4 * KB, f"v_w2eb{i}", [128, 16 * 128], BF16) for i in range(2)]
        qT = [sm.at(R1 + i * 16 * KB, f"v_qT{i}", [128, S], BF16) for i in range(2)]
        K_sb = sm.at(R1 + 32 * KB, "v_K", [128, S], BF16)
        V_sb = sm.at(R1 + 48 * KB, "v_V", [128, S], BF16)
        owt_sb = sm.at(R1, "v_owt", [128, DCN * TOK], BF16)
        oT_sb = sm.at(R1 + 32 * KB, "v_oT", [128, DCN * TOK], BF16)

        hT_sb = sm.at(R2, "v_hT", [128, DCN * TOK], BF16)
        w1b = [sm.at(R2 + 32 * KB + i * 4 * KB, f"v_w1b{i}", [128, DCN * 128], BF16) for i in range(2)]
        w3b = [sm.at(R2 + 40 * KB + i * 4 * KB, f"v_w3b{i}", [128, DCN * 128], BF16) for i in range(2)]
        watb = [sm.at(R2 + 32 * KB + i * 4 * KB, f"v_watb{i}", [128, DCN * 128], BF16) for i in range(2)]
        wql_sb = sm.at(R2, "v_wql", [128, DCN * 256], BF16)
        pr = [sm.at(R2 + i * 16 * KB, f"v_pr{i}", [128, 16 * TT], BF16) for i in range(2)]
        ln_db = [sm.at(R2 + 32 * KB + i * 4 * KB, f"v_lndb{i}", [128, LCN * TT], BF16) for i in range(2)]
        wkvb_sb = sm.at(R2 + 40 * KB, "v_wkvb", [128, LCN * 512], BF16)
        aT_sb = [sm.at(R2 + 44 * KB + i * KB, f"v_aT{i}", [128, TT], BF16) for i in range(2)]

        psf = [st.enter_context(nc.psum_tensor(f"psf{i}", [128, 512], F32)) for i in range(6)]
        pst = [st.enter_context(nc.psum_tensor(f"pst{i}", [128, 1024], BF16)) for i in range(2)]

        P = Plan(nc)
        for s in ("sS", "sG", "sSC", "sT", "sV", "sA", "sC",
                  "sS0", "sS1", "sG0", "sG1", "sSC0", "sSC1", "sW0", "sW1", "sP"):
            P.mksem(st, s)

        for dst, src in (
            (c_id, c_id_d), (c_h128, c_h128_d), (c_h64, c_h64_d), (c_idf, c_idf_d),
            (c_ms, c_ms_d), (c_mask, c_mask_d), (c_cs, c_cs_d), (c_ones, c_ones_d),
            (c_onesrow, c_onesrow_d), (c_wqb, c_wqb_d), (c_outb, c_outb_d),
            (c_nw, c_nw_d), (c_neginf, c_neginf_d), (c_eps, c_eps_d),
        ):
            P.dma("sync", dst[:, :], src[:, :], "sS")
        P.dma("sync", x_sb.ap().rearrange("p (dc t) -> p dc t", t=TOK),
              x_in.ap().rearrange("(dc p) t -> p dc t", p=128), "sS")
        P.barrier()

        pbank = {i: ("sV", 0) for i in range(6)}

        def load_w2c(c, wait_sT):
            par = c & 1
            P.wait("gpsimd", "sT", wait_sT)
            return P.dma("gpsimd", w2c_sb[par][:, :],
                         w2t[c * 128:(c + 1) * 128, :], f"sG{par}")

        w2c_ld = {0: load_w2c(0, 0), 1: load_w2c(1, 0)}
        w2c_done = {}
        w2a_free = [0, 0]
        rot = 0
        for c in range(HCN):
            par = c & 1
            P.wait("tensor", f"sG{par}", w2c_ld[c])
            banks = [(rot + s4) % 6 for s4 in range(4)]
            t_done = None
            for s4, bk in enumerate(banks):
                sname, v = pbank[bk]
                P.wait("tensor", sname, v)
                t_done = P.op("tensor", "matmul", psf[bk][:, :], c_h128[:, :],
                              w2c_sb[par][:, s4 * 512:(s4 + 1) * 512],
                              start=True, stop=True,
                              inc=("sT" if s4 == 3 else None))
            w2c_done[c] = t_done
            if c + 2 < HCN:
                w2c_ld[c + 2] = load_w2c(c + 2, t_done)
            P.wait("vector", "sT", t_done)
            P.wait("vector", f"sW{par}", w2a_free[par])
            for s4, bk in enumerate(banks):
                v_done = P.op("vector", "tensor_copy",
                              w2a_sb[par][:, s4 * 512:(s4 + 1) * 512], psf[bk][:, :],
                              inc="sV")
                pbank[bk] = ("sV", v_done)
            rot = (rot + 4) % 6
            P.wait("sync", "sV", v_done)
            w2a_free[par] = P.dma("sync", w2a_d[c * 128:(c + 1) * 128, :],
                                  w2a_sb[par][:, :], f"sW{par}")
        P.barrier()

        w2a_r = w2a_d.ap().rearrange("(c pz) d -> pz c d", pz=128)
        w2e_r = w2eff_d.ap().rearrange("dc hrow hc m -> hrow hc dc m")
        pbank = {i: ("sV", 0) for i in range(6)}

        def load_wc(pp, wait_sT):
            par = pp & 1
            P.wait("sync", "sT", wait_sT)
            P.dma("sync", wc_sb[par][0:64, :], w2a_r[2 * pp], f"sS{par}")
            return P.dma("sync", wc_sb[par][64:128, :], w2a_r[2 * pp + 1],
                         f"sS{par}")

        wc_ld = {0: load_wc(0, 0), 1: load_wc(1, 0)}
        weff_free = [0, 0]
        rot = 0
        for pp in range(64):
            par = pp & 1
            P.wait("tensor", f"sS{par}", wc_ld[pp])
            banks = [(rot + s4) % 6 for s4 in range(4)]
            t_done = None
            for s4, bk in enumerate(banks):
                sname, v = pbank[bk]
                P.wait("tensor", sname, v)
                t_done = P.op("tensor", "matmul", psf[bk][:, :], c_h64[:, :],
                              wc_sb[par][:, s4 * 512:(s4 + 1) * 512],
                              start=True, stop=True,
                              inc=("sT" if s4 == 3 else None))
            if pp + 2 < 64:
                wc_ld[pp + 2] = load_wc(pp + 2, t_done)
            P.wait("vector", "sT", t_done)
            P.wait("vector", f"sW{par}", weff_free[par])
            for s4, bk in enumerate(banks):
                v_done = P.op("vector", "tensor_scalar",
                              out=weff_sb[par][:, s4 * 512:(s4 + 1) * 512],
                              in0=psf[bk][:, :], scalar1=RSQN, scalar2=None,
                              op0=ALU.mult, inc="sV")
                pbank[bk] = ("sV", v_done)
            rot = (rot + 4) % 6
            P.wait("sync", "sV", v_done)
            P.dma("sync", w2e_r[2 * pp],
                  weff_sb[par][0:64, :].rearrange("c (dc m) -> c dc m", m=128),
                  f"sW{par}")
            weff_free[par] = P.dma(
                "sync", w2e_r[2 * pp + 1],
                weff_sb[par][64:128, :].rearrange("c (dc m) -> c dc m", m=128),
                f"sW{par}")
        P.barrier()

        for hl in range(2):
            P.op("vector", "tensor_scalar", out=rB512[:, 0:128], in0=c_idf[:, :],
                 scalar1=c_cs[:, 2 * hl:2 * hl + 1], scalar2=None, op0=ALU.mult)
            P.op("vector", "scalar_tensor_tensor", out=R_sb[hl][:, :], in0=c_ms[:, :],
                 scalar=c_cs[:, 2 * hl + 1:2 * hl + 2], in1=rB512[:, 0:128],
                 op0=ALU.mult, op1=ALU.add, inc="sV")
        P.barrier()

        def emit_ln(src_sb):
            sq_par = 0
            consumed = [0, 0]
            for tt in range(2):
                for dc in range(DCN):
                    xs = src_sb[:, dc * TOK + tt * TT: dc * TOK + (tt + 1) * TT]
                    P.wait("scalar", "sT", consumed[sq_par])
                    a_done = P.op("scalar", "activation", out=sc512[sq_par][:, :],
                                  in_=xs, func=AF.Square, inc="sA")
                    P.wait("tensor", "sA", a_done)
                    P.op("tensor", "matmul", psf[2 + tt][0:1, :], c_ones[:, :],
                         sc512[sq_par][:, :],
                         start=(dc == 0), stop=(dc == DCN - 1))
                    t_done = P.op("tensor", "matmul", psf[tt][0:1, :], c_ones[:, :], xs,
                                  start=(dc == 0), stop=(dc == DCN - 1), inc="sT")
                    consumed[sq_par] = t_done
                    sq_par ^= 1
            P.wait("vector", "sT", P.cnt["sT"])
            for tt in range(2):
                rsl = slice(tt * TT, (tt + 1) * TT)
                P.op("vector", "tensor_scalar", out=r_a[0:1, rsl], in0=psf[tt][0:1, :],
                     scalar1=1.0 / D, scalar2=None, op0=ALU.mult)
                P.op("vector", "tensor_scalar", out=r_b[0:1, rsl], in0=psf[2 + tt][0:1, :],
                     scalar1=1.0 / D, scalar2=None, op0=ALU.mult)
                P.op("vector", "tensor_mul", psf[4][0:1, :], r_a[0:1, rsl], r_a[0:1, rsl])
                v_done = P.op("vector", "tensor_sub", r_b[0:1, rsl], r_b[0:1, rsl],
                              psf[4][0:1, :], inc="sV")
                P.wait("scalar", "sV", v_done)
                a_done = P.op("scalar", "activation", out=psf[5][0:1, :], in_=r_b[0:1, rsl],
                              func=AF.Sqrt, bias=c_eps[0:1, 0:1], scale=1.0, inc="sA")
                P.wait("vector", "sA", a_done)
                P.op("vector", "reciprocal", r_b[0:1, rsl], psf[5][0:1, :], inc="sV")
            P.wait("tensor", "sV", P.cnt["sV"])
            bc_done = None
            for tt in range(2):
                rsl = slice(tt * TT, (tt + 1) * TT)
                P.op("tensor", "matmul", psf[2 + tt][:, :], c_onesrow[:, :], r_a[0:1, rsl],
                     start=True, stop=True)
                bc_done = P.op("tensor", "matmul", psf[4 + tt][:, :], c_onesrow[:, :],
                               r_b[0:1, rsl], start=True, stop=True,
                               inc=("sT" if tt == 1 else None))
            P.wait("vector", "sT", bc_done)
            par = 0
            for dc in range(DCN):
                for tt in range(2):
                    xs = src_sb[:, dc * TOK + tt * TT: dc * TOK + (tt + 1) * TT]
                    hs = hT_sb[:, dc * TOK + tt * TT: dc * TOK + (tt + 1) * TT]
                    P.op("vector", "tensor_sub", sc512[par][:, :], xs, psf[2 + tt][:, :])
                    P.op("vector", "tensor_mul", hs, sc512[par][:, :], psf[4 + tt][:, :],
                         inc="sV")
                    par ^= 1
            P.barrier()

        def emit_ffn(res_sb):
            for qtr in range(4):
                gbank = {i: ("sV", 0) for i in range(6)}

                def load_w13(i, wait_sT):
                    hc = qtr * 16 + i
                    par = i & 1
                    P.wait("gpsimd", "sT", wait_sT)
                    P.dma("gpsimd",
                          w1b[par].ap().rearrange("p (dc m) -> p dc m", m=128),
                          w1t[hc], f"sG{par}")
                    return P.dma(
                        "gpsimd",
                        w3b[par].ap().rearrange("p (dc m) -> p dc m", m=128),
                        w3t[hc], f"sG{par}")

                w_ld = {0: load_w13(0, 0), 1: load_w13(1, 0)}
                for i in range(16):
                    par = i & 1
                    b = [(4 * i) % 6, (4 * i + 1) % 6, (4 * i + 2) % 6, (4 * i + 3) % 6]
                    P.wait("tensor", f"sG{par}", w_ld[i])
                    for bb in b:
                        sname, v = gbank[bb]
                        P.wait("tensor", sname, v)
                    for k in range(DCN):
                        lhs = w1b[par][:, k * 128:(k + 1) * 128]
                        P.op("tensor", "matmul", psf[b[0]][:, :], lhs,
                             hT_sb[:, k * TOK:k * TOK + TT],
                             start=(k == 0), stop=(k == DCN - 1))
                        P.op("tensor", "matmul", psf[b[1]][:, :], lhs,
                             hT_sb[:, k * TOK + TT:k * TOK + 2 * TT],
                             start=(k == 0), stop=(k == DCN - 1))
                    t_done = None
                    for k in range(DCN):
                        lhs = w3b[par][:, k * 128:(k + 1) * 128]
                        P.op("tensor", "matmul", psf[b[2]][:, :], lhs,
                             hT_sb[:, k * TOK:k * TOK + TT],
                             start=(k == 0), stop=(k == DCN - 1))
                        t_done = P.op("tensor", "matmul", psf[b[3]][:, :], lhs,
                                      hT_sb[:, k * TOK + TT:k * TOK + 2 * TT],
                                      start=(k == 0), stop=(k == DCN - 1),
                                      inc=("sT" if k == DCN - 1 else None))
                    if i + 2 < 16:
                        w_ld[i + 2] = load_w13(i + 2, t_done)
                    P.wait("scalar", "sT", t_done)
                    P.wait("vector", "sT", t_done)
                    for tt in range(2):
                        spar = (2 * i + tt) & 1
                        a_done = P.op("scalar", "activation", out=sc512[spar][:, :],
                                      in_=psf[b[tt]][:, :], func=AF.Silu, inc="sA")
                        P.wait("vector", "sA", a_done)
                        v_done = P.op("vector", "tensor_mul",
                                      g_sb[:, i * TOK + tt * TT:i * TOK + (tt + 1) * TT],
                                      sc512[spar][:, :], psf[b[2 + tt]][:, :], inc="sV")
                        gbank[b[tt]] = ("sA", a_done)
                        gbank[b[2 + tt]] = ("sV", v_done)
                        P.wait("scalar", "sV", v_done)
                g_ready = P.cnt["sV"]

                def load_w2e(dc, wait_sT):
                    par = dc & 1
                    P.wait("sync", "sT", wait_sT)
                    return P.dma(
                        "sync", w2eb[par].ap().rearrange("p (hc m) -> p hc m", m=128),
                        w2eff_d[dc, :, qtr * 16:(qtr + 1) * 16, :], f"sS{par}")

                e_ld = {0: load_w2e(0, 0), 1: load_w2e(1, 0)}
                mb_free = [0, 0, 0, 0]
                P.wait("tensor", "sV", g_ready)
                for dc in range(DCN):
                    par = dc & 1
                    P.wait("tensor", f"sS{par}", e_ld[dc])
                    bk0, bk1 = (2 * dc) % 4, (2 * dc) % 4 + 1
                    P.wait("tensor", "sV", mb_free[bk0])
                    P.wait("tensor", "sV", mb_free[bk1])
                    t_done = None
                    for hcl in range(16):
                        lhs = w2eb[par][:, hcl * 128:(hcl + 1) * 128]
                        P.op("tensor", "matmul", psf[bk0][:, :], lhs,
                             g_sb[:, hcl * TOK:hcl * TOK + TT],
                             start=(hcl == 0), stop=(hcl == 15))
                        t_done = P.op("tensor", "matmul", psf[bk1][:, :], lhs,
                                      g_sb[:, hcl * TOK + TT:hcl * TOK + 2 * TT],
                                      start=(hcl == 0), stop=(hcl == 15),
                                      inc=("sT" if hcl == 15 else None))
                    P.wait("vector", "sT", t_done)
                    for tt, bk in ((0, bk0), (1, bk1)):
                        mb_free[bk] = P.op(
                            "vector", "scalar_tensor_tensor",
                            out=res_sb[:, dc * TOK + tt * TT:dc * TOK + (tt + 1) * TT],
                            in0=psf[bk][:, :], scalar=0.5,
                            in1=res_sb[:, dc * TOK + tt * TT:dc * TOK + (tt + 1) * TT],
                            op0=ALU.mult, op1=ALU.add, inc="sV")
                    if dc + 2 < DCN:
                        e_ld[dc + 2] = load_w2e(dc + 2, t_done)
                P.barrier()

        emit_ln(x_sb)
        emit_ffn(x_sb)
        if stage == "ffn1":
            fin = P.dma("sync", out_x.ap().rearrange("(dc p) t -> p dc t", p=128),
                        x_sb.ap().rearrange("p (dc t) -> p dc t", t=TOK), "sS")
            P.wait("sync", "sS", fin)
            _emit_block(nc, P)
            return nc

        emit_ln(x_sb)
        P.dma("scalar", ghl_c[0:D, :].rearrange("(dc p) t -> p dc t", p=128),
              hT_sb.ap().rearrange("p (dc t) -> p dc t", t=TOK), "sSC")
        st_free = [0, 0]
        for tt in range(2):
            def load_wat(lc, wait_sT):
                par = lc & 1
                P.wait("gpsimd", "sT", wait_sT)
                return P.dma(
                    "gpsimd", watb[par].ap().rearrange("p (dc m) -> p dc m", m=128),
                    wkv_at[lc], f"sG{par}")

            at_ld = {0: load_wat(0, 0), 1: load_wat(1, 0)}
            for lc in range(LCN):
                par = lc & 1
                P.wait("tensor", f"sG{par}", at_ld[lc])
                t_done = None
                for k in range(DCN):
                    t_done = P.op("tensor", "matmul", psf[lc][:, :],
                                  watb[par][:, k * 128:(k + 1) * 128],
                                  hT_sb[:, k * TOK + tt * TT:k * TOK + (tt + 1) * TT],
                                  start=(k == 0), stop=(k == DCN - 1),
                                  inc=("sT" if k == DCN - 1 else None))
                if lc + 2 < LCN:
                    at_ld[lc + 2] = load_wat(lc + 2, t_done)
            lat_done = P.cnt["sT"]
            P.wait("scalar", "sT", lat_done)
            sq_done = {}
            for lc in range(LCN):
                spar = lc & 1
                P.wait("scalar", "sT", sq_done.get(lc - 2, 0))
                a_done = P.op("scalar", "activation", out=sc512[spar][:, :],
                              in_=psf[lc][:, :], func=AF.Square, inc="sA")
                P.wait("tensor", "sA", a_done)
                sq_done[lc] = P.op("tensor", "matmul", psf[4][0:1, :], c_ones[:, :],
                                   sc512[spar][:, :], start=(lc == 0),
                                   stop=(lc == LCN - 1), inc="sT")
            P.wait("vector", "sT", P.cnt["sT"])
            v_done = P.op("vector", "tensor_scalar", out=r_b[0:1, 0:TT],
                          in0=psf[4][0:1, :], scalar1=1.0 / DL, scalar2=None,
                          op0=ALU.mult, inc="sV")
            P.wait("scalar", "sV", v_done)
            a_done = P.op("scalar", "activation", out=psf[5][0:1, :], in_=r_b[0:1, 0:TT],
                          func=AF.Sqrt, bias=c_eps[0:1, 1:2], scale=1.0, inc="sA")
            P.wait("vector", "sA", a_done)
            v_done = P.op("vector", "reciprocal", r_b[0:1, 0:TT], psf[5][0:1, :], inc="sV")
            P.wait("tensor", "sV", v_done)
            t_done = P.op("tensor", "matmul", psf[5][:, :], c_onesrow[:, :],
                          r_b[0:1, 0:TT], start=True, stop=True, inc="sT")
            P.wait("vector", "sT", t_done)
            P.op("vector", "tensor_copy", rB512[:, :], psf[5][:, :])
            for lc in range(LCN):
                spar = lc & 1
                P.wait("vector", f"sSC{spar}", st_free[spar])
                v_done = P.op("vector", "scalar_tensor_tensor", out=stg[spar][:, :],
                              in0=psf[lc][:, :], scalar=c_nw[:, lc:lc + 1],
                              in1=rB512[:, :], op0=ALU.mult, op1=ALU.mult, inc="sV")
                P.wait("scalar", "sV", v_done)
                st_free[spar] = P.dma(
                    "scalar",
                    ghl_c[D + lc * 128:D + (lc + 1) * 128, tt * TT:(tt + 1) * TT],
                    stg[spar][:, :], f"sSC{spar}")
            P.barrier()
        P.dma("scalar", xspill_d.ap().rearrange("(dc p) t -> p dc t", p=128),
              x_sb.ap().rearrange("p (dc t) -> p dc t", t=TOK), "sSC")
        P.barrier()
        P.op("gpsimd", "collective_compute", "AllGather", ALU.bypass,
             replica_groups=[list(range(NCORES))],
             ins=[ghl_c.ap().opt()], outs=[ghl_g.ap().opt()], inc="sC")
        P.barrier()

        ghl_r = ghl_g.ap().rearrange("(rk f20 p) t -> rk p f20 t", rk=8, f20=20, p=128)

        P.dma("gpsimd", wql_sb.ap().rearrange("p (dc m) -> p dc m", m=256),
              wqt_s.ap().rearrange("(dc p) m -> p dc m", p=128), "sG")
        P.dma("gpsimd", wkvb_sb.ap().rearrange("p (lc f) -> p lc f", f=512),
              wkvb_s.ap().rearrange("(lc p) f -> p lc f", p=128), "sG")
        P.barrier()

        def load_h2(t16, wait_sT):
            par = t16 & 1
            rk, slot = t16 // 2, t16 % 2
            P.wait("sync", "sT", wait_sT)
            return P.dma(
                "sync", h2b[par].ap().rearrange("p (dc t) -> p dc t", t=TT),
                ghl_r[rk, :, 0:16, slot * TT:(slot + 1) * TT], f"sS{par}")

        h2_ld = {0: load_h2(0, 0), 1: load_h2(1, 0)}
        sc_free = [0, 0]
        bfree = {}
        for t16 in range(16):
            par = t16 & 1
            P.wait("tensor", f"sS{par}", h2_ld[t16])
            t_last = None
            for hl in range(2):
                bq = (2 * t16 + hl) % 3
                bq2 = 3 + (2 * t16 + hl) % 3
                sname, v = bfree.get(bq, ("sV", 0))
                P.wait("tensor", sname, v)
                t_done = None
                for k in range(DCN):
                    t_done = P.op("tensor", "matmul", psf[bq][:, :],
                                  wql_sb[:, k * 256 + hl * 128:k * 256 + (hl + 1) * 128],
                                  h2b[par][:, k * TT:(k + 1) * TT],
                                  start=(k == 0), stop=(k == DCN - 1),
                                  inc=("sT" if k == DCN - 1 else None))
                spar = (2 * t16 + hl) & 1
                P.wait("vector", "sT", t_done)
                P.wait("vector", "sT", sc_free[spar])
                v_done = P.op("vector", "tensor_scalar", out=sc512[spar][:, :],
                              in0=psf[bq][:, :], scalar1=c_wqb[:, hl:hl + 1],
                              scalar2=None, op0=ALU.add, inc="sV")
                bfree[bq] = ("sV", v_done)
                sname, v = bfree.get(bq2, ("sV", 0))
                P.wait("tensor", sname, v)
                P.wait("tensor", "sV", v_done)
                t2 = P.op("tensor", "matmul", psf[bq2][:, :], R_sb[hl][:, :],
                          sc512[spar][:, :], start=True, stop=True, inc="sT")
                sc_free[spar] = t2
                t_last = t2
                P.wait("vector", "sT", t2)
                v2 = P.op("vector", "tensor_copy",
                          qT[hl][:, t16 * TT:(t16 + 1) * TT], psf[bq2][:, :], inc="sV")
                bfree[bq2] = ("sV", v2)
            if t16 + 2 < 16:
                h2_ld[t16 + 2] = load_h2(t16 + 2, t_last)
        P.barrier()

        if stage == "attn":
            P.dma("sync", d_ghl.ap(), ghl_g.ap(), "sS")
            P.dma("sync", d_q0[:, :], qT[0][:, :], "sS")
            P.barrier()

        for hl in range(2):
            def load_ln(t16, wait_sT):
                par = t16 & 1
                rk, slot = t16 // 2, t16 % 2
                P.wait("sync", "sT", wait_sT)
                return P.dma(
                    "sync", ln_db[par].ap().rearrange("p (lc t) -> p lc t", t=TT),
                    ghl_r[rk, :, 16:20, slot * TT:(slot + 1) * TT], f"sS{par}")

            ln_ld = {0: load_ln(0, 0), 1: load_ln(1, 0)}
            scp = 0
            vtp = 0
            bfree = {}
            tfree = {0: ("sV", 0), 1: ("sV", 0)}
            for t16 in range(16):
                par = t16 & 1
                P.wait("tensor", f"sS{par}", ln_ld[t16])
                bk, bv, br = (3 * t16) % 6, (3 * t16 + 1) % 6, (3 * t16 + 2) % 6
                for bb in (bk, bv, br):
                    sname, v = bfree.get(bb, ("sV", 0))
                    P.wait("tensor", sname, v)
                for lc in range(LCN):
                    P.op("tensor", "matmul", psf[bk][:, :],
                         wkvb_sb[:, lc * 512 + hl * 128:lc * 512 + (hl + 1) * 128],
                         ln_db[par][:, lc * TT:(lc + 1) * TT],
                         start=(lc == 0), stop=(lc == LCN - 1))
                t_done = None
                for lc in range(LCN):
                    t_done = P.op(
                        "tensor", "matmul", psf[bv][:, :],
                        wkvb_sb[:, lc * 512 + 256 + hl * 128:lc * 512 + 256 + (hl + 1) * 128],
                        ln_db[par][:, lc * TT:(lc + 1) * TT],
                        start=(lc == 0), stop=(lc == LCN - 1),
                        inc=("sT" if lc == LCN - 1 else None))
                if t16 + 2 < 16:
                    ln_ld[t16 + 2] = load_ln(t16 + 2, t_done)
                P.wait("vector", "sT", t_done)
                P.wait("vector", "sT", sc_free[scp])
                vk = P.op("vector", "tensor_copy", sc512[scp][:, :], psf[bk][:, :],
                          inc="sV")
                bfree[bk] = ("sV", vk)
                P.wait("tensor", "sV", vk)
                tk = P.op("tensor", "matmul", psf[br][:, :], R_sb[hl][:, :],
                          sc512[scp][:, :], start=True, stop=True, inc="sT")
                sc_free[scp] = tk
                scp ^= 1
                P.wait("vector", "sT", tk)
                vk2 = P.op("vector", "tensor_copy",
                           K_sb[:, t16 * TT:(t16 + 1) * TT], psf[br][:, :], inc="sV")
                bfree[br] = ("sV", vk2)
                vv = P.op("vector", "tensor_copy", stg[vtp][:, :], psf[bv][:, :],
                          inc="sV")
                bfree[bv] = ("sV", vv)
                tb = t16 & 1
                sname, v = tfree[tb]
                P.wait("tensor", sname, v)
                P.wait("tensor", "sV", vv)
                tv = None
                for j in range(4):
                    tv = P.op("tensor", "transpose", pst[tb][:, j * 128:(j + 1) * 128],
                              stg[vtp][:, j * 128:(j + 1) * 128], c_id[:, :],
                              inc=("sT" if j == 3 else None))
                vtp ^= 1
                P.wait("vector", "sT", tv)
                vv2 = P.op("vector", "tensor_copy",
                           V_sb[:, t16 * TT:(t16 + 1) * TT], pst[tb][:, 0:512], inc="sV")
                tfree[tb] = ("sV", vv2)
            P.barrier()
            if stage == "attn" and hl == 0:
                P.dma("sync", d_K[:, :], K_sb[:, :], "sS")
                P.dma("sync", d_V[:, :], V_sb[:, :], "sS")
                P.barrier()

            st_c = {
                "s_free": [0, 0], "pr_free": [0, 0], "l_free": [0, 0],
                "aT_free": [0, 0], "ost_free": [0, 0],
                "sbank": {i: ("sV", 0) for i in range(4)},
                "tpar": {0: ("sV", 0), 1: ("sV", 0)},
                "oQ": ("sV", 0),
                "scale_done": [0, 0],
                "rl_free": [0, 0],
            }

            def emit_A_step(p, kc):
                cov = p // 2 + 1
                for sj in range(2):
                    js = 2 * p + sj
                    spar = (2 * kc + sj) & 3
                    sname, v = st_c["sbank"][spar]
                    P.wait("tensor", sname, v)
                    tsc = P.op("tensor", "matmul", psf[spar][:, :],
                               qT[hl][:, js * 128:(js + 1) * 128],
                               K_sb[:, kc * TT:(kc + 1) * TT],
                               start=True, stop=True, inc="sT")
                    P.wait("vector", "sT", tsc)
                    if kc == 0:
                        P.wait("vector", "sA", st_c["s_free"][sj])
                    mprev = c_neginf if kc == 0 else m_t[sj][(kc - 1) & 1]
                    ssl = s_sc[sj][:, kc * TT:(kc + 1) * TT]
                    if kc == cov - 1:
                        mv = (js % 4) * TT
                        P.op("vector", "scalar_tensor_tensor", out=ssl,
                             in0=psf[spar][:, :], scalar=SCALE_QK,
                             in1=c_mask[:, mv:mv + TT], op0=ALU.mult, op1=ALU.add)
                        vdone = P.op("vector", "tensor_scalar", out=ssl, in0=ssl,
                                     scalar1=1.0, scalar2=mprev[:, :],
                                     op0=ALU.mult, op1=ALU.max,
                                     accum_out=m_t[sj][kc & 1][:, :], inc="sV")
                    else:
                        vdone = P.op("vector", "tensor_scalar", out=ssl,
                                     in0=psf[spar][:, :], scalar1=SCALE_QK,
                                     scalar2=mprev[:, :], op0=ALU.mult, op1=ALU.max,
                                     accum_out=m_t[sj][kc & 1][:, :], inc="sV")
                    st_c["sbank"][spar] = ("sV", vdone)

            def emit_B(p):
                cov = p // 2 + 1
                deferred = []
                for sj in range(2):
                    mfin = m_t[sj][(cov - 1) & 1]
                    nm = P.op("vector", "tensor_scalar", out=negm_t[sj][:, :],
                              in0=mfin[:, :], scalar1=-1.0, scalar2=None,
                              op0=ALU.mult, inc="sV")
                    P.wait("scalar", "sV", nm)
                    P.wait("scalar", "sT", st_c["pr_free"][sj])
                    adone = P.op("scalar", "activation",
                                 out=pr[sj][:, 0:cov * TT],
                                 in_=s_sc[sj][:, 0:cov * TT],
                                 func=AF.Exp, bias=negm_t[sj][:, :], scale=1.0,
                                 accum_out=l_t[sj][:, :], inc="sA")
                    st_c["s_free"][sj] = adone

                    def scale_step(sj=sj, adone=adone, cov=cov):
                        P.wait("vector", "sA", adone)
                        P.op("vector", "reciprocal", rl_t[sj][:, :], l_t[sj][:, :],
                             inc="sV")
                        st_c["scale_done"][sj] = P.op(
                            "vector", "tensor_scalar", out=pr[sj][:, 0:cov * TT],
                            in0=pr[sj][:, 0:cov * TT], scalar1=rl_t[sj][:, :],
                            scalar2=None, op0=ALU.mult, inc="sV")
                    deferred.append(scale_step)
                return deferred

            def make_C_steps(p):
                cov = p // 2 + 1
                opar = p & 1
                steps = []

                def pre():
                    P.wait("tensor", "sV", max(st_c["scale_done"]))
                    sname, v = st_c["oQ"]
                    P.wait("tensor", sname, v)
                steps.append(pre)

                def upstep(up, last):
                    u0, u1 = 2 * up, 2 * up + 1
                    tb = up & 1
                    sname, v = st_c["tpar"][tb]
                    P.wait("tensor", sname, v)
                    ttr = None
                    for pos, (sj, u) in enumerate(((0, u0), (1, u0), (0, u1), (1, u1))):
                        ttr = P.op("tensor", "transpose",
                                   pst[tb][:, pos * 128:(pos + 1) * 128],
                                   pr[sj][:, u * 128:(u + 1) * 128], c_id[:, :],
                                   inc=("sT" if pos == 3 else None))
                    P.wait("vector", "sT", ttr)
                    P.wait("vector", "sT", st_c["aT_free"][tb])
                    vev = P.op("vector", "tensor_copy", aT_sb[tb][:, :],
                               pst[tb][:, 0:512], inc="sV")
                    st_c["tpar"][tb] = ("sV", vev)
                    P.wait("tensor", "sV", vev)
                    P.op("tensor", "matmul", psf[4][:, 0:256],
                         V_sb[:, u0 * 128:(u0 + 1) * 128], aT_sb[tb][:, 0:256],
                         start=(up == 0), stop=False)
                    last_av = P.op("tensor", "matmul", psf[4][:, 0:256],
                                   V_sb[:, u1 * 128:(u1 + 1) * 128],
                                   aT_sb[tb][:, 256:512],
                                   start=False, stop=last, inc="sT")
                    st_c["aT_free"][tb] = last_av
                    if last:
                        st_c["pr_free"][0] = last_av
                        st_c["pr_free"][1] = last_av
                        P.wait("vector", "sT", last_av)
                        P.wait("vector", f"sSC{opar}", st_c["ost_free"][opar])
                        vo = P.op("vector", "tensor_copy", ost[opar][:, :],
                                  psf[4][:, 0:256], inc="sV")
                        st_c["oQ"] = ("sV", vo)
                        P.wait("scalar", "sV", vo)
                        st_c["ost_free"][opar] = P.dma(
                            "scalar",
                            oc_c[(p // 4) * 256 + hl * 128:
                                 (p // 4) * 256 + (hl + 1) * 128,
                                 (p % 4) * 256:(p % 4 + 1) * 256],
                            ost[opar][:, :], f"sSC{opar}")

                for up in range(cov * 2):
                    steps.append((lambda u=up, lst=(up == cov * 2 - 1):
                                  upstep(u, lst)))
                return steps

            cq = []
            for p in range(32):
                cov = p // 2 + 1
                for kc in range(cov):
                    emit_A_step(p, kc)
                    for _ in range(3):
                        if cq:
                            cq.pop(0)()
                while cq:
                    cq.pop(0)()
                cq = emit_B(p) + make_C_steps(p)
            while cq:
                cq.pop(0)()
            P.barrier()

        if stage == "attn":
            P.dma("sync", d_oc.ap(), oc_c.ap(), "sS")
            P.barrier()

        P.op("gpsimd", "collective_compute", "AllToAll", ALU.bypass,
             replica_groups=[list(range(NCORES))],
             ins=[oc_c.ap().opt()], outs=[oa_r.ap().opt()], inc="sC")
        P.barrier()

        if stage == "attn":
            P.dma("sync", d_oa.ap(), oa_r.ap(), "sS")
            P.barrier()

        P.dma("sync", x2_sb.ap().rearrange("p (dc t) -> p dc t", t=TOK),
              xspill_d.ap().rearrange("(dc p) t -> p dc t", p=128), "sS")
        for fc in range(DCN):
            P.dma("sync", oT_sb[:, fc * TOK:(fc + 1) * TOK],
                  oa_r[fc * 128:(fc + 1) * 128, :], "sS")
        P.barrier()
        for dhalf in range(2):
            wl = P.dma("gpsimd", owt_sb.ap().rearrange("p (fc d) -> p fc d", d=TOK),
                       owt[:, dhalf * TOK:(dhalf + 1) * TOK].rearrange("(fc p) d -> p fc d", p=128),
                       "sG")
            P.wait("tensor", "sG", wl)
            obank = {0: ("sV", 0), 1: ("sV", 0)}
            step = 0
            for dc8 in range(8):
                dc = dhalf * 8 + dc8
                for tt in range(2):
                    bk = step & 1
                    sname, v = obank[bk]
                    P.wait("tensor", sname, v)
                    t_done = None
                    for fc in range(DCN):
                        t_done = P.op(
                            "tensor", "matmul", psf[bk][:, :],
                            owt_sb[:, fc * TOK + dc8 * 128:fc * TOK + (dc8 + 1) * 128],
                            oT_sb[:, fc * TOK + tt * TT:fc * TOK + (tt + 1) * TT],
                            start=(fc == 0), stop=(fc == DCN - 1),
                            inc=("sT" if fc == DCN - 1 else None))
                    P.wait("vector", "sT", t_done)
                    vdone = P.op("vector", "scalar_tensor_tensor",
                                 out=x2_sb[:, dc * TOK + tt * TT:dc * TOK + (tt + 1) * TT],
                                 in0=psf[bk][:, :], scalar=c_outb[:, dc:dc + 1],
                                 in1=x2_sb[:, dc * TOK + tt * TT:dc * TOK + (tt + 1) * TT],
                                 op0=ALU.add, op1=ALU.add, inc="sV")
                    obank[bk] = ("sV", vdone)
                    step += 1
            P.barrier()

        if stage == "attn":
            fin = P.dma("sync", out_x.ap().rearrange("(dc p) t -> p dc t", p=128),
                        x2_sb.ap().rearrange("p (dc t) -> p dc t", t=TOK), "sS")
            P.wait("sync", "sS", fin)
            _emit_block(nc, P)
            return nc

        emit_ln(x2_sb)
        emit_ffn(x2_sb)

        fin = P.dma("sync", out_x.ap().rearrange("(dc p) t -> p dc t", p=128),
                    x2_sb.ap().rearrange("p (dc t) -> p dc t", t=TOK), "sS")
        P.wait("sync", "sS", fin)
        _emit_block(nc, P)
    return nc


def _emit_block(nc, P):
    with nc.Block() as blk:
        @blk.tensor
        def _(eng):
            P.run(eng, "tensor")

        @blk.vector
        def _(eng):
            P.run(eng, "vector")

        @blk.scalar
        def _(eng):
            P.run(eng, "scalar")

        @blk.sync
        def _(eng):
            P.run(eng, "sync")

        @blk.gpsimd
        def _(eng):
            P.run(eng, "gpsimd")


def _host_prepare(inputs):
    x = np.asarray(inputs["L0_attn_x"], np.float32)[0]
    freqs = np.asarray(inputs["L0_attn_freqs"], np.float32)
    wq = np.asarray(inputs["L0_attn_wq_w"], np.float32)
    wqb = np.asarray(inputs["L0_attn_wq_b"], np.float32)
    wkva = np.asarray(inputs["L0_attn_wkv_a_w"], np.float32)
    wkvb = np.asarray(inputs["L0_attn_wkv_b_w"], np.float32)
    nw = np.asarray(inputs["L0_attn_norm_w"], np.float32)
    ow = np.asarray(inputs["L0_attn_out_w"], np.float32)
    ob = np.asarray(inputs["L0_attn_out_b"], np.float32)
    W1 = np.asarray(inputs["L0_ffn_W1"], np.float32)
    W2 = np.asarray(inputs["L0_ffn_W2"], np.float32)
    W3 = np.asarray(inputs["L0_ffn_W3"], np.float32)

    xT = np.ascontiguousarray(x.T)
    w1t = np.ascontiguousarray(
        W1.T.reshape(DCN, 128, HCN, 128).transpose(2, 1, 0, 3))
    w3t = np.ascontiguousarray(
        W3.T.reshape(DCN, 128, HCN, 128).transpose(2, 1, 0, 3))
    w2t = np.ascontiguousarray(W2.T)
    wqt = np.ascontiguousarray(wq.T)
    wkv_at = np.ascontiguousarray(
        wkva.T.reshape(DCN, 128, LCN, 128).transpose(2, 1, 0, 3))
    wkv_bt = np.ascontiguousarray(wkvb.T)
    owt = np.ascontiguousarray(ow.T)

    h128 = _hadamard(128).astype(BF16NP)
    h64 = _hadamard(64)
    h64bd = np.zeros((128, 128), np.float32)
    h64bd[:64, :64] = h64
    h64bd[64:, 64:] = h64
    h64bd = h64bd.astype(BF16NP)
    ident = np.eye(128, dtype=np.float32).astype(BF16NP)
    identf = np.eye(128, dtype=np.float32)
    ms = np.zeros((128, 128), np.float32)
    for i in range(32):
        e, od = 64 + 2 * i, 64 + 2 * i + 1
        ms[e, od] = 1.0
        ms[od, e] = -1.0
    mask = np.zeros((128, 4 * TT), np.float32)
    q_i = np.arange(128)[:, None]
    kt_i = np.arange(TT)[None, :]
    for v in range(4):
        mask[:, v * TT:(v + 1) * TT] = np.where(kt_i <= v * 128 + q_i, 0.0, MASKVAL)
    ones = np.ones((128, 1), np.float32)
    onesrow = np.ones((1, 128), np.float32)
    outb = np.ascontiguousarray(ob.reshape(DCN, 128).T)
    nwt = np.ascontiguousarray(nw.reshape(LCN, 128).T)
    neginf = np.full((128, 1), -3.0e38, np.float32)
    epsc = np.array([[EPS_LN, EPS_RMS]], np.float32)

    ims = []
    for r in range(NCORES):
        g0, g1 = 2 * r, 2 * r + 1
        cs = np.zeros((128, 4), np.float32)
        for li, g in enumerate((g0, g1)):
            cvec = np.ones(128, np.float32)
            svec = np.zeros(128, np.float32)
            cvec[64 + 2 * np.arange(32)] = freqs[g, :, 0]
            cvec[64 + 2 * np.arange(32) + 1] = freqs[g, :, 0]
            svec[64 + 2 * np.arange(32)] = freqs[g, :, 1]
            svec[64 + 2 * np.arange(32) + 1] = freqs[g, :, 1]
            cs[:, 2 * li] = cvec
            cs[:, 2 * li + 1] = svec
        wkvb_sl = np.ascontiguousarray(np.concatenate(
            [wkv_bt[:, g0 * 128:(g0 + 1) * 128], wkv_bt[:, g1 * 128:(g1 + 1) * 128],
             wkv_bt[:, 2048 + g0 * 128:2048 + (g0 + 1) * 128],
             wkv_bt[:, 2048 + g1 * 128:2048 + (g1 + 1) * 128]], axis=1))
        wq_sl = np.ascontiguousarray(wqt[:, g0 * 128:(g1 + 1) * 128])
        wqb_sl = np.ascontiguousarray(
            np.stack([wqb[g0 * 128:(g0 + 1) * 128], wqb[g1 * 128:(g1 + 1) * 128]],
                     axis=1))
        ims.append(dict(
            x_in=np.ascontiguousarray(xT[:, r * TOK:(r + 1) * TOK]),
            w1t=w1t, w3t=w3t, w2t=w2t, wqt_s=wq_sl, wkv_at=wkv_at,
            wkvb_s=wkvb_sl, owt=owt,
            c_id=ident, c_h128=h128, c_h64=h64bd, c_idf=identf, c_ms=ms,
            c_mask=mask, c_cs=cs, c_ones=ones, c_onesrow=onesrow,
            c_wqb=wqb_sl, c_outb=outb, c_nw=nwt, c_neginf=neginf, c_eps=epsc,
        ))
    return ims


def kernel(**inputs):
    key = _STAGE
    if key not in _CACHE:
        _CACHE[key] = build_program(key)
    nc = _CACHE[key]
    ims = _host_prepare(inputs)
    res = run_bass_kernel_spmd(nc, ims, core_ids=list(range(NCORES)))
    out = np.concatenate([np.asarray(res.results[r]["out_x"]) for r in range(NCORES)],
                         axis=1)
    return np.ascontiguousarray(out.T).reshape(1, S, D).astype(np.float32)


# version 06 (final)
# speedup vs baseline: 1.7871x; 1.0074x over previous
import math
import os
import numpy as np
import ml_dtypes
import concourse.bass as bass
import concourse.mybir as mybir
from concourse.bass_utils import run_bass_kernel_spmd
from contextlib import ExitStack

os.environ.setdefault("TRN_ALLOW_ICI_ALL_TO_ALL", "1")

F32 = mybir.dt.float32
BF16 = mybir.dt.bfloat16
AF = mybir.ActivationFunctionType
ALU = mybir.AluOpType
BF16NP = ml_dtypes.bfloat16

S, D, NCORES = 8192, 2048, 8
TOK = S // NCORES
TT = 512
DCN = 16
HID, HCN = 8192, 64
NH, DH, DL, LCN = 16, 128, 512, 4
SCALE_QK = 1.0 / math.sqrt(128.0)
RSQN = 1.0 / math.sqrt(8192.0)
EPS_LN, EPS_RMS = 1e-5, 1e-6
MASKVAL = -1.0e30

ENGS = ("tensor", "vector", "scalar", "sync", "gpsimd")

_STAGE = "full"
_CACHE = {}


def _hadamard(n):
    h = np.array([[1.0]], np.float32)
    while h.shape[0] < n:
        h = np.block([[h, h], [h, -h]])
    return h.astype(np.float32)


class Plan:
    SELF_SEM = {"vector": "sV", "scalar": "sA"}

    def __init__(self, nc):
        self.nc = nc
        self.ops = {e: [] for e in ENGS}
        self.sems = {}
        self.cnt = {}
        self.last_w = {e: {} for e in ENGS}
        self.last_r = {e: {} for e in ENGS}
        self.guar = {e: {} for e in ENGS}

    def mksem(self, st, name):
        h = st.enter_context(self.nc.semaphore(name))
        self.sems[name] = h
        self.cnt[name] = 0

    def wait(self, eng, sem, val):
        if val is None or val <= 0:
            return
        if self.guar[eng].get(sem, 0) >= val:
            return
        self.guar[eng][sem] = val
        self.ops[eng].append(("wait", self.sems[sem], val))

    @staticmethod
    def _apname(v):
        t = getattr(v, "tensor", None)
        if t is not None:
            return getattr(t, "name", None)
        if hasattr(v, "name") and hasattr(v, "dtype"):
            return v.name
        return None

    def _classify(self, args, kw):
        writes, reads = [], []
        for i, a in enumerate(args):
            nm = self._apname(a)
            if nm is not None:
                (writes if i == 0 else reads).append(nm)
        for k, v in kw.items():
            nm = self._apname(v)
            if nm is not None:
                (writes if k in ("out", "accum_out") else reads).append(nm)
        return writes, reads

    def op(self, eng, method, *args, inc=None, incv=1, **kw):
        if eng in self.SELF_SEM and method != "dma_start":
            ssem = self.SELF_SEM[eng]
            writes, reads = self._classify(args, kw)
            need = 0
            for nm in writes + reads:
                need = max(need, self.last_w[eng].get(nm, 0))
            for nm in writes:
                need = max(need, self.last_r[eng].get(nm, 0))
            self.wait(eng, ssem, need)
            if inc is None:
                inc = ssem
            assert inc == ssem, (eng, inc)
            self.cnt[inc] += incv
            tot = self.cnt[inc]
            for nm in writes:
                self.last_w[eng][nm] = tot
            for nm in reads:
                self.last_r[eng][nm] = tot
            self.ops[eng].append(("op", method, args, kw, self.sems[inc], incv))
            return tot
        sem = self.sems[inc] if inc is not None else None
        tot = None
        if inc is not None:
            self.cnt[inc] += incv
            tot = self.cnt[inc]
        self.ops[eng].append(("op", method, args, kw, sem, incv))
        return tot

    def dma(self, eng, out, in_, inc, **kw):
        return self.op(eng, "dma_start", out=out, in_=in_, inc=inc, incv=16, **kw)

    def barrier(self):
        snap = dict(self.cnt)
        for e in ENGS:
            for s, v in snap.items():
                self.wait(e, s, v)

    def run(self, eng_obj, eng_name):
        for rec in self.ops[eng_name]:
            if rec[0] == "wait":
                eng_obj.wait_ge(rec[1], rec[2])
            else:
                _, method, args, kw, sem, iv = rec
                inst = getattr(eng_obj, method)(*args, **kw)
                if sem is not None:
                    inst.then_inc(sem, iv)


class SbufMap:
    def __init__(self, nc, st, total_bytes):
        self.nc = nc
        self.total = total_bytes
        self.base = (nc.sbuf_base + 31) // 32 * 32
        assert self.base + total_bytes <= nc.sbuf_top, (self.base, nc.sbuf_top)

    def at(self, offset, name, shape, dtype):
        per_part = int(np.prod(shape[1:])) * mybir.dt.size(dtype)
        assert offset + per_part <= self.total, (name, offset, per_part)
        return self.nc.alloc_sbuf_tensor_at(name, shape, dtype,
                                            offset=self.base + offset)


def build_program(stage):
    nc = bass.Bass("TRN2", target_bir_lowering=False)

    def din(name, shape, dt=F32):
        return nc.dram_tensor(name, shape, dt, kind="ExternalInput")

    x_in = din("x_in", [D, TOK])
    w1t = din("w1t", [HCN, 128, DCN, 128])
    w3t = din("w3t", [HCN, 128, DCN, 128])
    w2t = din("w2t", [HID, D])
    wqt_s = din("wqt_s", [D, 256])
    wkv_at = din("wkv_at", [LCN, 128, DCN, 128])
    wkvb_s = din("wkvb_s", [DL, 512])
    owt = din("owt", [D, D])
    c_id_d = din("c_id", [128, 128], BF16)
    c_h128_d = din("c_h128", [128, 128], BF16)
    c_h64_d = din("c_h64", [128, 128], BF16)
    c_idf_d = din("c_idf", [128, 128])
    c_ms_d = din("c_ms", [128, 128])
    c_mask_d = din("c_mask", [128, 4 * TT])
    c_cs_d = din("c_cs", [128, 4])
    c_ones_d = din("c_ones", [128, 1])
    c_onesrow_d = din("c_onesrow", [1, 128])
    c_wqb_d = din("c_wqb", [128, 2])
    c_outb_d = din("c_outb", [128, DCN])
    c_nw_d = din("c_nw", [128, LCN])
    c_neginf_d = din("c_neginf", [128, 1])
    c_eps_d = din("c_eps", [1, 2])

    w2a_d = nc.dram_tensor("w2a", [HID, D], BF16)
    w2eff_d = nc.dram_tensor("w2eff", [DCN, 128, HCN, 128], BF16)
    xspill_d = nc.dram_tensor("xspill", [D, TOK], F32)
    ghl_c = nc.dram_tensor("ghl_c", [D + DL, TOK], BF16)
    ghl_g = nc.dram_tensor("ghl_g", [NCORES * (D + DL), TOK], BF16, addr_space="Shared")
    oc_c = nc.dram_tensor("oc_c", [D, TOK], BF16)
    oa_r = nc.dram_tensor("oa_r", [D, TOK], BF16)
    out_x = nc.dram_tensor("out_x", [D, TOK], F32, kind="ExternalOutput")
    if stage == "attn":
        d_ghl = nc.dram_tensor("d_ghl", [NCORES * (D + DL), TOK], BF16,
                               kind="ExternalOutput")
        d_q0 = nc.dram_tensor("d_q0", [128, S], BF16, kind="ExternalOutput")
        d_K = nc.dram_tensor("d_K", [128, S], BF16, kind="ExternalOutput")
        d_V = nc.dram_tensor("d_V", [128, S], BF16, kind="ExternalOutput")
        d_oc = nc.dram_tensor("d_oc", [D, TOK], BF16, kind="ExternalOutput")
        d_oa = nc.dram_tensor("d_oa", [D, TOK], BF16, kind="ExternalOutput")

    st = ExitStack()
    with st:
        KB = 1024
        sm = SbufMap(nc, st, 206 * KB)
        o = 0

        def alloc(name, shape, dtype):
            nonlocal o
            h = sm.at(o, name, shape, dtype)
            o += (int(np.prod(shape[1:])) * mybir.dt.size(dtype) + 31) // 32 * 32
            return h

        c_id = alloc("s_id", [128, 128], BF16)
        c_h128 = alloc("s_h128", [128, 128], BF16)
        c_h64 = alloc("s_h64", [128, 128], BF16)
        c_idf = alloc("s_idf", [128, 128], F32)
        c_ms = alloc("s_ms", [128, 128], F32)
        c_mask = alloc("s_mask", [128, 4 * TT], F32)
        c_cs = alloc("s_cs", [128, 4], F32)
        c_ones = alloc("s_ones", [128, 1], F32)
        c_onesrow = alloc("s_onesrow", [1, 128], F32)
        c_wqb = alloc("s_wqb", [128, 2], F32)
        c_outb = alloc("s_outb", [128, DCN], F32)
        c_nw = alloc("s_nw", [128, LCN], F32)
        c_neginf = alloc("s_neginf", [128, 1], F32)
        c_eps = alloc("s_eps", [1, 2], F32)
        r_a = alloc("s_ra", [1, TOK], F32)
        r_b = alloc("s_rb", [1, TOK], F32)
        rB512 = alloc("s_rB512", [128, TT], F32)
        sc512 = [alloc(f"s_sc{i}", [128, TT], F32) for i in range(2)]
        stg = [alloc(f"s_stg{i}", [128, TT], BF16) for i in range(2)]
        R_sb = [alloc(f"s_R{i}", [128, 128], F32) for i in range(2)]
        ost = [alloc(f"s_ost{i}", [128, 256], BF16) for i in range(2)]
        m_t = [[alloc(f"s_m{sj}{pp}", [128, 1], F32) for pp in range(2)] for sj in range(2)]
        l_t = [alloc(f"s_l{sj}", [128, 1], F32) for sj in range(2)]
        negm_t = [alloc(f"s_negm{sj}", [128, 1], F32) for sj in range(2)]
        rl_t = [alloc(f"s_rl{sj}", [128, 1], F32) for sj in range(2)]
        assert o <= 30 * KB, o

        RX, R1, R2 = 30 * KB, 94 * KB, 158 * KB
        x_sb = sm.at(RX, "v_x", [128, DCN * TOK], F32)
        h2b = [sm.at(RX + i * 16 * KB, f"v_h2b{i}", [128, DCN * TT], BF16) for i in range(2)]
        s_sc = [sm.at(RX + i * 32 * KB, f"v_s{i}", [128, 16 * TT], F32) for i in range(2)]
        x2_sb = sm.at(RX, "v_x2", [128, DCN * TOK], F32)

        w2c_sb = [sm.at(R1 + i * 4 * KB, f"v_w2c{i}", [128, D], BF16) for i in range(2)]
        w2a_sb = [sm.at(R1 + 8 * KB + i * 4 * KB, f"v_w2a{i}", [128, D], BF16) for i in range(2)]
        wc_sb = [sm.at(R1 + 16 * KB + i * 4 * KB, f"v_wc{i}", [128, D], BF16) for i in range(2)]
        weff_sb = [sm.at(R1 + 24 * KB + i * 4 * KB, f"v_weff{i}", [128, D], BF16) for i in range(2)]
        g_sb = sm.at(R1, "v_g", [128, 16 * TOK], BF16)
        w2eb = [sm.at(R1 + 32 * KB + i * 4 * KB, f"v_w2eb{i}", [128, 16 * 128], BF16) for i in range(2)]
        qT = [sm.at(R1 + i * 16 * KB, f"v_qT{i}", [128, S], BF16) for i in range(2)]
        K_sb = sm.at(R1 + 32 * KB, "v_K", [128, S], BF16)
        V_sb = sm.at(R1 + 48 * KB, "v_V", [128, S], BF16)
        owt_sb = sm.at(R1, "v_owt", [128, DCN * TOK], BF16)
        oT_sb = sm.at(R1 + 32 * KB, "v_oT", [128, DCN * TOK], BF16)

        hT_sb = sm.at(R2, "v_hT", [128, DCN * TOK], BF16)
        w1b = [sm.at(R2 + 32 * KB + i * 4 * KB, f"v_w1b{i}", [128, DCN * 128], BF16) for i in range(2)]
        w3b = [sm.at(R2 + 40 * KB + i * 4 * KB, f"v_w3b{i}", [128, DCN * 128], BF16) for i in range(2)]
        watb = [sm.at(R2 + 32 * KB + i * 4 * KB, f"v_watb{i}", [128, DCN * 128], BF16) for i in range(2)]
        wql_sb = sm.at(R2, "v_wql", [128, DCN * 256], BF16)
        pr = [sm.at(R2 + i * 16 * KB, f"v_pr{i}", [128, 16 * TT], BF16) for i in range(2)]
        ln_db = [sm.at(R2 + 32 * KB + i * 4 * KB, f"v_lndb{i}", [128, LCN * TT], BF16) for i in range(2)]
        wkvb_sb = sm.at(R2 + 40 * KB, "v_wkvb", [128, LCN * 512], BF16)
        aT_sb = [sm.at(R2 + 44 * KB + i * KB, f"v_aT{i}", [128, TT], BF16) for i in range(2)]

        psf = [st.enter_context(nc.psum_tensor(f"psf{i}", [128, 512], F32)) for i in range(6)]
        pst = [st.enter_context(nc.psum_tensor(f"pst{i}", [128, 1024], BF16)) for i in range(2)]

        P = Plan(nc)
        for s in ("sS", "sG", "sSC", "sT", "sV", "sA", "sC",
                  "sS0", "sS1", "sG0", "sG1", "sSC0", "sSC1", "sW0", "sW1", "sP"):
            P.mksem(st, s)

        for dst, src in (
            (c_id, c_id_d), (c_h128, c_h128_d), (c_h64, c_h64_d), (c_idf, c_idf_d),
            (c_ms, c_ms_d), (c_mask, c_mask_d), (c_cs, c_cs_d), (c_ones, c_ones_d),
            (c_onesrow, c_onesrow_d), (c_wqb, c_wqb_d), (c_outb, c_outb_d),
            (c_nw, c_nw_d), (c_neginf, c_neginf_d), (c_eps, c_eps_d),
        ):
            P.dma("sync", dst[:, :], src[:, :], "sS")
        P.dma("sync", x_sb.ap().rearrange("p (dc t) -> p dc t", t=TOK),
              x_in.ap().rearrange("(dc p) t -> p dc t", p=128), "sS")
        P.barrier()

        pbank = {i: ("sV", 0) for i in range(6)}

        def load_w2c(c, wait_sT):
            par = c & 1
            P.wait("gpsimd", "sT", wait_sT)
            return P.dma("gpsimd", w2c_sb[par][:, :],
                         w2t[c * 128:(c + 1) * 128, :], f"sG{par}")

        w2c_ld = {0: load_w2c(0, 0), 1: load_w2c(1, 0)}
        w2c_done = {}
        w2a_free = [0, 0]
        rot = 0
        for c in range(HCN):
            par = c & 1
            P.wait("tensor", f"sG{par}", w2c_ld[c])
            banks = [(rot + s4) % 6 for s4 in range(4)]
            t_done = None
            for s4, bk in enumerate(banks):
                sname, v = pbank[bk]
                P.wait("tensor", sname, v)
                t_done = P.op("tensor", "matmul", psf[bk][:, :], c_h128[:, :],
                              w2c_sb[par][:, s4 * 512:(s4 + 1) * 512],
                              start=True, stop=True,
                              inc=("sT" if s4 == 3 else None))
            w2c_done[c] = t_done
            if c + 2 < HCN:
                w2c_ld[c + 2] = load_w2c(c + 2, t_done)
            P.wait("vector", "sT", t_done)
            P.wait("vector", f"sW{par}", w2a_free[par])
            for s4, bk in enumerate(banks):
                v_done = P.op("vector", "tensor_copy",
                              w2a_sb[par][:, s4 * 512:(s4 + 1) * 512], psf[bk][:, :],
                              inc="sV")
                pbank[bk] = ("sV", v_done)
            rot = (rot + 4) % 6
            P.wait("sync", "sV", v_done)
            w2a_free[par] = P.dma("sync", w2a_d[c * 128:(c + 1) * 128, :],
                                  w2a_sb[par][:, :], f"sW{par}")
        P.barrier()

        w2a_r = w2a_d.ap().rearrange("(c pz) d -> pz c d", pz=128)
        w2e_r = w2eff_d.ap().rearrange("dc hrow hc m -> hrow hc dc m")
        pbank = {i: ("sV", 0) for i in range(6)}

        def load_wc(pp, wait_sT):
            par = pp & 1
            P.wait("sync", "sT", wait_sT)
            P.dma("sync", wc_sb[par][0:64, :], w2a_r[2 * pp], f"sS{par}")
            return P.dma("sync", wc_sb[par][64:128, :], w2a_r[2 * pp + 1],
                         f"sS{par}")

        wc_ld = {0: load_wc(0, 0), 1: load_wc(1, 0)}
        weff_free = [0, 0]
        rot = 0
        for pp in range(64):
            par = pp & 1
            P.wait("tensor", f"sS{par}", wc_ld[pp])
            banks = [(rot + s4) % 6 for s4 in range(4)]
            t_done = None
            for s4, bk in enumerate(banks):
                sname, v = pbank[bk]
                P.wait("tensor", sname, v)
                t_done = P.op("tensor", "matmul", psf[bk][:, :], c_h64[:, :],
                              wc_sb[par][:, s4 * 512:(s4 + 1) * 512],
                              start=True, stop=True,
                              inc=("sT" if s4 == 3 else None))
            if pp + 2 < 64:
                wc_ld[pp + 2] = load_wc(pp + 2, t_done)
            P.wait("vector", "sT", t_done)
            P.wait("vector", f"sW{par}", weff_free[par])
            for s4, bk in enumerate(banks):
                v_done = P.op("vector", "tensor_scalar",
                              out=weff_sb[par][:, s4 * 512:(s4 + 1) * 512],
                              in0=psf[bk][:, :], scalar1=RSQN, scalar2=None,
                              op0=ALU.mult, inc="sV")
                pbank[bk] = ("sV", v_done)
            rot = (rot + 4) % 6
            P.wait("sync", "sV", v_done)
            P.dma("sync", w2e_r[2 * pp],
                  weff_sb[par][0:64, :].rearrange("c (dc m) -> c dc m", m=128),
                  f"sW{par}")
            weff_free[par] = P.dma(
                "sync", w2e_r[2 * pp + 1],
                weff_sb[par][64:128, :].rearrange("c (dc m) -> c dc m", m=128),
                f"sW{par}")
        P.barrier()

        for hl in range(2):
            P.op("vector", "tensor_scalar", out=rB512[:, 0:128], in0=c_idf[:, :],
                 scalar1=c_cs[:, 2 * hl:2 * hl + 1], scalar2=None, op0=ALU.mult)
            P.op("vector", "scalar_tensor_tensor", out=R_sb[hl][:, :], in0=c_ms[:, :],
                 scalar=c_cs[:, 2 * hl + 1:2 * hl + 2], in1=rB512[:, 0:128],
                 op0=ALU.mult, op1=ALU.add, inc="sV")
        P.barrier()

        def emit_ln(src_sb):
            sq_par = 0
            consumed = [0, 0]
            for tt in range(2):
                for dc in range(DCN):
                    xs = src_sb[:, dc * TOK + tt * TT: dc * TOK + (tt + 1) * TT]
                    P.wait("scalar", "sT", consumed[sq_par])
                    a_done = P.op("scalar", "activation", out=sc512[sq_par][:, :],
                                  in_=xs, func=AF.Square, inc="sA")
                    P.wait("tensor", "sA", a_done)
                    P.op("tensor", "matmul", psf[2 + tt][0:1, :], c_ones[:, :],
                         sc512[sq_par][:, :],
                         start=(dc == 0), stop=(dc == DCN - 1))
                    t_done = P.op("tensor", "matmul", psf[tt][0:1, :], c_ones[:, :], xs,
                                  start=(dc == 0), stop=(dc == DCN - 1), inc="sT")
                    consumed[sq_par] = t_done
                    sq_par ^= 1
            P.wait("vector", "sT", P.cnt["sT"])
            for tt in range(2):
                rsl = slice(tt * TT, (tt + 1) * TT)
                P.op("vector", "tensor_scalar", out=r_a[0:1, rsl], in0=psf[tt][0:1, :],
                     scalar1=1.0 / D, scalar2=None, op0=ALU.mult)
                P.op("vector", "tensor_scalar", out=r_b[0:1, rsl], in0=psf[2 + tt][0:1, :],
                     scalar1=1.0 / D, scalar2=None, op0=ALU.mult)
                P.op("vector", "tensor_mul", psf[4][0:1, :], r_a[0:1, rsl], r_a[0:1, rsl])
                v_done = P.op("vector", "tensor_sub", r_b[0:1, rsl], r_b[0:1, rsl],
                              psf[4][0:1, :], inc="sV")
                P.wait("scalar", "sV", v_done)
                a_done = P.op("scalar", "activation", out=psf[5][0:1, :], in_=r_b[0:1, rsl],
                              func=AF.Sqrt, bias=c_eps[0:1, 0:1], scale=1.0, inc="sA")
                P.wait("vector", "sA", a_done)
                P.op("vector", "reciprocal", r_b[0:1, rsl], psf[5][0:1, :], inc="sV")
            P.wait("tensor", "sV", P.cnt["sV"])
            bc_done = None
            for tt in range(2):
                rsl = slice(tt * TT, (tt + 1) * TT)
                P.op("tensor", "matmul", psf[2 + tt][:, :], c_onesrow[:, :], r_a[0:1, rsl],
                     start=True, stop=True)
                bc_done = P.op("tensor", "matmul", psf[4 + tt][:, :], c_onesrow[:, :],
                               r_b[0:1, rsl], start=True, stop=True,
                               inc=("sT" if tt == 1 else None))
            P.wait("vector", "sT", bc_done)
            par = 0
            for dc in range(DCN):
                for tt in range(2):
                    xs = src_sb[:, dc * TOK + tt * TT: dc * TOK + (tt + 1) * TT]
                    hs = hT_sb[:, dc * TOK + tt * TT: dc * TOK + (tt + 1) * TT]
                    P.op("vector", "tensor_sub", sc512[par][:, :], xs, psf[2 + tt][:, :])
                    P.op("vector", "tensor_mul", hs, sc512[par][:, :], psf[4 + tt][:, :],
                         inc="sV")
                    par ^= 1
            P.barrier()

        def emit_ffn(res_sb):
            w_prev = [0, 0]
            e_prev = [0, 0]
            gbank = {i: ("sV", 0) for i in range(6)}
            for qtr in range(4):

                def load_w13(i, wait_sT):
                    hc = qtr * 16 + i
                    par = i & 1
                    P.wait("gpsimd", "sT", wait_sT)
                    P.dma("gpsimd",
                          w1b[par].ap().rearrange("p (dc m) -> p dc m", m=128),
                          w1t[hc], f"sG{par}")
                    return P.dma(
                        "gpsimd",
                        w3b[par].ap().rearrange("p (dc m) -> p dc m", m=128),
                        w3t[hc], f"sG{par}")

                w_ld = {0: load_w13(0, w_prev[0]), 1: load_w13(1, w_prev[1])}
                for i in range(16):
                    par = i & 1
                    b = [(4 * i) % 6, (4 * i + 1) % 6, (4 * i + 2) % 6, (4 * i + 3) % 6]
                    P.wait("tensor", f"sG{par}", w_ld[i])
                    for bb in b:
                        sname, v = gbank[bb]
                        P.wait("tensor", sname, v)
                    for k in range(DCN):
                        lhs = w1b[par][:, k * 128:(k + 1) * 128]
                        P.op("tensor", "matmul", psf[b[0]][:, :], lhs,
                             hT_sb[:, k * TOK:k * TOK + TT],
                             start=(k == 0), stop=(k == DCN - 1))
                        P.op("tensor", "matmul", psf[b[1]][:, :], lhs,
                             hT_sb[:, k * TOK + TT:k * TOK + 2 * TT],
                             start=(k == 0), stop=(k == DCN - 1))
                    t_done = None
                    for k in range(DCN):
                        lhs = w3b[par][:, k * 128:(k + 1) * 128]
                        P.op("tensor", "matmul", psf[b[2]][:, :], lhs,
                             hT_sb[:, k * TOK:k * TOK + TT],
                             start=(k == 0), stop=(k == DCN - 1))
                        t_done = P.op("tensor", "matmul", psf[b[3]][:, :], lhs,
                                      hT_sb[:, k * TOK + TT:k * TOK + 2 * TT],
                                      start=(k == 0), stop=(k == DCN - 1),
                                      inc=("sT" if k == DCN - 1 else None))
                    w_prev[par] = t_done
                    if i + 2 < 16:
                        w_ld[i + 2] = load_w13(i + 2, t_done)
                    P.wait("scalar", "sT", t_done)
                    P.wait("vector", "sT", t_done)
                    for tt in range(2):
                        spar = (2 * i + tt) & 1
                        a_done = P.op("scalar", "activation", out=sc512[spar][:, :],
                                      in_=psf[b[tt]][:, :], func=AF.Silu, inc="sA")
                        P.wait("vector", "sA", a_done)
                        v_done = P.op("vector", "tensor_mul",
                                      g_sb[:, i * TOK + tt * TT:i * TOK + (tt + 1) * TT],
                                      sc512[spar][:, :], psf[b[2 + tt]][:, :], inc="sV")
                        gbank[b[tt]] = ("sA", a_done)
                        gbank[b[2 + tt]] = ("sV", v_done)
                        P.wait("scalar", "sV", v_done)
                g_ready = P.cnt["sV"]

                def load_w2e(dc, wait_sT):
                    par = dc & 1
                    P.wait("sync", "sT", wait_sT)
                    return P.dma(
                        "sync", w2eb[par].ap().rearrange("p (hc m) -> p hc m", m=128),
                        w2eff_d[dc, :, qtr * 16:(qtr + 1) * 16, :], f"sS{par}")

                e_ld = {0: load_w2e(0, e_prev[0]), 1: load_w2e(1, e_prev[1])}
                mb_free = [gbank[i][1] for i in range(4)]
                P.wait("tensor", "sV", g_ready)
                for dc in range(DCN):
                    par = dc & 1
                    P.wait("tensor", f"sS{par}", e_ld[dc])
                    bk0, bk1 = (2 * dc) % 4, (2 * dc) % 4 + 1
                    P.wait("tensor", "sV", mb_free[bk0])
                    P.wait("tensor", "sV", mb_free[bk1])
                    t_done = None
                    for hcl in range(16):
                        lhs = w2eb[par][:, hcl * 128:(hcl + 1) * 128]
                        P.op("tensor", "matmul", psf[bk0][:, :], lhs,
                             g_sb[:, hcl * TOK:hcl * TOK + TT],
                             start=(hcl == 0), stop=(hcl == 15))
                        t_done = P.op("tensor", "matmul", psf[bk1][:, :], lhs,
                                      g_sb[:, hcl * TOK + TT:hcl * TOK + 2 * TT],
                                      start=(hcl == 0), stop=(hcl == 15),
                                      inc=("sT" if hcl == 15 else None))
                    P.wait("vector", "sT", t_done)
                    for tt, bk in ((0, bk0), (1, bk1)):
                        mb_free[bk] = P.op(
                            "vector", "scalar_tensor_tensor",
                            out=res_sb[:, dc * TOK + tt * TT:dc * TOK + (tt + 1) * TT],
                            in0=psf[bk][:, :], scalar=0.5,
                            in1=res_sb[:, dc * TOK + tt * TT:dc * TOK + (tt + 1) * TT],
                            op0=ALU.mult, op1=ALU.add, inc="sV")
                    e_prev[par] = t_done
                    if dc + 2 < DCN:
                        e_ld[dc + 2] = load_w2e(dc + 2, t_done)
                    gbank[bk0] = ("sV", mb_free[bk0])
                    gbank[bk1] = ("sV", mb_free[bk1])
            P.barrier()

        emit_ln(x_sb)
        emit_ffn(x_sb)
        if stage == "ffn1":
            fin = P.dma("sync", out_x.ap().rearrange("(dc p) t -> p dc t", p=128),
                        x_sb.ap().rearrange("p (dc t) -> p dc t", t=TOK), "sS")
            P.wait("sync", "sS", fin)
            _emit_block(nc, P)
            return nc

        emit_ln(x_sb)
        P.dma("scalar", ghl_c[0:D, :].rearrange("(dc p) t -> p dc t", p=128),
              hT_sb.ap().rearrange("p (dc t) -> p dc t", t=TOK), "sSC")
        st_free = [0, 0]
        for tt in range(2):
            def load_wat(lc, wait_sT):
                par = lc & 1
                P.wait("gpsimd", "sT", wait_sT)
                return P.dma(
                    "gpsimd", watb[par].ap().rearrange("p (dc m) -> p dc m", m=128),
                    wkv_at[lc], f"sG{par}")

            at_ld = {0: load_wat(0, 0), 1: load_wat(1, 0)}
            for lc in range(LCN):
                par = lc & 1
                P.wait("tensor", f"sG{par}", at_ld[lc])
                t_done = None
                for k in range(DCN):
                    t_done = P.op("tensor", "matmul", psf[lc][:, :],
                                  watb[par][:, k * 128:(k + 1) * 128],
                                  hT_sb[:, k * TOK + tt * TT:k * TOK + (tt + 1) * TT],
                                  start=(k == 0), stop=(k == DCN - 1),
                                  inc=("sT" if k == DCN - 1 else None))
                if lc + 2 < LCN:
                    at_ld[lc + 2] = load_wat(lc + 2, t_done)
            lat_done = P.cnt["sT"]
            P.wait("scalar", "sT", lat_done)
            sq_done = {}
            for lc in range(LCN):
                spar = lc & 1
                P.wait("scalar", "sT", sq_done.get(lc - 2, 0))
                a_done = P.op("scalar", "activation", out=sc512[spar][:, :],
                              in_=psf[lc][:, :], func=AF.Square, inc="sA")
                P.wait("tensor", "sA", a_done)
                sq_done[lc] = P.op("tensor", "matmul", psf[4][0:1, :], c_ones[:, :],
                                   sc512[spar][:, :], start=(lc == 0),
                                   stop=(lc == LCN - 1), inc="sT")
            P.wait("vector", "sT", P.cnt["sT"])
            v_done = P.op("vector", "tensor_scalar", out=r_b[0:1, 0:TT],
                          in0=psf[4][0:1, :], scalar1=1.0 / DL, scalar2=None,
                          op0=ALU.mult, inc="sV")
            P.wait("scalar", "sV", v_done)
            a_done = P.op("scalar", "activation", out=psf[5][0:1, :], in_=r_b[0:1, 0:TT],
                          func=AF.Sqrt, bias=c_eps[0:1, 1:2], scale=1.0, inc="sA")
            P.wait("vector", "sA", a_done)
            v_done = P.op("vector", "reciprocal", r_b[0:1, 0:TT], psf[5][0:1, :], inc="sV")
            P.wait("tensor", "sV", v_done)
            t_done = P.op("tensor", "matmul", psf[5][:, :], c_onesrow[:, :],
                          r_b[0:1, 0:TT], start=True, stop=True, inc="sT")
            P.wait("vector", "sT", t_done)
            P.op("vector", "tensor_copy", rB512[:, :], psf[5][:, :])
            for lc in range(LCN):
                spar = lc & 1
                P.wait("vector", f"sSC{spar}", st_free[spar])
                v_done = P.op("vector", "scalar_tensor_tensor", out=stg[spar][:, :],
                              in0=psf[lc][:, :], scalar=c_nw[:, lc:lc + 1],
                              in1=rB512[:, :], op0=ALU.mult, op1=ALU.mult, inc="sV")
                P.wait("scalar", "sV", v_done)
                st_free[spar] = P.dma(
                    "scalar",
                    ghl_c[D + lc * 128:D + (lc + 1) * 128, tt * TT:(tt + 1) * TT],
                    stg[spar][:, :], f"sSC{spar}")
            P.barrier()
        P.dma("scalar", xspill_d.ap().rearrange("(dc p) t -> p dc t", p=128),
              x_sb.ap().rearrange("p (dc t) -> p dc t", t=TOK), "sSC")
        P.barrier()
        P.op("gpsimd", "collective_compute", "AllGather", ALU.bypass,
             replica_groups=[list(range(NCORES))],
             ins=[ghl_c.ap().opt()], outs=[ghl_g.ap().opt()], inc="sC")
        P.barrier()

        ghl_r = ghl_g.ap().rearrange("(rk f20 p) t -> rk p f20 t", rk=8, f20=20, p=128)

        P.dma("gpsimd", wql_sb.ap().rearrange("p (dc m) -> p dc m", m=256),
              wqt_s.ap().rearrange("(dc p) m -> p dc m", p=128), "sG")
        P.dma("gpsimd", wkvb_sb.ap().rearrange("p (lc f) -> p lc f", f=512),
              wkvb_s.ap().rearrange("(lc p) f -> p lc f", p=128), "sG")
        P.barrier()

        def load_h2(t16, wait_sT):
            par = t16 & 1
            rk, slot = t16 // 2, t16 % 2
            P.wait("sync", "sT", wait_sT)
            return P.dma(
                "sync", h2b[par].ap().rearrange("p (dc t) -> p dc t", t=TT),
                ghl_r[rk, :, 0:16, slot * TT:(slot + 1) * TT], f"sS{par}")

        h2_ld = {0: load_h2(0, 0), 1: load_h2(1, 0)}
        sc_free = [0, 0]
        bfree = {}
        for t16 in range(16):
            par = t16 & 1
            P.wait("tensor", f"sS{par}", h2_ld[t16])
            t_last = None
            for hl in range(2):
                bq = (2 * t16 + hl) % 3
                bq2 = 3 + (2 * t16 + hl) % 3
                sname, v = bfree.get(bq, ("sV", 0))
                P.wait("tensor", sname, v)
                t_done = None
                for k in range(DCN):
                    t_done = P.op("tensor", "matmul", psf[bq][:, :],
                                  wql_sb[:, k * 256 + hl * 128:k * 256 + (hl + 1) * 128],
                                  h2b[par][:, k * TT:(k + 1) * TT],
                                  start=(k == 0), stop=(k == DCN - 1),
                                  inc=("sT" if k == DCN - 1 else None))
                spar = (2 * t16 + hl) & 1
                P.wait("vector", "sT", t_done)
                P.wait("vector", "sT", sc_free[spar])
                v_done = P.op("vector", "tensor_scalar", out=sc512[spar][:, :],
                              in0=psf[bq][:, :], scalar1=c_wqb[:, hl:hl + 1],
                              scalar2=None, op0=ALU.add, inc="sV")
                bfree[bq] = ("sV", v_done)
                sname, v = bfree.get(bq2, ("sV", 0))
                P.wait("tensor", sname, v)
                P.wait("tensor", "sV", v_done)
                t2 = P.op("tensor", "matmul", psf[bq2][:, :], R_sb[hl][:, :],
                          sc512[spar][:, :], start=True, stop=True, inc="sT")
                sc_free[spar] = t2
                t_last = t2
                P.wait("vector", "sT", t2)
                v2 = P.op("vector", "tensor_copy",
                          qT[hl][:, t16 * TT:(t16 + 1) * TT], psf[bq2][:, :], inc="sV")
                bfree[bq2] = ("sV", v2)
            if t16 + 2 < 16:
                h2_ld[t16 + 2] = load_h2(t16 + 2, t_last)
        P.barrier()

        if stage == "attn":
            P.dma("sync", d_ghl.ap(), ghl_g.ap(), "sS")
            P.dma("sync", d_q0[:, :], qT[0][:, :], "sS")
            P.barrier()

        for hl in range(2):
            def load_ln(t16, wait_sT):
                par = t16 & 1
                rk, slot = t16 // 2, t16 % 2
                P.wait("sync", "sT", wait_sT)
                return P.dma(
                    "sync", ln_db[par].ap().rearrange("p (lc t) -> p lc t", t=TT),
                    ghl_r[rk, :, 16:20, slot * TT:(slot + 1) * TT], f"sS{par}")

            ln_ld = {0: load_ln(0, 0), 1: load_ln(1, 0)}
            scp = 0
            vtp = 0
            bfree = {}
            tfree = {0: ("sV", 0), 1: ("sV", 0)}
            for t16 in range(16):
                par = t16 & 1
                P.wait("tensor", f"sS{par}", ln_ld[t16])
                bk, bv, br = (3 * t16) % 6, (3 * t16 + 1) % 6, (3 * t16 + 2) % 6
                for bb in (bk, bv, br):
                    sname, v = bfree.get(bb, ("sV", 0))
                    P.wait("tensor", sname, v)
                for lc in range(LCN):
                    P.op("tensor", "matmul", psf[bk][:, :],
                         wkvb_sb[:, lc * 512 + hl * 128:lc * 512 + (hl + 1) * 128],
                         ln_db[par][:, lc * TT:(lc + 1) * TT],
                         start=(lc == 0), stop=(lc == LCN - 1))
                t_done = None
                for lc in range(LCN):
                    t_done = P.op(
                        "tensor", "matmul", psf[bv][:, :],
                        wkvb_sb[:, lc * 512 + 256 + hl * 128:lc * 512 + 256 + (hl + 1) * 128],
                        ln_db[par][:, lc * TT:(lc + 1) * TT],
                        start=(lc == 0), stop=(lc == LCN - 1),
                        inc=("sT" if lc == LCN - 1 else None))
                if t16 + 2 < 16:
                    ln_ld[t16 + 2] = load_ln(t16 + 2, t_done)
                P.wait("vector", "sT", t_done)
                P.wait("vector", "sT", sc_free[scp])
                vk = P.op("vector", "tensor_copy", sc512[scp][:, :], psf[bk][:, :],
                          inc="sV")
                bfree[bk] = ("sV", vk)
                P.wait("tensor", "sV", vk)
                tk = P.op("tensor", "matmul", psf[br][:, :], R_sb[hl][:, :],
                          sc512[scp][:, :], start=True, stop=True, inc="sT")
                sc_free[scp] = tk
                scp ^= 1
                P.wait("vector", "sT", tk)
                vk2 = P.op("vector", "tensor_copy",
                           K_sb[:, t16 * TT:(t16 + 1) * TT], psf[br][:, :], inc="sV")
                bfree[br] = ("sV", vk2)
                vv = P.op("vector", "tensor_copy", stg[vtp][:, :], psf[bv][:, :],
                          inc="sV")
                bfree[bv] = ("sV", vv)
                tb = t16 & 1
                sname, v = tfree[tb]
                P.wait("tensor", sname, v)
                P.wait("tensor", "sV", vv)
                tv = None
                for j in range(4):
                    tv = P.op("tensor", "transpose", pst[tb][:, j * 128:(j + 1) * 128],
                              stg[vtp][:, j * 128:(j + 1) * 128], c_id[:, :],
                              inc=("sT" if j == 3 else None))
                vtp ^= 1
                P.wait("vector", "sT", tv)
                vv2 = P.op("vector", "tensor_copy",
                           V_sb[:, t16 * TT:(t16 + 1) * TT], pst[tb][:, 0:512], inc="sV")
                tfree[tb] = ("sV", vv2)
            P.barrier()
            if stage == "attn" and hl == 0:
                P.dma("sync", d_K[:, :], K_sb[:, :], "sS")
                P.dma("sync", d_V[:, :], V_sb[:, :], "sS")
                P.barrier()

            st_c = {
                "s_free": [0, 0], "pr_free": [0, 0], "l_free": [0, 0],
                "aT_free": [0, 0], "ost_free": [0, 0],
                "sbank": {i: ("sV", 0) for i in range(4)},
                "tpar": {0: ("sV", 0), 1: ("sV", 0)},
                "oQ": ("sV", 0),
                "scale_done": [0, 0],
                "rl_free": [0, 0],
            }

            def emit_A_step(p, kc):
                cov = p // 2 + 1
                for sj in range(2):
                    js = 2 * p + sj
                    spar = (2 * kc + sj) & 3
                    sname, v = st_c["sbank"][spar]
                    P.wait("tensor", sname, v)
                    tsc = P.op("tensor", "matmul", psf[spar][:, :],
                               qT[hl][:, js * 128:(js + 1) * 128],
                               K_sb[:, kc * TT:(kc + 1) * TT],
                               start=True, stop=True, inc="sT")
                    P.wait("vector", "sT", tsc)
                    if kc == 0:
                        P.wait("vector", "sA", st_c["s_free"][sj])
                    mprev = c_neginf if kc == 0 else m_t[sj][(kc - 1) & 1]
                    ssl = s_sc[sj][:, kc * TT:(kc + 1) * TT]
                    if kc == cov - 1:
                        mv = (js % 4) * TT
                        P.op("vector", "scalar_tensor_tensor", out=ssl,
                             in0=psf[spar][:, :], scalar=SCALE_QK,
                             in1=c_mask[:, mv:mv + TT], op0=ALU.mult, op1=ALU.add)
                        vdone = P.op("vector", "tensor_scalar", out=ssl, in0=ssl,
                                     scalar1=1.0, scalar2=mprev[:, :],
                                     op0=ALU.mult, op1=ALU.max,
                                     accum_out=m_t[sj][kc & 1][:, :], inc="sV")
                    else:
                        vdone = P.op("vector", "tensor_scalar", out=ssl,
                                     in0=psf[spar][:, :], scalar1=SCALE_QK,
                                     scalar2=mprev[:, :], op0=ALU.mult, op1=ALU.max,
                                     accum_out=m_t[sj][kc & 1][:, :], inc="sV")
                    st_c["sbank"][spar] = ("sV", vdone)

            def emit_B(p):
                cov = p // 2 + 1
                deferred = []
                for sj in range(2):
                    mfin = m_t[sj][(cov - 1) & 1]
                    nm = P.op("vector", "tensor_scalar", out=negm_t[sj][:, :],
                              in0=mfin[:, :], scalar1=-1.0, scalar2=None,
                              op0=ALU.mult, inc="sV")
                    P.wait("scalar", "sV", nm)
                    P.wait("scalar", "sT", st_c["pr_free"][sj])
                    adone = P.op("scalar", "activation",
                                 out=pr[sj][:, 0:cov * TT],
                                 in_=s_sc[sj][:, 0:cov * TT],
                                 func=AF.Exp, bias=negm_t[sj][:, :], scale=1.0,
                                 accum_out=l_t[sj][:, :], inc="sA")
                    st_c["s_free"][sj] = adone

                    def scale_step(sj=sj, adone=adone, cov=cov):
                        P.wait("vector", "sA", adone)
                        P.op("vector", "reciprocal", rl_t[sj][:, :], l_t[sj][:, :],
                             inc="sV")
                        st_c["scale_done"][sj] = P.op(
                            "vector", "tensor_scalar", out=pr[sj][:, 0:cov * TT],
                            in0=pr[sj][:, 0:cov * TT], scalar1=rl_t[sj][:, :],
                            scalar2=None, op0=ALU.mult, inc="sV")
                    deferred.append(scale_step)
                return deferred

            def make_C_steps(p):
                cov = p // 2 + 1
                opar = p & 1
                steps = []

                def pre():
                    P.wait("tensor", "sV", max(st_c["scale_done"]))
                    sname, v = st_c["oQ"]
                    P.wait("tensor", sname, v)
                steps.append(pre)

                def upstep(up, last):
                    u0, u1 = 2 * up, 2 * up + 1
                    tb = up & 1
                    sname, v = st_c["tpar"][tb]
                    P.wait("tensor", sname, v)
                    ttr = None
                    for pos, (sj, u) in enumerate(((0, u0), (1, u0), (0, u1), (1, u1))):
                        ttr = P.op("tensor", "transpose",
                                   pst[tb][:, pos * 128:(pos + 1) * 128],
                                   pr[sj][:, u * 128:(u + 1) * 128], c_id[:, :],
                                   inc=("sT" if pos == 3 else None))
                    P.wait("vector", "sT", ttr)
                    P.wait("vector", "sT", st_c["aT_free"][tb])
                    vev = P.op("vector", "tensor_copy", aT_sb[tb][:, :],
                               pst[tb][:, 0:512], inc="sV")
                    st_c["tpar"][tb] = ("sV", vev)
                    P.wait("tensor", "sV", vev)
                    P.op("tensor", "matmul", psf[4][:, 0:256],
                         V_sb[:, u0 * 128:(u0 + 1) * 128], aT_sb[tb][:, 0:256],
                         start=(up == 0), stop=False)
                    last_av = P.op("tensor", "matmul", psf[4][:, 0:256],
                                   V_sb[:, u1 * 128:(u1 + 1) * 128],
                                   aT_sb[tb][:, 256:512],
                                   start=False, stop=last, inc="sT")
                    st_c["aT_free"][tb] = last_av
                    if last:
                        st_c["pr_free"][0] = last_av
                        st_c["pr_free"][1] = last_av
                        P.wait("vector", "sT", last_av)
                        P.wait("vector", f"sSC{opar}", st_c["ost_free"][opar])
                        vo = P.op("vector", "tensor_copy", ost[opar][:, :],
                                  psf[4][:, 0:256], inc="sV")
                        st_c["oQ"] = ("sV", vo)
                        P.wait("scalar", "sV", vo)
                        st_c["ost_free"][opar] = P.dma(
                            "scalar",
                            oc_c[(p // 4) * 256 + hl * 128:
                                 (p // 4) * 256 + (hl + 1) * 128,
                                 (p % 4) * 256:(p % 4 + 1) * 256],
                            ost[opar][:, :], f"sSC{opar}")

                for up in range(cov * 2):
                    steps.append((lambda u=up, lst=(up == cov * 2 - 1):
                                  upstep(u, lst)))
                return steps

            cq = []
            for p in range(32):
                cov = p // 2 + 1
                for kc in range(cov):
                    emit_A_step(p, kc)
                    for _ in range(3):
                        if cq:
                            cq.pop(0)()
                while cq:
                    cq.pop(0)()
                cq = emit_B(p) + make_C_steps(p)
            while cq:
                cq.pop(0)()
            P.barrier()

        if stage == "attn":
            P.dma("sync", d_oc.ap(), oc_c.ap(), "sS")
            P.barrier()

        P.op("gpsimd", "collective_compute", "AllToAll", ALU.bypass,
             replica_groups=[list(range(NCORES))],
             ins=[oc_c.ap().opt()], outs=[oa_r.ap().opt()], inc="sC")
        P.barrier()

        if stage == "attn":
            P.dma("sync", d_oa.ap(), oa_r.ap(), "sS")
            P.barrier()

        P.dma("sync", x2_sb.ap().rearrange("p (dc t) -> p dc t", t=TOK),
              xspill_d.ap().rearrange("(dc p) t -> p dc t", p=128), "sS")
        for fc in range(DCN):
            P.dma("sync", oT_sb[:, fc * TOK:(fc + 1) * TOK],
                  oa_r[fc * 128:(fc + 1) * 128, :], "sS")
        P.barrier()
        for dhalf in range(2):
            wl = P.dma("gpsimd", owt_sb.ap().rearrange("p (fc d) -> p fc d", d=TOK),
                       owt[:, dhalf * TOK:(dhalf + 1) * TOK].rearrange("(fc p) d -> p fc d", p=128),
                       "sG")
            P.wait("tensor", "sG", wl)
            obank = {0: ("sV", 0), 1: ("sV", 0)}
            step = 0
            for dc8 in range(8):
                dc = dhalf * 8 + dc8
                for tt in range(2):
                    bk = step & 1
                    sname, v = obank[bk]
                    P.wait("tensor", sname, v)
                    t_done = None
                    for fc in range(DCN):
                        t_done = P.op(
                            "tensor", "matmul", psf[bk][:, :],
                            owt_sb[:, fc * TOK + dc8 * 128:fc * TOK + (dc8 + 1) * 128],
                            oT_sb[:, fc * TOK + tt * TT:fc * TOK + (tt + 1) * TT],
                            start=(fc == 0), stop=(fc == DCN - 1),
                            inc=("sT" if fc == DCN - 1 else None))
                    P.wait("vector", "sT", t_done)
                    vdone = P.op("vector", "scalar_tensor_tensor",
                                 out=x2_sb[:, dc * TOK + tt * TT:dc * TOK + (tt + 1) * TT],
                                 in0=psf[bk][:, :], scalar=c_outb[:, dc:dc + 1],
                                 in1=x2_sb[:, dc * TOK + tt * TT:dc * TOK + (tt + 1) * TT],
                                 op0=ALU.add, op1=ALU.add, inc="sV")
                    obank[bk] = ("sV", vdone)
                    step += 1
            P.barrier()

        if stage == "attn":
            fin = P.dma("sync", out_x.ap().rearrange("(dc p) t -> p dc t", p=128),
                        x2_sb.ap().rearrange("p (dc t) -> p dc t", t=TOK), "sS")
            P.wait("sync", "sS", fin)
            _emit_block(nc, P)
            return nc

        emit_ln(x2_sb)
        emit_ffn(x2_sb)

        fin = P.dma("sync", out_x.ap().rearrange("(dc p) t -> p dc t", p=128),
                    x2_sb.ap().rearrange("p (dc t) -> p dc t", t=TOK), "sS")
        P.wait("sync", "sS", fin)
        _emit_block(nc, P)
    return nc


def _emit_block(nc, P):
    with nc.Block() as blk:
        @blk.tensor
        def _(eng):
            P.run(eng, "tensor")

        @blk.vector
        def _(eng):
            P.run(eng, "vector")

        @blk.scalar
        def _(eng):
            P.run(eng, "scalar")

        @blk.sync
        def _(eng):
            P.run(eng, "sync")

        @blk.gpsimd
        def _(eng):
            P.run(eng, "gpsimd")


def _host_prepare(inputs):
    x = np.asarray(inputs["L0_attn_x"], np.float32)[0]
    freqs = np.asarray(inputs["L0_attn_freqs"], np.float32)
    wq = np.asarray(inputs["L0_attn_wq_w"], np.float32)
    wqb = np.asarray(inputs["L0_attn_wq_b"], np.float32)
    wkva = np.asarray(inputs["L0_attn_wkv_a_w"], np.float32)
    wkvb = np.asarray(inputs["L0_attn_wkv_b_w"], np.float32)
    nw = np.asarray(inputs["L0_attn_norm_w"], np.float32)
    ow = np.asarray(inputs["L0_attn_out_w"], np.float32)
    ob = np.asarray(inputs["L0_attn_out_b"], np.float32)
    W1 = np.asarray(inputs["L0_ffn_W1"], np.float32)
    W2 = np.asarray(inputs["L0_ffn_W2"], np.float32)
    W3 = np.asarray(inputs["L0_ffn_W3"], np.float32)

    xT = np.ascontiguousarray(x.T)
    w1t = np.ascontiguousarray(
        W1.T.reshape(DCN, 128, HCN, 128).transpose(2, 1, 0, 3))
    w3t = np.ascontiguousarray(
        W3.T.reshape(DCN, 128, HCN, 128).transpose(2, 1, 0, 3))
    w2t = np.ascontiguousarray(W2.T)
    wqt = np.ascontiguousarray(wq.T)
    wkv_at = np.ascontiguousarray(
        wkva.T.reshape(DCN, 128, LCN, 128).transpose(2, 1, 0, 3))
    wkv_bt = np.ascontiguousarray(wkvb.T)
    owt = np.ascontiguousarray(ow.T)

    h128 = _hadamard(128).astype(BF16NP)
    h64 = _hadamard(64)
    h64bd = np.zeros((128, 128), np.float32)
    h64bd[:64, :64] = h64
    h64bd[64:, 64:] = h64
    h64bd = h64bd.astype(BF16NP)
    ident = np.eye(128, dtype=np.float32).astype(BF16NP)
    identf = np.eye(128, dtype=np.float32)
    ms = np.zeros((128, 128), np.float32)
    for i in range(32):
        e, od = 64 + 2 * i, 64 + 2 * i + 1
        ms[e, od] = 1.0
        ms[od, e] = -1.0
    mask = np.zeros((128, 4 * TT), np.float32)
    q_i = np.arange(128)[:, None]
    kt_i = np.arange(TT)[None, :]
    for v in range(4):
        mask[:, v * TT:(v + 1) * TT] = np.where(kt_i <= v * 128 + q_i, 0.0, MASKVAL)
    ones = np.ones((128, 1), np.float32)
    onesrow = np.ones((1, 128), np.float32)
    outb = np.ascontiguousarray(ob.reshape(DCN, 128).T)
    nwt = np.ascontiguousarray(nw.reshape(LCN, 128).T)
    neginf = np.full((128, 1), -3.0e38, np.float32)
    epsc = np.array([[EPS_LN, EPS_RMS]], np.float32)

    ims = []
    for r in range(NCORES):
        g0, g1 = 2 * r, 2 * r + 1
        cs = np.zeros((128, 4), np.float32)
        for li, g in enumerate((g0, g1)):
            cvec = np.ones(128, np.float32)
            svec = np.zeros(128, np.float32)
            cvec[64 + 2 * np.arange(32)] = freqs[g, :, 0]
            cvec[64 + 2 * np.arange(32) + 1] = freqs[g, :, 0]
            svec[64 + 2 * np.arange(32)] = freqs[g, :, 1]
            svec[64 + 2 * np.arange(32) + 1] = freqs[g, :, 1]
            cs[:, 2 * li] = cvec
            cs[:, 2 * li + 1] = svec
        wkvb_sl = np.ascontiguousarray(np.concatenate(
            [wkv_bt[:, g0 * 128:(g0 + 1) * 128], wkv_bt[:, g1 * 128:(g1 + 1) * 128],
             wkv_bt[:, 2048 + g0 * 128:2048 + (g0 + 1) * 128],
             wkv_bt[:, 2048 + g1 * 128:2048 + (g1 + 1) * 128]], axis=1))
        wq_sl = np.ascontiguousarray(wqt[:, g0 * 128:(g1 + 1) * 128])
        wqb_sl = np.ascontiguousarray(
            np.stack([wqb[g0 * 128:(g0 + 1) * 128], wqb[g1 * 128:(g1 + 1) * 128]],
                     axis=1))
        ims.append(dict(
            x_in=np.ascontiguousarray(xT[:, r * TOK:(r + 1) * TOK]),
            w1t=w1t, w3t=w3t, w2t=w2t, wqt_s=wq_sl, wkv_at=wkv_at,
            wkvb_s=wkvb_sl, owt=owt,
            c_id=ident, c_h128=h128, c_h64=h64bd, c_idf=identf, c_ms=ms,
            c_mask=mask, c_cs=cs, c_ones=ones, c_onesrow=onesrow,
            c_wqb=wqb_sl, c_outb=outb, c_nw=nwt, c_neginf=neginf, c_eps=epsc,
        ))
    return ims


def kernel(**inputs):
    key = _STAGE
    if key not in _CACHE:
        _CACHE[key] = build_program(key)
    nc = _CACHE[key]
    ims = _host_prepare(inputs)
    res = run_bass_kernel_spmd(nc, ims, core_ids=list(range(NCORES)))
    out = np.concatenate([np.asarray(res.results[r]["out_x"]) for r in range(NCORES)],
                         axis=1)
    return np.ascontiguousarray(out.T).reshape(1, S, D).astype(np.float32)
